# version 02
# baseline (speedup 1.0000x reference)
import numpy as np
import ml_dtypes
import concourse.bass as bass
import concourse.mybir as mybir
from concourse.bass_utils import run_bass_kernel_spmd

F32 = mybir.dt.float32
BF16 = mybir.dt.bfloat16
AF = mybir.ActivationFunctionType
ALU = mybir.AluOpType
AX = mybir.AxisListType

H = 2048
NH = 16
NF = 16
S = 8192
SCALE = float(128) ** -0.5
NCORE = 8
CH = 512
T_OWN = 2048
T_EXT = S + T_OWN
SLOT_PAD = [3, 7, 11, 15]
NPLAIN = sum(SLOT_PAD)
BIGNEG = -1.0e6
EPS = 1e-5
RH = 1.0 / H
E = 16

TC = 256
NTC = T_EXT // TC
PTC = 256
NPT = T_OWN // PTC

FFN_ES = [(e, e, "moe") for e in range(E)] + [(i, 16, "sh") for i in range(4)]

STAGE = "full"
DEBUG = False
SKIP_OUT = False
SERIAL_FFN = False
FFN_MEMSET = True
DBG_FFN_LIST = ["dbg_y", "dbg_a", "dbg_ln2", "dbg_cmb"]


def chunks_of_core(r):
    return [r, 7 - r, 8 + r, 15 - r]



def stage_inputs(inp):
    x = np.asarray(inp["L0_attn_x"])
    W_qkv = np.asarray(inp["L0_attn_W_qkv"])
    b_qkv = np.asarray(inp["L0_attn_b_qkv"])
    W_proj = np.asarray(inp["L0_attn_W_proj"])
    b_proj = np.asarray(inp["L0_attn_b_proj"])
    w_router = np.asarray(inp["L0_ffn_w_router"])
    w_gate = np.asarray(inp["L0_ffn_w_gate"])
    w_up = np.asarray(inp["L0_ffn_w_up"])
    w_down = np.asarray(inp["L0_ffn_w_down"])
    ws_gate = np.asarray(inp["L0_ffn_ws_gate"])
    ws_up = np.asarray(inp["L0_ffn_ws_up"])
    ws_down = np.asarray(inp["L0_ffn_ws_down"])
    w_sg = np.asarray(inp["L0_ffn_w_shared_gate"])

    C = np.ascontiguousarray
    sh = {}
    sh["w_kv"] = C(W_qkv[:, 2048:].reshape(16, 128, 256))
    sh["w_q"] = C(W_qkv[:, :2048].reshape(16, 128, 16, 128).transpose(2, 0, 1, 3))
    sh["w_proj"] = C(W_proj.reshape(16, 128, 16, 128).transpose(2, 0, 1, 3))
    sh["b_q"] = C(b_qkv[:2048].reshape(16, 128).T)
    sh["b_k"] = C(b_qkv[2048:2176].reshape(128, 1))
    sh["b_v"] = C(b_qkv[2176:2304].reshape(128, 1))
    sh["b_proj"] = C(b_proj.reshape(16, 128).T)
    sh["w_rt"] = C(np.concatenate([w_router, w_sg], axis=1).reshape(16, 128, 17))
    sh["w_g"] = C(w_gate.reshape(E, 16, 128, 11, 128).transpose(0, 3, 1, 2, 4))
    sh["w_u"] = C(w_up.reshape(E, 16, 128, 11, 128).transpose(0, 3, 1, 2, 4))
    sh["w_d"] = C(w_down.reshape(E, 11, 128, 8, 256).transpose(0, 3, 1, 2, 4))
    sh["ws_g"] = C(ws_gate.reshape(16, 128, 4, 11, 128).transpose(2, 3, 0, 1, 4))
    sh["ws_u"] = C(ws_up.reshape(16, 128, 4, 11, 128).transpose(2, 3, 0, 1, 4))
    sh["ws_d"] = C(ws_down.reshape(4, 11, 128, 8, 256).transpose(0, 3, 1, 2, 4))
    ii = np.arange(CH)
    tri_t = np.where(ii[None, :] >= ii[:, None], 0.0, BIGNEG).astype(np.float32)
    sh["tri_t"] = C(tri_t.reshape(4, 128, CH).transpose(1, 0, 2).reshape(128, 4 * CH))

    in_maps = []
    for c in range(NCORE):
        b, r = divmod(c, 4)
        ch = chunks_of_core(r)
        xb = x[b]
        own = np.concatenate([xb[CH * cc:CH * (cc + 1)] for cc in ch], axis=0)
        x_ext = np.concatenate([xb, own], axis=0).T
        x_chunks = C(x_ext.reshape(16, 128, NTC, TC).transpose(2, 0, 1, 3))
        m = dict(sh)
        m["x_ext"] = x_chunks
        bn = np.zeros((128, NPLAIN + 1), np.float32)
        col = 0
        for s_i in range(4):
            for j in range(SLOT_PAD[s_i]):
                if j >= ch[s_i]:
                    bn[:, col] = BIGNEG
                col += 1
        m["bigneg"] = C(bn)
        in_maps.append(m)
    return in_maps



class Ctx:
    def __init__(self, nc):
        self.nc = nc
        self.sems = {}
        self.cnt = {}

    def sem(self, name):
        if name not in self.sems:
            self.sems[name] = self.nc.alloc_semaphore(name)
            self.cnt[name] = 0
        return self.sems[name]

    def inc(self, inst, name, n=1):
        inst.then_inc(self.sem(name), n)
        self.cnt[name] += n
        return self.cnt[name]

    def val(self, name):
        self.sem(name)
        return self.cnt[name]


def build_program():
    nc = bass.Bass("TRN2", target_bir_lowering=False)
    ctx = Ctx(nc)

    def din(name, shape, dtype=F32):
        return nc.dram_tensor(name, list(shape), dtype, kind="ExternalInput").ap()

    x_ext = din("x_ext", [NTC, 16, 128, TC])
    w_kv_d = din("w_kv", [16, 128, 256])
    w_q_d = din("w_q", [16, 16, 128, 128])
    w_proj_d = din("w_proj", [16, 16, 128, 128])
    b_q_d = din("b_q", [128, 16])
    b_k_d = din("b_k", [128, 1])
    b_v_d = din("b_v", [128, 1])
    b_proj_dd = din("b_proj", [128, 16])
    w_rt_d = din("w_rt", [16, 128, 17])
    w_g_d = din("w_g", [E, 11, 16, 128, 128])
    w_u_d = din("w_u", [E, 11, 16, 128, 128])
    w_d_d = din("w_d", [E, 8, 11, 128, 256])
    ws_g_d = din("ws_g", [4, 11, 16, 128, 128])
    ws_u_d = din("ws_u", [4, 11, 16, 128, 128])
    ws_d_d = din("ws_d", [4, 8, 11, 128, 256])
    tri_t_d = din("tri_t", [128, 4 * CH])
    bigneg_d = din("bigneg", [128, NPLAIN + 1])

    out_d = nc.dram_tensor("out", [T_OWN, H], F32, kind="ExternalOutput").ap()
    dbg = {}
    if DEBUG:
        for name, shape, dt_ in [
            ("dbg_k", [128, T_EXT], BF16),
            ("dbg_v", [128, T_EXT], BF16),
            ("dbg_q", [128, NH * T_OWN], BF16),
            ("dbg_lnd", [128, NF * T_OWN], BF16),
            ("dbg_o", [128, NH * T_OWN], BF16),
            ("dbg_ln2", [128, NF * T_OWN], BF16),
            ("dbg_x2", [T_OWN, H], F32),
            ("dbg_cmb", [128, 16 * 17], F32),
            ("dbg_y", [128, NF * 1024], F32),
            ("dbg_a", [128, 11 * 512], BF16),
        ]:
            dbg[name] = nc.dram_tensor(name, shape, dt_, kind="ExternalOutput").ap()

    x2_dram = nc.dram_tensor("x2_scratch", [T_OWN, H], F32, kind="Internal").ap()

    it = nc.inline_tensor
    ident_bf_d = it(np.eye(128, dtype=ml_dtypes.bfloat16), name="ident_bf").ap()
    ident_f_d = it(np.eye(128, dtype=np.float32), name="ident_f").ap()
    ones_bf_d = it(np.ones((128, 1), dtype=ml_dtypes.bfloat16), name="ones_bf").ap()
    ones1f_d = it(np.ones((1, 128), dtype=np.float32), name="ones1f").ap()
    eps_d = it(np.full((128, 1), EPS, dtype=np.float32), name="epsc").ap()

    sb = nc.alloc_sbuf_tensor
    k_sb = sb("k_sb", [128, T_EXT], BF16)
    v_sb = sb("v_sb", [128, T_EXT], BF16)
    bufA = sb("bufA", [128, NF * T_OWN], BF16)
    bufB = sb("bufB", [128, NF * T_OWN], BF16)
    bufD = sb("bufD", [128, 8192], F32)
    c_ident_bf = sb("c_ident_bf", [128, 128], BF16)
    c_ident_f = sb("c_ident_f", [128, 128], F32)
    c_ones_bf = sb("c_ones_bf", [128, 1], BF16)
    c_ones1f = sb("c_ones1f", [1, 128], F32)
    c_eps = sb("c_eps", [128, 1], F32)
    c_bq = sb("c_bq", [128, 16], F32)
    c_bk = sb("c_bk", [128, 1], F32)
    c_bv = sb("c_bv", [128, 1], F32)
    c_bproj = sb("c_bproj", [128, 16], F32)
    c_bigneg = sb("c_bigneg", [128, NPLAIN + 1], F32)
    c_wrt = sb("c_wrt", [128, 16 * 17], BF16)
    rows = sb("rows", [1, 1024], F32)
    maxk = sb("maxk", [1, 1], F32)
    combine = sb("combine", [128, 16 * 17], F32)
    sig = sb("sig", [128, 8], F32)

    bufA_f = bufA[:].bitcast(F32)
    bufB_f = bufB[:].bitcast(F32)
    bufD_bf = bufD[:].bitcast(BF16)
    k_bfv = k_sb[:]
    v_bfv = v_sb[:]

    pb = [nc.alloc_psum_tensor(f"pb{i}", [128, 512], F32) for i in range(6)]
    pb_bf = nc.alloc_psum_tensor("pb_bf", [128, 1024], BF16)
    pb7 = nc.alloc_psum_tensor("pb7", [128, 512], F32)

    xbf_t = bufB[:, 0:NF * TC]
    sq_t = bufB[:, NF * TC:2 * NF * TC]
    ln1c_t = bufB[:, 2 * NF * TC:3 * NF * TC]
    vfm_t = bufB[:, 3 * NF * TC:3 * NF * TC + TC]
    wkv_sb = bufB[:, 16384:16384 + 16 * 256]
    tmp0 = bufB_f[:, 6400:6656]
    tmp1 = bufB_f[:, 6656:6912]

    def xstage(i):
        o = (i % 2) * (NF * TC)
        return bufD[:, o:o + NF * TC]

    att_stt = [bufD[:, 0:512], bufD[:, 512:1024]]
    att_bb = bufD[:, 1024:1536]
    att_rs = bufD[:, 1536:2048]
    att_pt = [bufD_bf[:, 4096:4608], bufD_bf[:, 4608:5120]]
    att_sq = bufD_bf[:, 10240:10240 + 2048]
    att_tri = bufD[:, 6144:8192]
    bnd_row = bufD[0:1, 2560:4608]

    pr_xr = bufD[:, 0:NF * PTC]
    pr_tmp = [bufD[:, NF * PTC:NF * PTC + 256], bufD[:, NF * PTC + 256:NF * PTC + 512]]
    pr_x2t = bufD[:, NF * PTC + 512:NF * PTC + 512 + 2048]
    v_f32 = v_sb[:].bitcast(F32)
    pr_a1 = v_f32[:, 0:NF * PTC]
    wp_buf = [k_bfv[:, 0:2048], k_bfv[:, 2048:4096]]

    rt_ex = bufD[:, 0:16]
    rt_pr = bufD[:, 16:32]
    rt_wk = bufD[:, 32:48]
    rt_mk = bufD[:, 48:64]

    ff_a = bufD_bf[:, 0:11 * 512]
    ff_sg = [bufD[:, 4096:4608], bufD[:, 4608:5120]]
    wg_buf = [k_bfv[:, 0:2048], k_bfv[:, 2048:4096]]
    wu_buf = [k_bfv[:, 4096:6144], k_bfv[:, 6144:8192]]
    wd_buf = [v_bfv[:, 0:2816], v_bfv[:, 2816:5632]]
    ff_ysq = k_bfv[:, 8192:10240]
    out_x2r = bufD[:, 5632:5632 + 2048]
    out_x2r = bufD[:, 6144 - 2048 + 2048:8192]
    out_x2r = bufD[:, 6144:8192]
    out_t1 = bufD[:, 4096 - 2048 + 0:4096]
    out_t1 = bufD[:, 2048 + 1024:2048 + 1024 + 2048]
    out_ot = bufD[:, 0:2048]
    y_acc = bufA_f
    ln2 = bufB

    blk = nc.Block()
    blk.__enter__()

    V = ctx.val

    def ph_ld():
        def f_sync(sy):
            for dst, src in [
                (c_ident_bf[:], ident_bf_d), (c_ident_f[:], ident_f_d),
                (c_ones_bf[:], ones_bf_d), (c_ones1f[:], ones1f_d),
                (c_eps[:], eps_d), (c_bq[:], b_q_d), (c_bk[:], b_k_d),
                (c_bv[:], b_v_d), (c_bproj[:], b_proj_dd),
                (c_bigneg[:], bigneg_d),
            ]:
                ctx.inc(sy.dma_start(out=dst, in_=src), "ld", 16)
        blk.sync(f_sync)

        def f_gp(g):
            ctx.inc(g.dma_start(
                out=wkv_sb.rearrange("p (c n) -> p c n", c=16),
                in_=w_kv_d.rearrange("c p n -> p c n")), "ld", 16)
            ctx.inc(g.dma_start(
                out=c_wrt[:].rearrange("p (c n) -> p c n", c=16),
                in_=w_rt_d.rearrange("c p n -> p c n")), "ld", 16)
        blk.gpsimd(f_gp)

    def ph_p0a():
        ld_done = V("ld")

        def f_sync(sy):
            sy.wait_ge(ctx.sem("ld"), ld_done)
            for i in range(NTC):
                if i >= 2:
                    sy.wait_ge(ctx.sem("d_norm"), i - 1)
                ctx.inc(sy.dma_start(
                    out=xstage(i).rearrange("p (c n) -> p c n", c=16),
                    in_=x_ext[i].rearrange("c p n -> p c n")), "s_x", 16)
        blk.sync(f_sync)

        def f_sca(sc):
            sc.wait_ge(ctx.sem("ld"), ld_done)
            for i in range(NTC):
                xs = xstage(i)
                sc.wait_ge(ctx.sem("s_x"), 16 * (i + 1))
                if i >= 1:
                    sc.wait_ge(ctx.sem("t_stats"), 2 * i)
                ctx.inc(sc.activation(xbf_t, xs, AF.Copy), "a_cast")
                ctx.inc(sc.activation(sq_t, xs, AF.Square), "a_sq")
                sc.wait_ge(ctx.sem("d_rows"), 2 * (i + 1))
                if i >= 1:
                    sc.wait_ge(ctx.sem("d_rr"), i)
                ctx.inc(sc.activation(rows[0:1, 512:768], rows[0:1, 0:256],
                                      AF.Square), "a_mu2")
                sc.wait_ge(ctx.sem("d_var"), i + 1)
                ctx.inc(sc.activation(rows[0:1, 512:768], rows[0:1, 768:1024],
                                      AF.Sqrt, bias=c_eps[0:1, :]), "a_sqrt")
                sc.wait_ge(ctx.sem("t_kv"), 2 * (i + 1))
                if i >= 1:
                    sc.wait_ge(ctx.sem("t_vt"), 2 * i)
                ctx.inc(sc.activation(k_sb[:, TC * i:TC * (i + 1)], pb[0][:, 0:TC],
                                      AF.Identity, bias=c_bk[:]), "a_k")
                ctx.inc(sc.activation(vfm_t, pb[1][:, 0:TC],
                                      AF.Identity, bias=c_bv[:]), "a_vf")
        blk.scalar(f_sca)

        def f_ten(t):
            for i in range(NTC):
                t.wait_ge(ctx.sem("a_cast"), i + 1)
                t.wait_ge(ctx.sem("a_sq"), i + 1)
                if i >= 1:
                    t.wait_ge(ctx.sem("d_rows"), 2 * i)
                mm = None
                for fc in range(16):
                    mm = t.matmul(pb[2][0:1, 0:TC], c_ones_bf[:],
                                  xbf_t[:, TC * fc:TC * (fc + 1)],
                                  start=(fc == 0), stop=(fc == 15))
                ctx.inc(mm, "t_stats")
                for fc in range(16):
                    mm = t.matmul(pb[3][0:1, 0:TC], c_ones_bf[:],
                                  sq_t[:, TC * fc:TC * (fc + 1)],
                                  start=(fc == 0), stop=(fc == 15))
                ctx.inc(mm, "t_stats")
                t.wait_ge(ctx.sem("d_rr"), i + 1)
                if i >= 1:
                    t.wait_ge(ctx.sem("d_norm"), i)
                ctx.inc(t.matmul(pb[4][:, 0:TC], c_ones1f[:], rows[0:1, 0:TC],
                                 start=True, stop=True), "t_bc")
                ctx.inc(t.matmul(pb[5][:, 0:TC], c_ones1f[:], rows[0:1, 768:768 + TC],
                                 start=True, stop=True), "t_bc")
                t.wait_ge(ctx.sem("d_norm"), i + 1)
                if i >= 1:
                    t.wait_ge(ctx.sem("a_k"), i)
                    t.wait_ge(ctx.sem("a_vf"), i)
                ln1 = (ln1c_t if i < 32 else None)
                for which, pbi in ((0, pb[0]), (1, pb[1])):
                    mm = None
                    for fc in range(16):
                        if i < 32:
                            rhs = ln1c_t[:, TC * fc:TC * (fc + 1)]
                        else:
                            rhs = bufA[:, 2048 * fc + TC * (i - 32):2048 * fc + TC * (i - 32) + TC]
                        lhsT = wkv_sb[:, 256 * fc + 128 * which:256 * fc + 128 * which + 128]
                        mm = t.matmul(pbi[:, 0:TC], lhsT, rhs,
                                      start=(fc == 0), stop=(fc == 15))
                    ctx.inc(mm, "t_kv")
                t.wait_ge(ctx.sem("a_vf"), i + 1)
                if i >= 1:
                    t.wait_ge(ctx.sem("d_vt"), 2 * i)
                ctx.inc(t.transpose(pb_bf[:, 0:128], vfm_t[:, 0:128], c_ident_bf[:]), "t_vt")
                ctx.inc(t.transpose(pb_bf[:, 128:256], vfm_t[:, 128:256], c_ident_bf[:]), "t_vt")
        blk.tensor(f_ten)

        def f_vec(v):
            for i in range(NTC):
                v.wait_ge(ctx.sem("t_stats"), 2 * (i + 1))
                if i >= 1:
                    v.wait_ge(ctx.sem("t_bc"), 2 * i)
                ctx.inc(v.tensor_scalar_mul(rows[0:1, 0:TC], pb[2][0:1, 0:TC], RH), "d_rows")
                ctx.inc(v.tensor_scalar_mul(rows[0:1, 256:256 + TC], pb[3][0:1, 0:TC], RH), "d_rows")
                v.wait_ge(ctx.sem("a_mu2"), i + 1)
                ctx.inc(v.tensor_sub(rows[0:1, 768:768 + TC], rows[0:1, 256:256 + TC],
                                     rows[0:1, 512:512 + TC]), "d_var")
                v.wait_ge(ctx.sem("a_sqrt"), i + 1)
                ctx.inc(v.reciprocal(rows[0:1, 768:768 + TC], rows[0:1, 512:512 + TC]), "d_rr")
                v.wait_ge(ctx.sem("t_bc"), 2 * (i + 1))
                if i >= 1:
                    v.wait_ge(ctx.sem("t_kv"), 2 * i)
                xs = xstage(i)
                for fc in range(16):
                    tm = tmp0 if fc % 2 == 0 else tmp1
                    sub = v.tensor_sub(tm, xs[:, TC * fc:TC * (fc + 1)], pb[4][:, 0:TC])
                    se = ctx.inc(sub, "dse")
                    v.wait_ge(ctx.sem("dse"), se)
                    if i < 32:
                        dst = ln1c_t[:, TC * fc:TC * (fc + 1)]
                    else:
                        dst = bufA[:, 2048 * fc + TC * (i - 32):2048 * fc + TC * (i - 32) + TC]
                    mul = v.tensor_mul(dst, tm, pb[5][:, 0:TC])
                    if fc == 15:
                        ctx.inc(mul, "d_norm")
                v.wait_ge(ctx.sem("t_vt"), 2 * (i + 1))
                ctx.inc(v.tensor_copy(v_sb[:, TC * i:TC * (i + 1)], pb_bf[:, 0:256]),
                        "d_vt", 2)
        blk.vector(f_vec)

    def ph_barrier():
        tgt = ctx.val("bar") + 5
        for attr in (blk.sync, blk.gpsimd, blk.tensor, blk.vector, blk.scalar):
            def f(eng, tgt=tgt):
                ctx.inc(eng.nop(), "bar")
                eng.wait_ge(ctx.sem("bar"), tgt)
            attr(f)
        assert ctx.cnt["bar"] == tgt

    def ph_p0b():
        b_swq = V("s_wq")
        b_tq = V("t_q")
        b_aq = V("a_q")
        wqb = [bufD_bf[:, 0:2048], bufD_bf[:, 2048:4096]]

        def f_gp(g):
            for oc in range(16):
                if oc >= 2:
                    g.wait_ge(ctx.sem("t_q"), b_tq + 4 * (oc - 1))
                ctx.inc(g.dma_start(
                    out=wqb[oc % 2].rearrange("p (c n) -> p c n", c=16),
                    in_=w_q_d[oc].rearrange("c p n -> p c n")), "s_wq", 16)
        blk.gpsimd(f_gp)

        def f_ten(t):
            for oc in range(16):
                t.wait_ge(ctx.sem("s_wq"), b_swq + 16 * (oc + 1))
                for tg in range(4):
                    idx = 4 * oc + tg
                    if idx >= 2:
                        t.wait_ge(ctx.sem("a_q"), b_aq + idx - 1)
                    mm = None
                    for fc in range(16):
                        mm = t.matmul(pb[idx % 2][:, 0:512],
                                      wqb[oc % 2][:, 128 * fc:128 * (fc + 1)],
                                      bufA[:, 2048 * fc + 512 * tg:2048 * fc + 512 * (tg + 1)],
                                      start=(fc == 0), stop=(fc == 15))
                    ctx.inc(mm, "t_q")
        blk.tensor(f_ten)

        def f_sca(sc):
            for oc in range(16):
                for tg in range(4):
                    idx = 4 * oc + tg
                    sc.wait_ge(ctx.sem("t_q"), b_tq + idx + 1)
                    ctx.inc(sc.activation(
                        bufB[:, 2048 * oc + 512 * tg:2048 * oc + 512 * (tg + 1)],
                        pb[idx % 2][:, 0:512], AF.Identity,
                        bias=c_bq[:, oc:oc + 1]), "a_q")
        blk.scalar(f_sca)

    def ph_bounds():
        b_aks = V("a_ks")
        b_tkn = V("t_kn")
        b_dkm = V("d_km")
        b_aqs = V("a_qs")
        b_tqn = V("t_qn")
        b_aqb = V("a_qb")
        b_dqb = V("d_qb")

        b_ld = V("ld")

        def f_sync(sy):
            ctx.inc(sy.dma_start(out=att_tri, in_=tri_t_d), "ld", 16)
        blk.sync(f_sync)

        def f_vec0(v):
            ctx.inc(v.memset(maxk[:], 0.0), "d_km")
        blk.vector(f_vec0)

        def f_sca(sc):
            sc.wait_ge(ctx.sem("ld"), b_ld + 16)
            for cc in range(20):
                if cc >= 1:
                    sc.wait_ge(ctx.sem("t_kn"), b_tkn + cc)
                ctx.inc(sc.activation(att_sq[:, 0:512], k_sb[:, 512 * cc:512 * (cc + 1)],
                                      AF.Square), "a_ks")
        blk.scalar(f_sca)

        def f_ten(t):
            for cc in range(20):
                t.wait_ge(ctx.sem("a_ks"), b_aks + cc + 1)
                if cc >= 1:
                    t.wait_ge(ctx.sem("d_km"), b_dkm + 2 * cc)
                ctx.inc(t.matmul(pb[2][0:1, 0:512], c_ones_bf[:], att_sq[:, 0:512],
                                 start=True, stop=True), "t_kn")
        blk.tensor(f_ten)

        def f_vec(v):
            v.wait_ge(ctx.sem("d_km"), 1)
            for cc in range(20):
                v.wait_ge(ctx.sem("t_kn"), b_tkn + cc + 1)
                e1 = ctx.inc(v.reduce_max(rows[0:1, 4:5], pb[2][0:1, 0:512], axis=AX.X),
                             "d_km")
                v.wait_ge(ctx.sem("d_km"), e1)
                ctx.inc(v.tensor_max(maxk[:], maxk[:], rows[0:1, 4:5]), "d_km")
            e1 = ctx.inc(v.tensor_scalar_mul(maxk[:], maxk[:], 1.0404), "d_km")
        blk.vector(f_vec)

        def f_sca2(sc):
            sc.wait_ge(ctx.sem("d_km"), b_dkm + 42)
            ctx.inc(sc.activation(maxk[:], maxk[:], AF.Sqrt), "a_mk")
        blk.scalar(f_sca2)

    def ph_attn():
        b_tbb = V("t_bb")
        b_abb = V("a_bb")
        b_tst = V("t_st")
        b_dstt = V("d_stt")
        b_aexp = V("a_exp")
        b_tpv = V("t_pv")
        b_tsg = V("t_sg")
        b_drsg = V("d_rsg")
        b_trs = V("t_rs")
        b_ars = V("a_rs")
        b_dof = V("d_of")
        b_dtri = V("d_tri")

        b_tbn = V("t_bn")
        b_aqs = V("a_qs")
        b_abq = V("a_bq")
        b_dbn = V("d_bn")

        def iters():
            for h_ in range(16):
                for s_ in range(4):
                    yield h_, s_

        def kbs(s_):
            off = sum(SLOT_PAD[:s_])
            out = [(512 * j, off + j, False) for j in range(SLOT_PAD[s_])]
            out.append((S + 512 * s_, NPLAIN, True))
            return out

        tiles_before = {}
        tb_cum = 0
        for h_, s_ in iters():
            tiles_before[(h_, s_)] = tb_cum
            tb_cum += 4 * (SLOT_PAD[s_] + 1)
        total_tiles = tb_cum

        def f_ten_real(t):
            t.wait_ge(ctx.sem("a_mk"), 1)
            for itn, (h_, s_) in enumerate(iters()):
                kl = kbs(s_)
                t0 = tiles_before[(h_, s_)]
                ntile = 4 * len(kl)
                p2 = itn % 2
                if s_ == 0:
                    t.wait_ge(ctx.sem("a_qs"), b_aqs + h_ + 1)
                    for tb in range(4):
                        if itn >= 1 or tb >= 1:
                            t.wait_ge(ctx.sem("a_bq"), b_abq + 4 * h_ + tb)
                        if tb == 0 and itn >= 1:
                            t.wait_ge(ctx.sem("a_rs"), b_ars + itn)
                        ctx.inc(t.matmul(pb7[0:1, 0:512], c_ones_bf[:],
                                         att_sq[:, 512 * tb:512 * (tb + 1)],
                                         start=True, stop=True), "t_bn")
                t.wait_ge(ctx.sem("d_bn"), b_dbn + 4 * h_ + s_ + 1)
                if s_ >= 1:
                    t.wait_ge(ctx.sem("a_rs"), b_ars + itn)
                else:
                    t.wait_ge(ctx.sem("a_bq"), b_abq + 4 * (h_ + 1))
                ctx.inc(t.matmul(pb7[:, 0:512], c_ones1f[:],
                                 bnd_row[0:1, 512 * s_:512 * (s_ + 1)],
                                 start=True, stop=True), "t_bb")
                qrhs = bufB[:, 2048 * h_ + 512 * s_:2048 * h_ + 512 * s_ + 512]

                def pv_sg(ti):
                    gt = t0 + ti
                    kb, ks = divmod(ti, 4)
                    kcol, _, _ = kl[kb]
                    t.wait_ge(ctx.sem("a_exp"), b_aexp + gt + 1)
                    if ti == 0 and itn >= 2:
                        t.wait_ge(ctx.sem("d_of"), b_dof + itn - 1)
                        t.wait_ge(ctx.sem("d_rsg"), b_drsg + itn - 1)
                    ctx.inc(t.matmul(pb[2 + p2][:, 0:512],
                                     v_sb[:, kcol + 128 * ks:kcol + 128 * (ks + 1)],
                                     att_pt[gt % 2],
                                     start=(ti == 0), stop=(ti == ntile - 1)), "t_pv")
                    ctx.inc(t.matmul(pb[4 + p2][0:1, 0:512], c_ones_bf[:],
                                     att_pt[gt % 2],
                                     start=(ti == 0), stop=(ti == ntile - 1)), "t_sg")

                for ti in range(ntile):
                    kb, ks = divmod(ti, 4)
                    kcol, bcol, isdiag = kl[kb]
                    gt = t0 + ti
                    if gt >= 2:
                        t.wait_ge(ctx.sem("d_stt"), b_dstt + gt - 1)
                    ctx.inc(t.matmul(pb[gt % 2][:, 0:512],
                                     k_sb[:, kcol + 128 * ks:kcol + 128 * (ks + 1)],
                                     qrhs, start=True, stop=True), "t_st")
                    if ti >= 1:
                        pv_sg(ti - 1)
                pv_sg(ntile - 1)
                t.wait_ge(ctx.sem("d_rsg"), b_drsg + itn + 1)
                t.wait_ge(ctx.sem("a_bb"), b_abb + itn + 1)
                ctx.inc(t.matmul(pb7[:, 0:512], c_ones1f[:], rows[0:1, 0:512],
                                 start=True, stop=True), "t_rs")
        blk.tensor(f_ten_real)

        def f_vec(v):
            for itn, (h_, s_) in enumerate(iters()):
                kl = kbs(s_)
                t0 = tiles_before[(h_, s_)]
                ntile = 4 * len(kl)
                p2 = itn % 2
                if s_ == 0:
                    for tb in range(4):
                        v.wait_ge(ctx.sem("a_bq"), b_abq + 4 * h_ + tb + 1)
                        ctx.inc(v.tensor_scalar_mul(
                            bnd_row[0:1, 512 * tb:512 * (tb + 1)],
                            bnd_row[0:1, 512 * tb:512 * (tb + 1)], maxk[0:1, :]),
                            "d_bn")
                v.wait_ge(ctx.sem("a_bb"), b_abb + itn + 1)
                for ti in range(ntile):
                    kb, ks = divmod(ti, 4)
                    kcol, bcol, isdiag = kl[kb]
                    gt = t0 + ti
                    v.wait_ge(ctx.sem("t_st"), b_tst + gt + 1)
                    if gt >= 2:
                        v.wait_ge(ctx.sem("a_exp"), b_aexp + gt - 1)
                    if isdiag:
                        e1 = ctx.inc(v.tensor_add(pb[gt % 2][:, 0:512], pb[gt % 2][:, 0:512],
                                                  att_tri[:, 512 * ks:512 * (ks + 1)]),
                                     "d_tri")
                        v.wait_ge(ctx.sem("d_tri"), e1)
                    ctx.inc(v.scalar_tensor_tensor(
                        out=att_stt[gt % 2], in0=pb[gt % 2][:, 0:512],
                        scalar=c_bigneg[:, bcol:bcol + 1], in1=att_bb,
                        op0=ALU.add, op1=ALU.subtract), "d_stt")
                v.wait_ge(ctx.sem("t_sg"), b_tsg + t0 + ntile)
                if itn >= 1:
                    v.wait_ge(ctx.sem("t_rs"), b_trs + itn)
                ctx.inc(v.reciprocal(rows[0:1, 0:512], pb[4 + p2][0:1, 0:512]), "d_rsg")
                v.wait_ge(ctx.sem("a_rs"), b_ars + itn + 1)
                v.wait_ge(ctx.sem("t_pv"), b_tpv + t0 + ntile)
                ctx.inc(v.tensor_mul(
                    bufA[:, 2048 * h_ + 512 * s_:2048 * h_ + 512 * s_ + 512],
                    pb[2 + p2][:, 0:512], att_rs), "d_of")
        blk.vector(f_vec)

        def f_sca(sc):
            sc.wait_ge(ctx.sem("a_mk"), 1)
            for itn, (h_, s_) in enumerate(iters()):
                kl = kbs(s_)
                t0 = tiles_before[(h_, s_)]
                ntile = 4 * len(kl)
                if s_ == 0:
                    if itn >= 1:
                        sc.wait_ge(ctx.sem("t_bn"), b_tbn + 4 * h_)
                    ctx.inc(sc.activation(att_sq[:, 0:2048],
                                          bufB[:, 2048 * h_:2048 * (h_ + 1)],
                                          AF.Square), "a_qs")
                    for tb in range(4):
                        sc.wait_ge(ctx.sem("t_bn"), b_tbn + 4 * h_ + tb + 1)
                        if itn >= 1 and tb == 0:
                            sc.wait_ge(ctx.sem("t_bb"), b_tbb + itn)
                        ctx.inc(sc.activation(bnd_row[0:1, 512 * tb:512 * (tb + 1)],
                                              pb7[0:1, 0:512], AF.Sqrt), "a_bq")
                sc.wait_ge(ctx.sem("t_bb"), b_tbb + itn + 1)
                if itn >= 1:
                    sc.wait_ge(ctx.sem("d_stt"), b_dstt + t0)
                ctx.inc(sc.activation(att_bb, pb7[:, 0:512], AF.Copy), "a_bb")
                for ti in range(ntile):
                    gt = t0 + ti
                    sc.wait_ge(ctx.sem("d_stt"), b_dstt + gt + 1)
                    if gt >= 2:
                        sc.wait_ge(ctx.sem("t_pv"), b_tpv + gt - 1)
                    ctx.inc(sc.activation(att_pt[gt % 2], att_stt[gt % 2], AF.Exp,
                                          scale=SCALE), "a_exp")
                sc.wait_ge(ctx.sem("t_rs"), b_trs + itn + 1)
                ctx.inc(sc.activation(att_rs, pb7[:, 0:512], AF.Copy), "a_rs")
        blk.scalar(f_sca)

    def ph_proj():
        b_sxr = V("s_xr")
        b_swp = V("s_wp")
        b_tpj = V("t_pj")
        b_apj = V("a_pj")
        b_tps = V("t_ps")
        b_dprw = V("d_prw")
        b_apm = V("a_pm")
        b_dpv = V("d_pv")
        b_apsq = V("a_psq")
        b_dprr = V("d_prr")
        b_tpbc = V("t_pbc")
        b_dx2 = V("d_x2")
        b_al2 = V("a_l2")
        b_tl2s = V("t_l2s")
        b_dl2r = V("d_l2r")
        b_al2m = V("a_l2m")
        b_dl2v = V("d_l2v")
        b_al2q = V("a_l2q")
        b_dl2r2 = V("d_l2r2")
        b_tl2bc = V("t_l2bc")
        b_dln2 = V("d_ln2")
        b_txt = V("t_xt")
        b_axt = V("a_xt")
        b_sx2w = V("s_x2w")

        pr_bt = [k_bfv[:, 4096:4352], k_bfv[:, 4352:4608]]
        pr_st = [k_bfv[:, 4608:4864], k_bfv[:, 4864:5120]]

        def f_sync(sy):
            for pt in range(NPT):
                if pt >= 1:
                    sy.wait_ge(ctx.sem("d_x2"), b_dx2 + 48 * pt)
                ctx.inc(sy.dma_start(
                    out=pr_xr.rearrange("p (c n) -> p c n", c=16),
                    in_=x_ext[32 + pt].rearrange("c p n -> p c n")), "s_xr", 16)
                for tsb in range(2):
                    sy.wait_ge(ctx.sem("a_xt"), b_axt + 16 * (2 * pt + tsb + 1))
                    ctx.inc(sy.dma_start(
                        out=x2_dram[PTC * pt + 128 * tsb:PTC * pt + 128 * (tsb + 1), :],
                        in_=pr_x2t), "s_x2w", 16)
        blk.sync(f_sync)

        def f_gp(g):
            for pt in range(NPT):
                for oc in range(16):
                    idx = 16 * pt + oc
                    if idx >= 2:
                        g.wait_ge(ctx.sem("t_pj"), b_tpj + idx - 1)
                    ctx.inc(g.dma_start(
                        out=wp_buf[oc % 2].rearrange("p (c n) -> p c n", c=16),
                        in_=w_proj_d[oc].rearrange("c p n -> p c n")), "s_wp", 16)
        blk.gpsimd(f_gp)

        def f_ten(t):
            for pt in range(NPT):
                for oc in range(16):
                    idx = 16 * pt + oc
                    t.wait_ge(ctx.sem("s_wp"), b_swp + 16 * (idx + 1))
                    if idx >= 2:
                        t.wait_ge(ctx.sem("a_pj"), b_apj + 3 * (idx - 1))
                    mm = None
                    for fc in range(16):
                        mm = t.matmul(pb[oc % 2][:, 0:PTC],
                                      wp_buf[oc % 2][:, 128 * fc:128 * (fc + 1)],
                                      bufA[:, 2048 * fc + PTC * pt:2048 * fc + PTC * (pt + 1)],
                                      start=(fc == 0), stop=(fc == 15))
                    ctx.inc(mm, "t_pj")
                    t.wait_ge(ctx.sem("a_pj"), b_apj + 3 * (idx + 1))
                    if pt >= 1 and oc == 0:
                        t.wait_ge(ctx.sem("d_prw"), b_dprw + 2 * pt)
                    ctx.inc(t.matmul(pb[2][0:1, 0:PTC], c_ones_bf[:], pr_bt[oc % 2],
                                     start=(oc == 0), stop=(oc == 15)), "t_ps")
                    ctx.inc(t.matmul(pb[3][0:1, 0:PTC], c_ones_bf[:], pr_st[oc % 2],
                                     start=(oc == 0), stop=(oc == 15)), "t_ps")
                t.wait_ge(ctx.sem("d_prr"), b_dprr + pt + 1)
                if pt >= 1:
                    t.wait_ge(ctx.sem("d_x2"), b_dx2 + 48 * pt)
                ctx.inc(t.matmul(pb[4][:, 0:PTC], c_ones1f[:], rows[0:1, 0:PTC],
                                 start=True, stop=True), "t_pbc")
                ctx.inc(t.matmul(pb[5][:, 0:PTC], c_ones1f[:], rows[0:1, 768:768 + PTC],
                                 start=True, stop=True), "t_pbc")
                for fc in range(16):
                    lidx = 16 * pt + fc
                    t.wait_ge(ctx.sem("a_l2"), b_al2 + 2 * (lidx + 1))
                    if pt >= 1 and fc == 0:
                        t.wait_ge(ctx.sem("d_l2r"), b_dl2r + 2 * pt)
                    ctx.inc(t.matmul(pb[2][0:1, 0:PTC], c_ones_bf[:], pr_bt[fc % 2],
                                     start=(fc == 0), stop=(fc == 15)), "t_l2s")
                    ctx.inc(t.matmul(pb[3][0:1, 0:PTC], c_ones_bf[:], pr_st[fc % 2],
                                     start=(fc == 0), stop=(fc == 15)), "t_l2s")
                t.wait_ge(ctx.sem("d_l2r2"), b_dl2r2 + pt + 1)
                t.wait_ge(ctx.sem("d_x2"), b_dx2 + 48 * (pt + 1))
                ctx.inc(t.matmul(pb[4][:, 0:PTC], c_ones1f[:], rows[0:1, 0:PTC],
                                 start=True, stop=True), "t_l2bc")
                ctx.inc(t.matmul(pb[5][:, 0:PTC], c_ones1f[:], rows[0:1, 768:768 + PTC],
                                 start=True, stop=True), "t_l2bc")
                t.wait_ge(ctx.sem("d_ln2"), b_dln2 + 16 * (pt + 1))
                for tsb in range(2):
                    for fc in range(16):
                        ti = 32 * pt + 16 * tsb + fc
                        if ti >= 2:
                            t.wait_ge(ctx.sem("a_xt"), b_axt + ti - 1)
                        ctx.inc(t.transpose(
                            pb[4 + fc % 2][:, 0:128],
                            pr_a1[:, PTC * fc + 128 * tsb:PTC * fc + 128 * (tsb + 1)],
                            c_ident_f[:]), "t_xt")
        blk.tensor(f_ten)

        def f_sca(sc):
            for pt in range(NPT):
                for oc in range(16):
                    idx = 16 * pt + oc
                    sc.wait_ge(ctx.sem("t_pj"), b_tpj + idx + 1)
                    if pt >= 1 and oc == 0:
                        sc.wait_ge(ctx.sem("t_xt"), b_txt + 32 * pt)
                    if idx >= 2:
                        sc.wait_ge(ctx.sem("t_ps"), b_tps + 2 * (idx - 1))
                    ctx.inc(sc.activation(pr_a1[:, PTC * oc:PTC * (oc + 1)],
                                          pb[oc % 2][:, 0:PTC], AF.Identity,
                                          bias=c_bproj[:, oc:oc + 1]), "a_pj")
                    e = ctx.inc(sc.activation(pr_bt[oc % 2],
                                              pb[oc % 2][:, 0:PTC], AF.Identity,
                                              bias=c_bproj[:, oc:oc + 1]), "a_pj")
                    sc.wait_ge(ctx.sem("a_pj"), e)
                    ctx.inc(sc.activation(pr_st[oc % 2], pr_bt[oc % 2],
                                          AF.Square), "a_pj")
                sc.wait_ge(ctx.sem("d_prw"), b_dprw + 2 * (pt + 1))
                if pt >= 1:
                    sc.wait_ge(ctx.sem("d_l2r2"), b_dl2r2 + pt)
                ctx.inc(sc.activation(rows[0:1, 512:768], rows[0:1, 0:256], AF.Square),
                        "a_pm")
                sc.wait_ge(ctx.sem("d_pv"), b_dpv + pt + 1)
                ctx.inc(sc.activation(rows[0:1, 512:768], rows[0:1, 768:1024], AF.Sqrt,
                                      bias=c_eps[0:1, :]), "a_psq")
                sc.wait_ge(ctx.sem("d_x2"), b_dx2 + 48 * (pt + 1))
                for fc in range(16):
                    lidx = 16 * pt + fc
                    if lidx >= 2:
                        sc.wait_ge(ctx.sem("t_l2s"), b_tl2s + 2 * (lidx - 1))
                    e = ctx.inc(sc.activation(pr_bt[fc % 2],
                                              pr_a1[:, PTC * fc:PTC * (fc + 1)],
                                              AF.Copy), "a_l2")
                    sc.wait_ge(ctx.sem("a_l2"), e)
                    ctx.inc(sc.activation(pr_st[fc % 2], pr_bt[fc % 2], AF.Square),
                            "a_l2")
                sc.wait_ge(ctx.sem("d_l2r"), b_dl2r + 2 * (pt + 1))
                sc.wait_ge(ctx.sem("d_prr"), b_dprr + pt + 1)
                ctx.inc(sc.activation(rows[0:1, 512:768], rows[0:1, 0:256], AF.Square),
                        "a_l2m")
                sc.wait_ge(ctx.sem("d_l2v"), b_dl2v + pt + 1)
                ctx.inc(sc.activation(rows[0:1, 512:768], rows[0:1, 768:1024], AF.Sqrt,
                                      bias=c_eps[0:1, :]), "a_l2q")
                for tsb in range(2):
                    for fc in range(16):
                        ti = 32 * pt + 16 * tsb + fc
                        sc.wait_ge(ctx.sem("t_xt"), b_txt + ti + 1)
                        ctx.inc(sc.activation(pr_x2t[:, 128 * fc:128 * (fc + 1)],
                                              pb[4 + fc % 2][:, 0:128],
                                              AF.Copy), "a_xt")
                    if not (pt == NPT - 1 and tsb == 1):
                        sc.wait_ge(ctx.sem("s_x2w"), b_sx2w + 16 * (2 * pt + tsb + 1))
        blk.scalar(f_sca)

        def f_vec(v):
            for pt in range(NPT):
                v.wait_ge(ctx.sem("t_ps"), b_tps + 2 * 16 * (pt + 1))
                if pt >= 1:
                    v.wait_ge(ctx.sem("t_l2bc"), b_tl2bc + 2 * pt)
                ctx.inc(v.tensor_scalar_mul(rows[0:1, 0:PTC], pb[2][0:1, 0:PTC], RH), "d_prw")
                ctx.inc(v.tensor_scalar_mul(rows[0:1, 256:256 + PTC], pb[3][0:1, 0:PTC], RH), "d_prw")
                v.wait_ge(ctx.sem("a_pm"), b_apm + pt + 1)
                ctx.inc(v.tensor_sub(rows[0:1, 768:768 + PTC], rows[0:1, 256:256 + PTC],
                                     rows[0:1, 512:512 + PTC]), "d_pv")
                v.wait_ge(ctx.sem("a_psq"), b_apsq + pt + 1)
                ctx.inc(v.reciprocal(rows[0:1, 768:768 + PTC], rows[0:1, 512:512 + PTC]), "d_prr")
                v.wait_ge(ctx.sem("t_pbc"), b_tpbc + 2 * (pt + 1))
                v.wait_ge(ctx.sem("s_xr"), b_sxr + 16 * (pt + 1))
                for fc in range(16):
                    tm = pr_tmp[fc % 2]
                    a1 = pr_a1[:, PTC * fc:PTC * (fc + 1)]
                    e1 = ctx.inc(v.tensor_sub(tm, a1, pb[4][:, 0:PTC]), "dse")
                    v.wait_ge(ctx.sem("dse"), e1)
                    e2 = ctx.inc(v.tensor_mul(tm, tm, pb[5][:, 0:PTC]), "dse")
                    v.wait_ge(ctx.sem("dse"), e2)
                    ctx.inc(v.tensor_add(a1, tm, pr_xr[:, PTC * fc:PTC * (fc + 1)]),
                            "d_x2", 3)
                v.wait_ge(ctx.sem("t_l2s"), b_tl2s + 2 * 16 * (pt + 1))
                ctx.inc(v.tensor_scalar_mul(rows[0:1, 0:PTC], pb[2][0:1, 0:PTC], RH), "d_l2r")
                ctx.inc(v.tensor_scalar_mul(rows[0:1, 256:256 + PTC], pb[3][0:1, 0:PTC], RH), "d_l2r")
                v.wait_ge(ctx.sem("a_l2m"), b_al2m + pt + 1)
                ctx.inc(v.tensor_sub(rows[0:1, 768:768 + PTC], rows[0:1, 256:256 + PTC],
                                     rows[0:1, 512:512 + PTC]), "d_l2v")
                v.wait_ge(ctx.sem("a_l2q"), b_al2q + pt + 1)
                ctx.inc(v.reciprocal(rows[0:1, 768:768 + PTC], rows[0:1, 512:512 + PTC]), "d_l2r2")
                v.wait_ge(ctx.sem("t_l2bc"), b_tl2bc + 2 * (pt + 1))
                for fc in range(16):
                    tm = pr_tmp[fc % 2]
                    a1 = pr_a1[:, PTC * fc:PTC * (fc + 1)]
                    e1 = ctx.inc(v.tensor_sub(tm, a1, pb[4][:, 0:PTC]), "dse")
                    v.wait_ge(ctx.sem("dse"), e1)
                    ctx.inc(v.tensor_mul(
                        bufB[:, 2048 * fc + PTC * pt:2048 * fc + PTC * (pt + 1)],
                        tm, pb[5][:, 0:PTC]), "d_ln2")
        blk.vector(f_vec)

    def ph_router():
        b_trt = V("t_rt")
        b_drt = V("d_rt")
        b_art = V("a_rt")

        def f_ten(t):
            for ts in range(16):
                if ts >= 1:
                    t.wait_ge(ctx.sem("a_rt"), b_art + 2 * ts)
                mm = None
                for fc in range(16):
                    mm = t.matmul(pb7[:, 0:17],
                                  bufB[:, 2048 * fc + 128 * ts:2048 * fc + 128 * (ts + 1)],
                                  c_wrt[:, 17 * fc:17 * (fc + 1)],
                                  start=(fc == 0), stop=(fc == 15))
                ctx.inc(mm, "t_rt")
        blk.tensor(f_ten)

        def f_vec(v):
            for ts in range(16):
                v.wait_ge(ctx.sem("t_rt"), b_trt + ts + 1)
                e = ctx.inc(v.reduce_max(sig[:, 0:1], pb7[:, 0:16], axis=AX.X), "dse")
                v.wait_ge(ctx.sem("dse"), e)
                ctx.inc(v.tensor_scalar_mul(sig[:, 1:2], sig[:, 0:1], -1.0), "d_rt1")
                v.wait_ge(ctx.sem("a_rt1"), ts + 1)
                e = ctx.inc(v.reciprocal(sig[:, 3:4], sig[:, 2:3]), "dse")
                v.wait_ge(ctx.sem("dse"), e)
                e = ctx.inc(v.tensor_scalar_mul(rt_pr, rt_ex, sig[:, 3:4]), "dse")
                v.wait_ge(ctx.sem("dse"), e)
                e = ctx.inc(v.tensor_copy(rt_wk, rt_pr), "dse")
                v.wait_ge(ctx.sem("dse"), e)
                for k in range(3):
                    e = ctx.inc(v.reduce_max(sig[:, 4:5], rt_wk, axis=AX.X), "dse")
                    v.wait_ge(ctx.sem("dse"), e)
                    e = ctx.inc(v.tensor_scalar(out=rt_mk, in0=rt_wk,
                                                scalar1=sig[:, 4:5], scalar2=BIGNEG,
                                                op0=ALU.is_ge, op1=ALU.mult), "dse")
                    v.wait_ge(ctx.sem("dse"), e)
                    e = ctx.inc(v.tensor_add(rt_wk, rt_wk, rt_mk), "dse")
                    v.wait_ge(ctx.sem("dse"), e)
                e = ctx.inc(v.reduce_max(sig[:, 4:5], rt_wk, axis=AX.X), "dse")
                v.wait_ge(ctx.sem("dse"), e)
                e = ctx.inc(v.tensor_scalar(out=rt_mk, in0=rt_pr,
                                            scalar1=sig[:, 4:5], scalar2=None,
                                            op0=ALU.is_ge), "dse")
                v.wait_ge(ctx.sem("dse"), e)
                ctx.inc(v.tensor_mul(combine[:, 17 * ts:17 * ts + 16], rt_pr, rt_mk),
                        "d_rt")
        blk.vector(f_vec)

        def f_sca(sc):
            for ts in range(16):
                sc.wait_ge(ctx.sem("d_rt1"), ts + 1)
                ctx.inc(sc.activation(rt_ex, pb7[:, 0:16], AF.Exp,
                                      bias=sig[:, 1:2], accum_out=sig[:, 2:3]), "a_rt1")
                ctx.inc(sc.activation(combine[:, 17 * ts + 16:17 * ts + 17],
                                      pb7[:, 16:17], AF.Sigmoid), "a_rt", 2)
        blk.scalar(f_sca)

    def ph_ffn(hf):
        sfx = ""
        b_sfw = V("s_fw" + sfx)
        b_sfd = V("s_fd" + sfx)
        b_tgu = V("t_gu" + sfx)
        b_asi = V("a_si" + sfx)
        b_da = V("d_a" + sfx)
        b_ty = V("t_y" + sfx)
        b_dy = V("d_y" + sfx)
        b_dms = V("d_ms" + sfx)

        NES = len(FFN_ES)

        def srcs(pe):
            wi, ce, fam = FFN_ES[pe]
            if fam == "moe":
                return w_g_d[wi], w_u_d[wi], w_d_d[wi], ce
            return ws_g_d[wi], ws_u_d[wi], ws_d_d[wi], ce

        def f_vec0(v):
            if FFN_MEMSET:
                ctx.inc(v.memset(y_acc[:, 0:8192], 0.0), "d_ms" + sfx)
                ctx.inc(v.memset(y_acc[:, 8192:16384], 0.0), "d_ms" + sfx)
            else:
                ctx.inc(v.engine_nop(), "d_ms" + sfx)
                ctx.inc(v.engine_nop(), "d_ms" + sfx)
        blk.vector(f_vec0)

        def f_gp(g):
            g.wait_ge(ctx.sem("d_ms" + sfx), b_dms + 2)
            for pe in range(NES):
                gsrc, usrc, dsrc, ce = srcs(pe)
                for gg in range(2):
                    for ic in range(11):
                        icg = (pe * 2 + gg) * 11 + ic
                        if icg >= 2:
                            g.wait_ge(ctx.sem("d_a" + sfx), b_da + icg - 1)
                        ctx.inc(g.dma_start(
                            out=wg_buf[icg % 2].rearrange("p (c n) -> p c n", c=16),
                            in_=gsrc[ic].rearrange("c p n -> p c n")), "s_fw" + sfx, 16)
                        ctx.inc(g.dma_start(
                            out=wu_buf[icg % 2].rearrange("p (c n) -> p c n", c=16),
                            in_=usrc[ic].rearrange("c p n -> p c n")), "s_fw" + sfx, 16)
                    for hgc in range(8):
                        hgg = (pe * 2 + gg) * 8 + hgc
                        if hgg >= 2:
                            g.wait_ge(ctx.sem("t_y" + sfx), b_ty + 4 * (hgg - 1))
                        ctx.inc(g.dma_start(
                            out=wd_buf[hgc % 2].rearrange("p (c n) -> p c n", c=11),
                            in_=dsrc[hgc].rearrange("c p n -> p c n")), "s_fd" + sfx, 16)
        blk.gpsimd(f_gp)

        def f_ten(t):
            for pe in range(NES):
                for gg in range(2):
                    rhs0 = 1024 * hf + 512 * gg
                    for ic in range(11):
                        icg = (pe * 2 + gg) * 11 + ic
                        t.wait_ge(ctx.sem("s_fw" + sfx), b_sfw + 32 * (icg + 1))
                        if SERIAL_FFN and icg >= 1:
                            t.wait_ge(ctx.sem("a_si" + sfx), b_asi + icg)
                            t.wait_ge(ctx.sem("d_a" + sfx), b_da + icg)
                        elif icg >= 2:
                            t.wait_ge(ctx.sem("a_si" + sfx), b_asi + icg - 1)
                            t.wait_ge(ctx.sem("d_a" + sfx), b_da + icg - 1)
                        for which, buf in ((0, wg_buf), (1, wu_buf)):
                            mm = None
                            for fc in range(16):
                                mm = t.matmul(
                                    pb[which * 2 + icg % 2][:, 0:512],
                                    buf[icg % 2][:, 128 * fc:128 * (fc + 1)],
                                    bufB[:, 2048 * fc + rhs0:2048 * fc + rhs0 + 512],
                                    start=(fc == 0), stop=(fc == 15))
                            ctx.inc(mm, "t_gu" + sfx)
                    for hgc in range(8):
                        hgg = (pe * 2 + gg) * 8 + hgc
                        t.wait_ge(ctx.sem("s_fd" + sfx), b_sfd + 16 * (hgg + 1))
                        t.wait_ge(ctx.sem("d_a" + sfx), b_da + (pe * 2 + gg) * 11 + 11)
                        for tsub in range(4):
                            yt = 4 * hgg + tsub
                            if SERIAL_FFN and yt >= 1:
                                t.wait_ge(ctx.sem("d_y" + sfx), b_dy + yt)
                            elif yt >= 2:
                                t.wait_ge(ctx.sem("d_y" + sfx), b_dy + yt - 1)
                            mm = None
                            for ic in range(11):
                                mm = t.matmul(
                                    pb[4 + yt % 2][:, 0:256],
                                    ff_a[:, 512 * ic + 128 * tsub:512 * ic + 128 * (tsub + 1)],
                                    wd_buf[hgc % 2][:, 256 * ic:256 * (ic + 1)],
                                    start=(ic == 0), stop=(ic == 10))
                            ctx.inc(mm, "t_y" + sfx)
        blk.tensor(f_ten)

        def f_sca(sc):
            for pe in range(NES):
                for gg in range(2):
                    for ic in range(11):
                        icg = (pe * 2 + gg) * 11 + ic
                        sc.wait_ge(ctx.sem("t_gu" + sfx), b_tgu + 2 * icg + 1)
                        if icg >= 2:
                            sc.wait_ge(ctx.sem("d_a" + sfx), b_da + icg - 1)
                        ctx.inc(sc.activation(ff_sg[icg % 2], pb[icg % 2][:, 0:512],
                                              AF.Silu), "a_si" + sfx)
        blk.scalar(f_sca)

        def f_vec(v):
            for pe in range(NES):
                _, ce, _ = FFN_ES[pe]
                for gg in range(2):
                    for ic in range(11):
                        icg = (pe * 2 + gg) * 11 + ic
                        v.wait_ge(ctx.sem("a_si" + sfx), b_asi + icg + 1)
                        v.wait_ge(ctx.sem("t_gu" + sfx), b_tgu + 2 * (icg + 1))
                        if pe * 2 + gg >= 1:
                            v.wait_ge(ctx.sem("t_y" + sfx), b_ty + 32 * (pe * 2 + gg))
                        ctx.inc(v.tensor_mul(ff_a[:, 512 * ic:512 * (ic + 1)],
                                             pb[2 + icg % 2][:, 0:512],
                                             ff_sg[icg % 2]), "d_a" + sfx)
                    for hgc in range(8):
                        hgg = (pe * 2 + gg) * 8 + hgc
                        for tsub in range(4):
                            yt = 4 * hgg + tsub
                            v.wait_ge(ctx.sem("t_y" + sfx), b_ty + yt + 1)
                            gts = 4 * gg + tsub
                            ysl = y_acc[:, 2048 * gts + 256 * hgc:2048 * gts + 256 * (hgc + 1)]
                            ccol = combine[:, 17 * (8 * hf + gts) + ce:17 * (8 * hf + gts) + ce + 1]
                            n = ctx.cnt["d_y" + sfx] + 1 - b_dy
                            if n > 64:
                                v.wait_ge(ctx.sem("d_y" + sfx), b_dy + n - 64)
                            ctx.inc(v.scalar_tensor_tensor(
                                out=ysl, in0=pb[4 + yt % 2][:, 0:256],
                                scalar=ccol, in1=ysl,
                                op0=ALU.mult, op1=ALU.add), "d_y" + sfx)
        blk.vector(f_vec)

    def ph_out(hf):
        sfx = ""
        b_do = V("d_o" + sfx)
        b_dmu = V("d_mu" + sfx)
        b_ay = V("a_ysq" + sfx)
        b_am2 = V("a_m2" + sfx)
        b_dt2 = V("d_t2" + sfx)
        b_dvr = V("d_vr" + sfx)
        b_asr = V("a_sr" + sfx)
        b_sxr2 = V("s_xr2" + sfx)
        b_sout = V("s_out")

        def f_sync(sy):
            sy.wait_ge(ctx.sem("d_y" + sfx), V("d_y" + sfx))
            for gts in range(8):
                if gts >= 1:
                    sy.wait_ge(ctx.sem("d_o" + sfx), b_do + gts)
                ctx.inc(sy.dma_start(
                    out=out_x2r,
                    in_=x2_dram[1024 * hf + 128 * gts:1024 * hf + 128 * (gts + 1), :]),
                    "s_xr2" + sfx, 16)
                sy.wait_ge(ctx.sem("d_o" + sfx), b_do + gts + 1)
                ctx.inc(sy.dma_start(
                    out=out_d[1024 * hf + 128 * gts:1024 * hf + 128 * (gts + 1), :],
                    in_=out_ot), "s_out", 16)
        blk.sync(f_sync)

        def f_vec(v):
            v.wait_ge(ctx.sem("t_y" + sfx), V("t_y" + sfx))
            for gts in range(8):
                ysl = y_acc[:, 2048 * gts:2048 * (gts + 1)]
                e = ctx.inc(v.reduce_sum(sig[:, 0:1], ysl, axis=AX.X), "dse")
                v.wait_ge(ctx.sem("dse"), e)
                if gts >= 1:
                    v.wait_ge(ctx.sem("a_m2" + sfx), b_am2 + gts)
                ctx.inc(v.tensor_scalar_mul(sig[:, 2:3], sig[:, 0:1], RH),
                        "d_mu" + sfx)
                v.wait_ge(ctx.sem("a_ysq" + sfx), b_ay + gts + 1)
                e = ctx.inc(v.tensor_scalar_mul(sig[:, 3:4], sig[:, 1:2], RH),
                            "d_t2" + sfx)
                v.wait_ge(ctx.sem("d_t2" + sfx), e)
                v.wait_ge(ctx.sem("a_m2" + sfx), b_am2 + gts + 1)
                e = ctx.inc(v.tensor_sub(sig[:, 5:6], sig[:, 3:4], sig[:, 4:5]),
                            "d_vr" + sfx)
                v.wait_ge(ctx.sem("d_vr" + sfx), b_dvr + gts + 1)
                v.wait_ge(ctx.sem("a_sr" + sfx), b_asr + gts + 1)
                e = ctx.inc(v.reciprocal(sig[:, 7:8], sig[:, 6:7]), "dse")
                v.wait_ge(ctx.sem("dse"), e)
                e = ctx.inc(v.tensor_scalar(out=out_t1, in0=ysl,
                                            scalar1=sig[:, 2:3], scalar2=sig[:, 7:8],
                                            op0=ALU.subtract, op1=ALU.mult), "dse")
                v.wait_ge(ctx.sem("dse"), e)
                v.wait_ge(ctx.sem("s_xr2" + sfx), b_sxr2 + 16 * (gts + 1))
                if gts >= 1:
                    v.wait_ge(ctx.sem("s_out"), b_sout + 16 * gts)
                ctx.inc(v.tensor_add(out_ot, out_t1, out_x2r), "d_o" + sfx)
        blk.vector(f_vec)

        def f_sca(sc):
            for gts in range(8):
                ysl = y_acc[:, 2048 * gts:2048 * (gts + 1)]
                sc.wait_ge(ctx.sem("t_y" + sfx), V("t_y" + sfx))
                if gts >= 1:
                    sc.wait_ge(ctx.sem("d_t2" + sfx), b_dt2 + gts)
                ctx.inc(sc.activation(ff_ysq, ysl, AF.Square, accum_out=sig[:, 1:2]),
                        "a_ysq" + sfx)
                sc.wait_ge(ctx.sem("d_mu" + sfx), b_dmu + gts + 1)
                ctx.inc(sc.activation(sig[:, 4:5], sig[:, 2:3], AF.Square),
                        "a_m2" + sfx)
                sc.wait_ge(ctx.sem("d_vr" + sfx), b_dvr + gts + 1)
                ctx.inc(sc.activation(sig[:, 6:7], sig[:, 5:6], AF.Sqrt,
                                      bias=c_eps[:]), "a_sr" + sfx)
        blk.scalar(f_sca)

    def ph_dbg(names):
        if not DEBUG:
            return
        def f_sync(sy):
            for nm in names:
                src = {"dbg_k": k_sb[:], "dbg_v": v_sb[:], "dbg_q": bufB[:],
                       "dbg_lnd": bufA[:], "dbg_o": bufA[:], "dbg_ln2": bufB[:],
                       "dbg_cmb": combine[:], "dbg_y": y_acc[:, 0:16384],
                       "dbg_a": ff_a}[nm]
                ctx.inc(sy.dma_start(out=dbg[nm], in_=src), "s_dbg", 16)
            sy.wait_ge(ctx.sem("s_dbg"), ctx.cnt["s_dbg"])
        blk.sync(f_sync)

    def ph_dbg_x2():
        if not DEBUG:
            return
        def f_sync(sy):
            ctx.inc(sy.dma_start(out=dbg["dbg_x2"], in_=x2_dram), "s_dbg", 16)
            sy.wait_ge(ctx.sem("s_dbg"), ctx.cnt["s_dbg"])
        blk.sync(f_sync)

    def ph_final():
        def f_vec(v):
            v.wait_ge(ctx.sem("s_out"), ctx.cnt["s_out"])
        blk.vector(f_vec)

    ph_ld()
    ph_p0a()
    ph_barrier()
    if STAGE in ("p0", "attn", "proj", "full", "ffn0"):
        ph_p0b()
        ph_barrier()
    if STAGE == "p0":
        ph_dbg(["dbg_k", "dbg_v", "dbg_q", "dbg_lnd"])
    if STAGE in ("attn", "proj", "full", "ffn0"):
        ph_bounds()
        ph_barrier()
        ph_attn()
        ph_barrier()
    if STAGE == "attn":
        ph_dbg(["dbg_k", "dbg_v", "dbg_o"])
    if STAGE in ("proj", "full", "ffn0"):
        ph_proj()
        ph_barrier()
        ph_router()
        ph_barrier()
    if STAGE == "proj":
        ph_dbg(["dbg_ln2", "dbg_cmb"])
        ph_dbg_x2()
    if STAGE in ("full", "ffn0"):
        if FFN_ES:
            ph_ffn(0)
            ph_barrier()
    if STAGE == "ffn0":
        ph_dbg(DBG_FFN_LIST)
        if not SKIP_OUT:
            ph_out(0)
            ph_barrier()
            ph_dbg_x2()
            ph_final()
        else:
            def f_fin(v):
                v.wait_ge(ctx.sem("s_dbg"), ctx.cnt["s_dbg"])
            blk.vector(f_fin)
    if STAGE in ("full",):
        ph_out(0)
        ph_barrier()
        ph_ffn(1)
        ph_barrier()
        ph_out(1)
        ph_final()
    blk.__exit__(None, None, None)
    return nc



def assemble(results):
    out = np.zeros((2, S, H), np.float32)
    for c in range(NCORE):
        b, r = divmod(c, 4)
        ch = chunks_of_core(r)
        o = results[c]["out"]
        for s_i, cc in enumerate(ch):
            out[b, CH * cc:CH * (cc + 1), :] = o[CH * s_i:CH * (s_i + 1), :]
    return out


def kernel(**inputs):
    in_maps = stage_inputs(inputs)
    nc = build_program()
    res = run_bass_kernel_spmd(nc, in_maps, core_ids=list(range(NCORE)))
    return assemble(res.results)


# version 04
# speedup vs baseline: 1.0323x; 1.0323x over previous
import numpy as np
import ml_dtypes
import concourse.bass as bass
import concourse.mybir as mybir
from concourse.bass_utils import run_bass_kernel_spmd

F32 = mybir.dt.float32
BF16 = mybir.dt.bfloat16
AF = mybir.ActivationFunctionType
ALU = mybir.AluOpType
AX = mybir.AxisListType

H = 2048
NH = 16
NF = 16
S = 8192
SCALE = float(128) ** -0.5
NCORE = 8
CH = 512
T_OWN = 2048
T_EXT = S + T_OWN
SLOT_PAD = [3, 7, 11, 15]
NPLAIN = sum(SLOT_PAD)
BIGNEG = -1.0e6
EPS = 1e-5
RH = 1.0 / H
E = 16

TC = 256
NTC = T_EXT // TC
PTC = 256
NPT = T_OWN // PTC

FFN_ES = [(e, e, "moe") for e in range(E)] + [(i, 16, "sh") for i in range(4)]

STAGE = "full"
DEBUG = False
SKIP_OUT = False
SERIAL_FFN = False
FFN_MEMSET = True
DBG_FFN_LIST = ["dbg_y", "dbg_a", "dbg_ln2", "dbg_cmb"]


def chunks_of_core(r):
    return [r, 7 - r, 8 + r, 15 - r]



def stage_inputs(inp):
    x = np.asarray(inp["L0_attn_x"])
    W_qkv = np.asarray(inp["L0_attn_W_qkv"])
    b_qkv = np.asarray(inp["L0_attn_b_qkv"])
    W_proj = np.asarray(inp["L0_attn_W_proj"])
    b_proj = np.asarray(inp["L0_attn_b_proj"])
    w_router = np.asarray(inp["L0_ffn_w_router"])
    w_gate = np.asarray(inp["L0_ffn_w_gate"])
    w_up = np.asarray(inp["L0_ffn_w_up"])
    w_down = np.asarray(inp["L0_ffn_w_down"])
    ws_gate = np.asarray(inp["L0_ffn_ws_gate"])
    ws_up = np.asarray(inp["L0_ffn_ws_up"])
    ws_down = np.asarray(inp["L0_ffn_ws_down"])
    w_sg = np.asarray(inp["L0_ffn_w_shared_gate"])

    C = np.ascontiguousarray
    sh = {}
    sh["w_kv"] = C(W_qkv[:, 2048:].reshape(16, 128, 256).transpose(1, 0, 2).reshape(128, 16 * 256))
    sh["w_q"] = C(W_qkv[:, :2048].reshape(16, 128, 16, 128).transpose(2, 1, 0, 3).reshape(16, 128, 16 * 128))
    sh["w_proj"] = C(W_proj.reshape(16, 128, 16, 128).transpose(2, 1, 0, 3).reshape(16, 128, 16 * 128))
    sh["b_q"] = C(b_qkv[:2048].reshape(16, 128).T)
    sh["b_k"] = C(b_qkv[2048:2176].reshape(128, 1))
    sh["b_v"] = C(b_qkv[2176:2304].reshape(128, 1))
    sh["b_proj"] = C(b_proj.reshape(16, 128).T)
    sh["w_rt"] = C(np.concatenate([w_router, w_sg], axis=1).reshape(16, 128, 17).transpose(1, 0, 2).reshape(128, 16 * 17))
    sh["w_g"] = C(w_gate.reshape(E, 16, 128, 11, 128).transpose(0, 3, 2, 1, 4).reshape(E, 11, 128, 16 * 128))
    sh["w_u"] = C(w_up.reshape(E, 16, 128, 11, 128).transpose(0, 3, 2, 1, 4).reshape(E, 11, 128, 16 * 128))
    sh["w_d"] = C(w_down.reshape(E, 11, 128, 8, 256).transpose(0, 3, 2, 1, 4).reshape(E, 8, 128, 11 * 256))
    sh["ws_g"] = C(ws_gate.reshape(16, 128, 4, 11, 128).transpose(2, 3, 1, 0, 4).reshape(4, 11, 128, 16 * 128))
    sh["ws_u"] = C(ws_up.reshape(16, 128, 4, 11, 128).transpose(2, 3, 1, 0, 4).reshape(4, 11, 128, 16 * 128))
    sh["ws_d"] = C(ws_down.reshape(4, 11, 128, 8, 256).transpose(0, 3, 2, 1, 4).reshape(4, 8, 128, 11 * 256))
    ii = np.arange(CH)
    tri_t = np.where(ii[None, :] >= ii[:, None], 0.0, BIGNEG).astype(np.float32)
    sh["tri_t"] = C(tri_t.reshape(4, 128, CH).transpose(1, 0, 2).reshape(128, 4 * CH))

    in_maps = []
    for c in range(NCORE):
        b, r = divmod(c, 4)
        ch = chunks_of_core(r)
        xb = x[b]
        own = np.concatenate([xb[CH * cc:CH * (cc + 1)] for cc in ch], axis=0)
        x_ext = np.concatenate([xb, own], axis=0).T
        x_chunks = C(x_ext.reshape(16, 128, NTC, TC).transpose(2, 1, 0, 3).reshape(NTC, 128, 16 * TC))
        m = dict(sh)
        m["x_ext"] = x_chunks
        bn = np.zeros((128, NPLAIN + 1), np.float32)
        col = 0
        for s_i in range(4):
            for j in range(SLOT_PAD[s_i]):
                if j >= ch[s_i]:
                    bn[:, col] = BIGNEG
                col += 1
        m["bigneg"] = C(bn)
        in_maps.append(m)
    return in_maps



class Ctx:
    def __init__(self, nc):
        self.nc = nc
        self.sems = {}
        self.cnt = {}

    def sem(self, name):
        if name not in self.sems:
            self.sems[name] = self.nc.alloc_semaphore(name)
            self.cnt[name] = 0
        return self.sems[name]

    def inc(self, inst, name, n=1):
        inst.then_inc(self.sem(name), n)
        self.cnt[name] += n
        return self.cnt[name]

    def val(self, name):
        self.sem(name)
        return self.cnt[name]


def build_program():
    nc = bass.Bass("TRN2", target_bir_lowering=False)
    ctx = Ctx(nc)

    def din(name, shape, dtype=F32):
        return nc.dram_tensor(name, list(shape), dtype, kind="ExternalInput").ap()

    x_ext = din("x_ext", [NTC, 128, 16 * TC])
    w_kv_d = din("w_kv", [128, 16 * 256])
    w_q_d = din("w_q", [16, 128, 16 * 128])
    w_proj_d = din("w_proj", [16, 128, 16 * 128])
    b_q_d = din("b_q", [128, 16])
    b_k_d = din("b_k", [128, 1])
    b_v_d = din("b_v", [128, 1])
    b_proj_dd = din("b_proj", [128, 16])
    w_rt_d = din("w_rt", [128, 16 * 17])
    w_g_d = din("w_g", [E, 11, 128, 16 * 128])
    w_u_d = din("w_u", [E, 11, 128, 16 * 128])
    w_d_d = din("w_d", [E, 8, 128, 11 * 256])
    ws_g_d = din("ws_g", [4, 11, 128, 16 * 128])
    ws_u_d = din("ws_u", [4, 11, 128, 16 * 128])
    ws_d_d = din("ws_d", [4, 8, 128, 11 * 256])
    tri_t_d = din("tri_t", [128, 4 * CH])
    bigneg_d = din("bigneg", [128, NPLAIN + 1])

    out_d = nc.dram_tensor("out", [T_OWN, H], F32, kind="ExternalOutput").ap()
    dbg = {}
    if DEBUG:
        for name, shape, dt_ in [
            ("dbg_k", [128, T_EXT], BF16),
            ("dbg_v", [128, T_EXT], BF16),
            ("dbg_q", [128, NH * T_OWN], BF16),
            ("dbg_lnd", [128, NF * T_OWN], BF16),
            ("dbg_o", [128, NH * T_OWN], BF16),
            ("dbg_ln2", [128, NF * T_OWN], BF16),
            ("dbg_x2", [T_OWN, H], F32),
            ("dbg_cmb", [128, 16 * 17], F32),
            ("dbg_y", [128, NF * 1024], F32),
            ("dbg_a", [128, 11 * 512], BF16),
        ]:
            dbg[name] = nc.dram_tensor(name, shape, dt_, kind="ExternalOutput").ap()

    x2_dram = nc.dram_tensor("x2_scratch", [T_OWN, H], F32, kind="Internal").ap()

    it = nc.inline_tensor
    ident_bf_d = it(np.eye(128, dtype=ml_dtypes.bfloat16), name="ident_bf").ap()
    ident_f_d = it(np.eye(128, dtype=np.float32), name="ident_f").ap()
    ones_bf_d = it(np.ones((128, 1), dtype=ml_dtypes.bfloat16), name="ones_bf").ap()
    ones1f_d = it(np.ones((1, 128), dtype=np.float32), name="ones1f").ap()
    eps_d = it(np.full((128, 1), EPS, dtype=np.float32), name="epsc").ap()

    sb = nc.alloc_sbuf_tensor
    k_sb = sb("k_sb", [128, T_EXT], BF16)
    v_sb = sb("v_sb", [128, T_EXT], BF16)
    bufA = sb("bufA", [128, NF * T_OWN], BF16)
    bufB = sb("bufB", [128, NF * T_OWN], BF16)
    bufD = sb("bufD", [128, 8192], F32)
    c_ident_bf = sb("c_ident_bf", [128, 128], BF16)
    c_ident_f = sb("c_ident_f", [128, 128], F32)
    c_ones_bf = sb("c_ones_bf", [128, 1], BF16)
    c_ones1f = sb("c_ones1f", [1, 128], F32)
    c_eps = sb("c_eps", [128, 1], F32)
    c_bq = sb("c_bq", [128, 16], F32)
    c_bk = sb("c_bk", [128, 1], F32)
    c_bv = sb("c_bv", [128, 1], F32)
    c_bproj = sb("c_bproj", [128, 16], F32)
    c_bigneg = sb("c_bigneg", [128, NPLAIN + 1], F32)
    c_wrt = sb("c_wrt", [128, 16 * 17], BF16)
    rows = sb("rows", [1, 1024], F32)
    maxk = sb("maxk", [1, 1], F32)
    combine = sb("combine", [128, 16 * 17], F32)
    sig = sb("sig", [128, 8], F32)

    bufA_f = bufA[:].bitcast(F32)
    bufB_f = bufB[:].bitcast(F32)
    bufD_bf = bufD[:].bitcast(BF16)
    k_bfv = k_sb[:]
    v_bfv = v_sb[:]

    pb = [nc.alloc_psum_tensor(f"pb{i}", [128, 512], F32) for i in range(6)]
    pb_bf = nc.alloc_psum_tensor("pb_bf", [128, 1024], BF16)
    pb7 = nc.alloc_psum_tensor("pb7", [128, 512], F32)

    xbf_t = [bufB[:, 0:4096], bufB[:, 20480:24576]]
    sq_t = [bufB[:, 4096:8192], bufB[:, 24576:28672]]
    ln1c_t = [bufB[:, 8192:12288], bufB[:, 28672:32768]]
    vfm_t = [bufB[:, 12288:12544], bufB[:, 13824:14080]]
    wkv_sb = bufB[:, 16384:16384 + 16 * 256]
    tmp0 = bufB_f[:, 6400:6656]
    tmp1 = bufB_f[:, 6656:6912]
    rows2 = bufB_f[0:1, 7040:8064]

    def xstage(i):
        o = (i % 2) * (NF * TC)
        return bufD[:, o:o + NF * TC]

    att_stt = [bufD[:, 0:512], bufD[:, 512:1024]]
    att_bb = bufD[:, 1024:1536]
    att_rs = bufD[:, 1536:2048]
    att_pt = [bufD_bf[:, 4096:4608], bufD_bf[:, 4608:5120]]
    att_sq = bufD_bf[:, 10240:10240 + 2048]
    att_tri = bufD[:, 6144:8192]
    bnd_row = bufD[0:1, 2560:4608]

    pr_xr = bufD[:, 0:NF * PTC]
    pr_tmp = [bufD[:, NF * PTC:NF * PTC + 256], bufD[:, NF * PTC + 256:NF * PTC + 512]]
    pr_x2t = bufD[:, NF * PTC + 512:NF * PTC + 512 + 2048]
    v_f32 = v_sb[:].bitcast(F32)
    pr_a1 = v_f32[:, 0:NF * PTC]
    wp_buf = [k_bfv[:, 0:2048], k_bfv[:, 2048:4096]]

    rt_ex = bufD[:, 0:16]
    rt_pr = bufD[:, 16:32]
    rt_wk = bufD[:, 32:48]
    rt_mk = bufD[:, 48:64]

    ff_a = bufD_bf[:, 0:11 * 512]
    ff_sg = [bufD[:, 4096:4608], bufD[:, 4608:5120]]
    wg_buf = [k_bfv[:, 0:2048], k_bfv[:, 2048:4096]]
    wu_buf = [k_bfv[:, 4096:6144], k_bfv[:, 6144:8192]]
    wd_buf = [v_bfv[:, 0:2816], v_bfv[:, 2816:5632], v_bfv[:, 5632:8448]]
    ff_ysq = k_bfv[:, 8192:10240]
    out_x2r = bufD[:, 5632:5632 + 2048]
    out_x2r = bufD[:, 6144 - 2048 + 2048:8192]
    out_x2r = bufD[:, 6144:8192]
    out_t1 = bufD[:, 4096 - 2048 + 0:4096]
    out_t1 = bufD[:, 2048 + 1024:2048 + 1024 + 2048]
    out_ot = bufD[:, 0:2048]
    y_acc = bufA_f
    ln2 = bufB

    blk = nc.Block()
    blk.__enter__()

    V = ctx.val

    def ph_ld():
        def f_sync(sy):
            for dst, src in [
                (c_ident_bf[:], ident_bf_d), (c_ident_f[:], ident_f_d),
                (c_ones_bf[:], ones_bf_d), (c_ones1f[:], ones1f_d),
                (c_eps[:], eps_d), (c_bq[:], b_q_d), (c_bk[:], b_k_d),
                (c_bv[:], b_v_d), (c_bproj[:], b_proj_dd),
                (c_bigneg[:], bigneg_d),
            ]:
                ctx.inc(sy.dma_start(out=dst, in_=src), "ld", 16)
        blk.sync(f_sync)

        def f_gp(g):
            ctx.inc(g.dma_start(out=wkv_sb, in_=w_kv_d), "ld", 16)
            ctx.inc(g.dma_start(out=c_wrt[:], in_=w_rt_d), "ld", 16)
        blk.gpsimd(f_gp)

    def ph_p0a():
        ld_done = V("ld")

        def rset(i):
            return rows[0:1, :] if i % 2 == 0 else rows2

        def f_sync(sy):
            sy.wait_ge(ctx.sem("ld"), ld_done)
            for i in range(NTC):
                if i >= 2:
                    sy.wait_ge(ctx.sem("d_norm"), i - 1)
                ctx.inc(sy.dma_start(out=xstage(i), in_=x_ext[i]), "s_x", 16)
        blk.sync(f_sync)

        def f_sca(sc):
            sc.wait_ge(ctx.sem("ld"), ld_done)
            for i in range(NTC):
                p = i % 2
                r = rset(i)
                xs = xstage(i)
                sc.wait_ge(ctx.sem("s_x"), 16 * (i + 1))
                if i >= 2:
                    sc.wait_ge(ctx.sem("t_stats"), 2 * (i - 1))
                ctx.inc(sc.activation(xbf_t[p], xs, AF.Copy), "a_cast")
                ctx.inc(sc.activation(sq_t[p], xs, AF.Square), "a_sq")
                sc.wait_ge(ctx.sem("d_rows"), 2 * (i + 1))
                if i >= 2:
                    sc.wait_ge(ctx.sem("d_rr"), i - 1)
                ctx.inc(sc.activation(r[0:1, 512:768], r[0:1, 0:256],
                                      AF.Square), "a_mu2")
                sc.wait_ge(ctx.sem("d_var"), i + 1)
                ctx.inc(sc.activation(r[0:1, 512:768], r[0:1, 768:1024],
                                      AF.Sqrt, bias=c_eps[0:1, :]), "a_sqrt")
                sc.wait_ge(ctx.sem("t_kv"), 2 * (i + 1))
                if i >= 2:
                    sc.wait_ge(ctx.sem("t_vt"), 2 * (i - 1))
                ctx.inc(sc.activation(k_sb[:, TC * i:TC * (i + 1)], pb[0][:, 0:TC],
                                      AF.Identity, bias=c_bk[:]), "a_k")
                ctx.inc(sc.activation(vfm_t[p], pb[1][:, 0:TC],
                                      AF.Identity, bias=c_bv[:]), "a_vf")
        blk.scalar(f_sca)

        def f_ten(t):
            for i in range(NTC):
                p = i % 2
                r = rset(i)
                t.wait_ge(ctx.sem("a_cast"), i + 1)
                t.wait_ge(ctx.sem("a_sq"), i + 1)
                if i >= 1:
                    t.wait_ge(ctx.sem("d_rows"), 2 * i)
                mm = None
                for fc in range(16):
                    mm = t.matmul(pb[2][0:1, 0:TC], c_ones_bf[:],
                                  xbf_t[p][:, TC * fc:TC * (fc + 1)],
                                  start=(fc == 0), stop=(fc == 15))
                ctx.inc(mm, "t_stats")
                for fc in range(16):
                    mm = t.matmul(pb[3][0:1, 0:TC], c_ones_bf[:],
                                  sq_t[p][:, TC * fc:TC * (fc + 1)],
                                  start=(fc == 0), stop=(fc == 15))
                ctx.inc(mm, "t_stats")
                t.wait_ge(ctx.sem("d_rr"), i + 1)
                if i >= 1:
                    t.wait_ge(ctx.sem("d_norm"), i)
                ctx.inc(t.matmul(pb[4][:, 0:TC], c_ones1f[:], r[0:1, 0:TC],
                                 start=True, stop=True), "t_bc")
                ctx.inc(t.matmul(pb[5][:, 0:TC], c_ones1f[:], r[0:1, 768:768 + TC],
                                 start=True, stop=True), "t_bc")
                t.wait_ge(ctx.sem("d_norm"), i + 1)
                if i >= 1:
                    t.wait_ge(ctx.sem("a_k"), i)
                    t.wait_ge(ctx.sem("a_vf"), i)
                for which, pbi in ((0, pb[0]), (1, pb[1])):
                    mm = None
                    for fc in range(16):
                        if i < 32:
                            rhs = ln1c_t[p][:, TC * fc:TC * (fc + 1)]
                        else:
                            rhs = bufA[:, 2048 * fc + TC * (i - 32):2048 * fc + TC * (i - 32) + TC]
                        lhsT = wkv_sb[:, 256 * fc + 128 * which:256 * fc + 128 * which + 128]
                        mm = t.matmul(pbi[:, 0:TC], lhsT, rhs,
                                      start=(fc == 0), stop=(fc == 15))
                    ctx.inc(mm, "t_kv")
                t.wait_ge(ctx.sem("a_vf"), i + 1)
                if i >= 1:
                    t.wait_ge(ctx.sem("d_vt"), 2 * i)
                ctx.inc(t.transpose(pb_bf[:, 0:128], vfm_t[p][:, 0:128], c_ident_bf[:]), "t_vt")
                ctx.inc(t.transpose(pb_bf[:, 128:256], vfm_t[p][:, 128:256], c_ident_bf[:]), "t_vt")
        blk.tensor(f_ten)

        def f_vec(v):
            for i in range(NTC):
                p = i % 2
                r = rset(i)
                v.wait_ge(ctx.sem("t_stats"), 2 * (i + 1))
                if i >= 2:
                    v.wait_ge(ctx.sem("t_bc"), 2 * (i - 1))
                ctx.inc(v.tensor_scalar_mul(r[0:1, 0:TC], pb[2][0:1, 0:TC], RH), "d_rows")
                ctx.inc(v.tensor_scalar_mul(r[0:1, 256:256 + TC], pb[3][0:1, 0:TC], RH), "d_rows")
                v.wait_ge(ctx.sem("a_mu2"), i + 1)
                ctx.inc(v.tensor_sub(r[0:1, 768:768 + TC], r[0:1, 256:256 + TC],
                                     r[0:1, 512:512 + TC]), "d_var")
                v.wait_ge(ctx.sem("a_sqrt"), i + 1)
                ctx.inc(v.reciprocal(r[0:1, 768:768 + TC], r[0:1, 512:512 + TC]), "d_rr")
                v.wait_ge(ctx.sem("t_bc"), 2 * (i + 1))
                if i >= 2:
                    v.wait_ge(ctx.sem("t_kv"), 2 * (i - 1))
                xs = xstage(i)
                for fc in range(16):
                    tm = tmp0 if fc % 2 == 0 else tmp1
                    sub = v.tensor_sub(tm, xs[:, TC * fc:TC * (fc + 1)], pb[4][:, 0:TC])
                    se = ctx.inc(sub, "dse")
                    v.wait_ge(ctx.sem("dse"), se)
                    if i < 32:
                        dst = ln1c_t[p][:, TC * fc:TC * (fc + 1)]
                    else:
                        dst = bufA[:, 2048 * fc + TC * (i - 32):2048 * fc + TC * (i - 32) + TC]
                    mul = v.tensor_mul(dst, tm, pb[5][:, 0:TC])
                    if fc == 15:
                        ctx.inc(mul, "d_norm")
                v.wait_ge(ctx.sem("t_vt"), 2 * (i + 1))
                ctx.inc(v.tensor_copy(v_sb[:, TC * i:TC * (i + 1)], pb_bf[:, 0:256]),
                        "d_vt", 2)
        blk.vector(f_vec)

    def ph_barrier():
        tgt = ctx.val("bar") + 5
        for attr in (blk.sync, blk.gpsimd, blk.tensor, blk.vector, blk.scalar):
            def f(eng, tgt=tgt):
                ctx.inc(eng.nop(), "bar")
                eng.wait_ge(ctx.sem("bar"), tgt)
            attr(f)
        assert ctx.cnt["bar"] == tgt

    def ph_p0b():
        b_swq = V("s_wq")
        b_tq = V("t_q")
        b_aq = V("a_q")
        wqb = [bufD_bf[:, 0:2048], bufD_bf[:, 2048:4096]]

        def f_gp(g):
            for oc in range(16):
                if oc >= 2:
                    g.wait_ge(ctx.sem("t_q"), b_tq + 4 * (oc - 1))
                ctx.inc(g.dma_start(out=wqb[oc % 2], in_=w_q_d[oc]), "s_wq", 16)
        blk.gpsimd(f_gp)

        def f_ten(t):
            for oc in range(16):
                t.wait_ge(ctx.sem("s_wq"), b_swq + 16 * (oc + 1))
                for tg in range(4):
                    idx = 4 * oc + tg
                    if idx >= 2:
                        t.wait_ge(ctx.sem("a_q"), b_aq + idx - 1)
                    mm = None
                    for fc in range(16):
                        mm = t.matmul(pb[idx % 2][:, 0:512],
                                      wqb[oc % 2][:, 128 * fc:128 * (fc + 1)],
                                      bufA[:, 2048 * fc + 512 * tg:2048 * fc + 512 * (tg + 1)],
                                      start=(fc == 0), stop=(fc == 15))
                    ctx.inc(mm, "t_q")
        blk.tensor(f_ten)

        def f_sca(sc):
            for oc in range(16):
                for tg in range(4):
                    idx = 4 * oc + tg
                    sc.wait_ge(ctx.sem("t_q"), b_tq + idx + 1)
                    ctx.inc(sc.activation(
                        bufB[:, 2048 * oc + 512 * tg:2048 * oc + 512 * (tg + 1)],
                        pb[idx % 2][:, 0:512], AF.Identity,
                        bias=c_bq[:, oc:oc + 1]), "a_q")
        blk.scalar(f_sca)

    def ph_bounds():
        b_aks = V("a_ks")
        b_tkn = V("t_kn")
        b_dkm = V("d_km")
        b_aqs = V("a_qs")
        b_tqn = V("t_qn")
        b_aqb = V("a_qb")
        b_dqb = V("d_qb")

        b_ld = V("ld")

        def f_sync(sy):
            ctx.inc(sy.dma_start(out=att_tri, in_=tri_t_d), "ld", 16)
        blk.sync(f_sync)

        def f_vec0(v):
            ctx.inc(v.memset(maxk[:], 0.0), "d_km")
        blk.vector(f_vec0)

        def f_sca(sc):
            sc.wait_ge(ctx.sem("ld"), b_ld + 16)
            for cc in range(20):
                if cc >= 1:
                    sc.wait_ge(ctx.sem("t_kn"), b_tkn + cc)
                ctx.inc(sc.activation(att_sq[:, 0:512], k_sb[:, 512 * cc:512 * (cc + 1)],
                                      AF.Square), "a_ks")
        blk.scalar(f_sca)

        def f_ten(t):
            for cc in range(20):
                t.wait_ge(ctx.sem("a_ks"), b_aks + cc + 1)
                if cc >= 1:
                    t.wait_ge(ctx.sem("d_km"), b_dkm + 2 * cc)
                ctx.inc(t.matmul(pb[2][0:1, 0:512], c_ones_bf[:], att_sq[:, 0:512],
                                 start=True, stop=True), "t_kn")
        blk.tensor(f_ten)

        def f_vec(v):
            v.wait_ge(ctx.sem("d_km"), 1)
            for cc in range(20):
                v.wait_ge(ctx.sem("t_kn"), b_tkn + cc + 1)
                e1 = ctx.inc(v.reduce_max(rows[0:1, 4:5], pb[2][0:1, 0:512], axis=AX.X),
                             "d_km")
                v.wait_ge(ctx.sem("d_km"), e1)
                ctx.inc(v.tensor_max(maxk[:], maxk[:], rows[0:1, 4:5]), "d_km")
            e1 = ctx.inc(v.tensor_scalar_mul(maxk[:], maxk[:], 1.0404), "d_km")
        blk.vector(f_vec)

        def f_sca2(sc):
            sc.wait_ge(ctx.sem("d_km"), b_dkm + 42)
            ctx.inc(sc.activation(maxk[:], maxk[:], AF.Sqrt), "a_mk")
        blk.scalar(f_sca2)

    def ph_attn():
        b_tbb = V("t_bb")
        b_abb = V("a_bb")
        b_tst = V("t_st")
        b_dstt = V("d_stt")
        b_aexp = V("a_exp")
        b_tpv = V("t_pv")
        b_tsg = V("t_sg")
        b_drsg = V("d_rsg")
        b_trs = V("t_rs")
        b_ars = V("a_rs")
        b_dof = V("d_of")
        b_dtri = V("d_tri")

        b_tbn = V("t_bn")
        b_aqs = V("a_qs")
        b_abq = V("a_bq")
        b_dbn = V("d_bn")

        def iters():
            for h_ in range(16):
                for s_ in range(4):
                    yield h_, s_

        def kbs(s_):
            off = sum(SLOT_PAD[:s_])
            out = [(512 * j, off + j, False) for j in range(SLOT_PAD[s_])]
            out.append((S + 512 * s_, NPLAIN, True))
            return out

        tiles_before = {}
        tb_cum = 0
        for h_, s_ in iters():
            tiles_before[(h_, s_)] = tb_cum
            tb_cum += 4 * (SLOT_PAD[s_] + 1)
        total_tiles = tb_cum

        def f_ten_real(t):
            t.wait_ge(ctx.sem("a_mk"), 1)
            for itn, (h_, s_) in enumerate(iters()):
                kl = kbs(s_)
                t0 = tiles_before[(h_, s_)]
                ntile = 4 * len(kl)
                p2 = itn % 2
                if s_ == 0:
                    t.wait_ge(ctx.sem("a_qs"), b_aqs + h_ + 1)
                    for tb in range(4):
                        if itn >= 1 or tb >= 1:
                            t.wait_ge(ctx.sem("a_bq"), b_abq + 4 * h_ + tb)
                        if tb == 0 and itn >= 1:
                            t.wait_ge(ctx.sem("a_rs"), b_ars + itn)
                        ctx.inc(t.matmul(pb7[0:1, 0:512], c_ones_bf[:],
                                         att_sq[:, 512 * tb:512 * (tb + 1)],
                                         start=True, stop=True), "t_bn")
                t.wait_ge(ctx.sem("d_bn"), b_dbn + 4 * h_ + s_ + 1)
                if s_ >= 1:
                    t.wait_ge(ctx.sem("a_rs"), b_ars + itn)
                else:
                    t.wait_ge(ctx.sem("a_bq"), b_abq + 4 * (h_ + 1))
                ctx.inc(t.matmul(pb7[:, 0:512], c_ones1f[:],
                                 bnd_row[0:1, 512 * s_:512 * (s_ + 1)],
                                 start=True, stop=True), "t_bb")
                qrhs = bufB[:, 2048 * h_ + 512 * s_:2048 * h_ + 512 * s_ + 512]

                def pv_sg(ti):
                    gt = t0 + ti
                    kb, ks = divmod(ti, 4)
                    kcol, _, _ = kl[kb]
                    t.wait_ge(ctx.sem("a_exp"), b_aexp + gt + 1)
                    if ti == 0 and itn >= 2:
                        t.wait_ge(ctx.sem("d_of"), b_dof + itn - 1)
                        t.wait_ge(ctx.sem("d_rsg"), b_drsg + itn - 1)
                    ctx.inc(t.matmul(pb[2 + p2][:, 0:512],
                                     v_sb[:, kcol + 128 * ks:kcol + 128 * (ks + 1)],
                                     att_pt[gt % 2],
                                     start=(ti == 0), stop=(ti == ntile - 1)), "t_pv")
                    ctx.inc(t.matmul(pb[4 + p2][0:1, 0:512], c_ones_bf[:],
                                     att_pt[gt % 2],
                                     start=(ti == 0), stop=(ti == ntile - 1)), "t_sg")

                for ti in range(ntile):
                    kb, ks = divmod(ti, 4)
                    kcol, bcol, isdiag = kl[kb]
                    gt = t0 + ti
                    if gt >= 2:
                        t.wait_ge(ctx.sem("d_stt"), b_dstt + gt - 1)
                    ctx.inc(t.matmul(pb[gt % 2][:, 0:512],
                                     k_sb[:, kcol + 128 * ks:kcol + 128 * (ks + 1)],
                                     qrhs, start=True, stop=True), "t_st")
                    if ti >= 1:
                        pv_sg(ti - 1)
                pv_sg(ntile - 1)
                t.wait_ge(ctx.sem("d_rsg"), b_drsg + itn + 1)
                t.wait_ge(ctx.sem("a_bb"), b_abb + itn + 1)
                ctx.inc(t.matmul(pb7[:, 0:512], c_ones1f[:], rows[0:1, 0:512],
                                 start=True, stop=True), "t_rs")
        blk.tensor(f_ten_real)

        def f_vec(v):
            for itn, (h_, s_) in enumerate(iters()):
                kl = kbs(s_)
                t0 = tiles_before[(h_, s_)]
                ntile = 4 * len(kl)
                p2 = itn % 2
                if s_ == 0:
                    for tb in range(4):
                        v.wait_ge(ctx.sem("a_bq"), b_abq + 4 * h_ + tb + 1)
                        ctx.inc(v.tensor_scalar_mul(
                            bnd_row[0:1, 512 * tb:512 * (tb + 1)],
                            bnd_row[0:1, 512 * tb:512 * (tb + 1)], maxk[0:1, :]),
                            "d_bn")
                v.wait_ge(ctx.sem("a_bb"), b_abb + itn + 1)
                for ti in range(ntile):
                    kb, ks = divmod(ti, 4)
                    kcol, bcol, isdiag = kl[kb]
                    gt = t0 + ti
                    v.wait_ge(ctx.sem("t_st"), b_tst + gt + 1)
                    if gt >= 2:
                        v.wait_ge(ctx.sem("a_exp"), b_aexp + gt - 1)
                    if isdiag:
                        e1 = ctx.inc(v.tensor_add(pb[gt % 2][:, 0:512], pb[gt % 2][:, 0:512],
                                                  att_tri[:, 512 * ks:512 * (ks + 1)]),
                                     "d_tri")
                        v.wait_ge(ctx.sem("d_tri"), e1)
                    ctx.inc(v.scalar_tensor_tensor(
                        out=att_stt[gt % 2], in0=pb[gt % 2][:, 0:512],
                        scalar=c_bigneg[:, bcol:bcol + 1], in1=att_bb,
                        op0=ALU.add, op1=ALU.subtract), "d_stt")
                v.wait_ge(ctx.sem("t_sg"), b_tsg + t0 + ntile)
                if itn >= 1:
                    v.wait_ge(ctx.sem("t_rs"), b_trs + itn)
                ctx.inc(v.reciprocal(rows[0:1, 0:512], pb[4 + p2][0:1, 0:512]), "d_rsg")
                v.wait_ge(ctx.sem("a_rs"), b_ars + itn + 1)
                v.wait_ge(ctx.sem("t_pv"), b_tpv + t0 + ntile)
                ctx.inc(v.tensor_mul(
                    bufA[:, 2048 * h_ + 512 * s_:2048 * h_ + 512 * s_ + 512],
                    pb[2 + p2][:, 0:512], att_rs), "d_of")
        blk.vector(f_vec)

        def f_sca(sc):
            sc.wait_ge(ctx.sem("a_mk"), 1)
            for itn, (h_, s_) in enumerate(iters()):
                kl = kbs(s_)
                t0 = tiles_before[(h_, s_)]
                ntile = 4 * len(kl)
                if s_ == 0:
                    if itn >= 1:
                        sc.wait_ge(ctx.sem("t_bn"), b_tbn + 4 * h_)
                    ctx.inc(sc.activation(att_sq[:, 0:2048],
                                          bufB[:, 2048 * h_:2048 * (h_ + 1)],
                                          AF.Square), "a_qs")
                    for tb in range(4):
                        sc.wait_ge(ctx.sem("t_bn"), b_tbn + 4 * h_ + tb + 1)
                        if itn >= 1 and tb == 0:
                            sc.wait_ge(ctx.sem("t_bb"), b_tbb + itn)
                        ctx.inc(sc.activation(bnd_row[0:1, 512 * tb:512 * (tb + 1)],
                                              pb7[0:1, 0:512], AF.Sqrt), "a_bq")
                sc.wait_ge(ctx.sem("t_bb"), b_tbb + itn + 1)
                if itn >= 1:
                    sc.wait_ge(ctx.sem("d_stt"), b_dstt + t0)
                ctx.inc(sc.activation(att_bb, pb7[:, 0:512], AF.Copy), "a_bb")
                for ti in range(ntile):
                    gt = t0 + ti
                    sc.wait_ge(ctx.sem("d_stt"), b_dstt + gt + 1)
                    if gt >= 2:
                        sc.wait_ge(ctx.sem("t_pv"), b_tpv + gt - 1)
                    ctx.inc(sc.activation(att_pt[gt % 2], att_stt[gt % 2], AF.Exp,
                                          scale=SCALE), "a_exp")
                sc.wait_ge(ctx.sem("t_rs"), b_trs + itn + 1)
                ctx.inc(sc.activation(att_rs, pb7[:, 0:512], AF.Copy), "a_rs")
        blk.scalar(f_sca)

    def ph_proj():
        b_sxr = V("s_xr")
        b_swp = V("s_wp")
        b_tpj = V("t_pj")
        b_apj = V("a_pj")
        b_tps = V("t_ps")
        b_dprw = V("d_prw")
        b_apm = V("a_pm")
        b_dpv = V("d_pv")
        b_apsq = V("a_psq")
        b_dprr = V("d_prr")
        b_tpbc = V("t_pbc")
        b_dx2 = V("d_x2")
        b_al2 = V("a_l2")
        b_tl2s = V("t_l2s")
        b_dl2r = V("d_l2r")
        b_al2m = V("a_l2m")
        b_dl2v = V("d_l2v")
        b_al2q = V("a_l2q")
        b_dl2r2 = V("d_l2r2")
        b_tl2bc = V("t_l2bc")
        b_dln2 = V("d_ln2")
        b_txt = V("t_xt")
        b_axt = V("a_xt")
        b_sx2w = V("s_x2w")

        pr_bt = [k_bfv[:, 4096:4352], k_bfv[:, 4352:4608]]
        pr_st = [k_bfv[:, 4608:4864], k_bfv[:, 4864:5120]]

        def f_sync(sy):
            for pt in range(NPT):
                if pt >= 1:
                    sy.wait_ge(ctx.sem("d_x2"), b_dx2 + 48 * pt)
                ctx.inc(sy.dma_start(out=pr_xr, in_=x_ext[32 + pt]), "s_xr", 16)
                for tsb in range(2):
                    sy.wait_ge(ctx.sem("a_xt"), b_axt + 16 * (2 * pt + tsb + 1))
                    ctx.inc(sy.dma_start(
                        out=x2_dram[PTC * pt + 128 * tsb:PTC * pt + 128 * (tsb + 1), :],
                        in_=pr_x2t), "s_x2w", 16)
        blk.sync(f_sync)

        def f_gp(g):
            for pt in range(NPT):
                for oc in range(16):
                    idx = 16 * pt + oc
                    if idx >= 2:
                        g.wait_ge(ctx.sem("t_pj"), b_tpj + idx - 1)
                    ctx.inc(g.dma_start(out=wp_buf[oc % 2], in_=w_proj_d[oc]), "s_wp", 16)
        blk.gpsimd(f_gp)

        def f_ten(t):
            for pt in range(NPT):
                for oc in range(16):
                    idx = 16 * pt + oc
                    t.wait_ge(ctx.sem("s_wp"), b_swp + 16 * (idx + 1))
                    if idx >= 2:
                        t.wait_ge(ctx.sem("a_pj"), b_apj + 3 * (idx - 1))
                    mm = None
                    for fc in range(16):
                        mm = t.matmul(pb[oc % 2][:, 0:PTC],
                                      wp_buf[oc % 2][:, 128 * fc:128 * (fc + 1)],
                                      bufA[:, 2048 * fc + PTC * pt:2048 * fc + PTC * (pt + 1)],
                                      start=(fc == 0), stop=(fc == 15))
                    ctx.inc(mm, "t_pj")
                    t.wait_ge(ctx.sem("a_pj"), b_apj + 3 * (idx + 1))
                    if pt >= 1 and oc == 0:
                        t.wait_ge(ctx.sem("d_prw"), b_dprw + 2 * pt)
                    ctx.inc(t.matmul(pb[2][0:1, 0:PTC], c_ones_bf[:], pr_bt[oc % 2],
                                     start=(oc == 0), stop=(oc == 15)), "t_ps")
                    ctx.inc(t.matmul(pb[3][0:1, 0:PTC], c_ones_bf[:], pr_st[oc % 2],
                                     start=(oc == 0), stop=(oc == 15)), "t_ps")
                t.wait_ge(ctx.sem("d_prr"), b_dprr + pt + 1)
                if pt >= 1:
                    t.wait_ge(ctx.sem("d_x2"), b_dx2 + 48 * pt)
                ctx.inc(t.matmul(pb[4][:, 0:PTC], c_ones1f[:], rows[0:1, 0:PTC],
                                 start=True, stop=True), "t_pbc")
                ctx.inc(t.matmul(pb[5][:, 0:PTC], c_ones1f[:], rows[0:1, 768:768 + PTC],
                                 start=True, stop=True), "t_pbc")
                for fc in range(16):
                    lidx = 16 * pt + fc
                    t.wait_ge(ctx.sem("a_l2"), b_al2 + 2 * (lidx + 1))
                    if pt >= 1 and fc == 0:
                        t.wait_ge(ctx.sem("d_l2r"), b_dl2r + 2 * pt)
                    ctx.inc(t.matmul(pb[2][0:1, 0:PTC], c_ones_bf[:], pr_bt[fc % 2],
                                     start=(fc == 0), stop=(fc == 15)), "t_l2s")
                    ctx.inc(t.matmul(pb[3][0:1, 0:PTC], c_ones_bf[:], pr_st[fc % 2],
                                     start=(fc == 0), stop=(fc == 15)), "t_l2s")
                t.wait_ge(ctx.sem("d_l2r2"), b_dl2r2 + pt + 1)
                t.wait_ge(ctx.sem("d_x2"), b_dx2 + 48 * (pt + 1))
                ctx.inc(t.matmul(pb[4][:, 0:PTC], c_ones1f[:], rows[0:1, 0:PTC],
                                 start=True, stop=True), "t_l2bc")
                ctx.inc(t.matmul(pb[5][:, 0:PTC], c_ones1f[:], rows[0:1, 768:768 + PTC],
                                 start=True, stop=True), "t_l2bc")
                t.wait_ge(ctx.sem("d_ln2"), b_dln2 + 16 * (pt + 1))
                for tsb in range(2):
                    for fc in range(16):
                        ti = 32 * pt + 16 * tsb + fc
                        if ti >= 2:
                            t.wait_ge(ctx.sem("a_xt"), b_axt + ti - 1)
                        ctx.inc(t.transpose(
                            pb[4 + fc % 2][:, 0:128],
                            pr_a1[:, PTC * fc + 128 * tsb:PTC * fc + 128 * (tsb + 1)],
                            c_ident_f[:]), "t_xt")
        blk.tensor(f_ten)

        def f_sca(sc):
            for pt in range(NPT):
                for oc in range(16):
                    idx = 16 * pt + oc
                    sc.wait_ge(ctx.sem("t_pj"), b_tpj + idx + 1)
                    if pt >= 1 and oc == 0:
                        sc.wait_ge(ctx.sem("t_xt"), b_txt + 32 * pt)
                    if idx >= 2:
                        sc.wait_ge(ctx.sem("t_ps"), b_tps + 2 * (idx - 1))
                    ctx.inc(sc.activation(pr_a1[:, PTC * oc:PTC * (oc + 1)],
                                          pb[oc % 2][:, 0:PTC], AF.Identity,
                                          bias=c_bproj[:, oc:oc + 1]), "a_pj")
                    e = ctx.inc(sc.activation(pr_bt[oc % 2],
                                              pb[oc % 2][:, 0:PTC], AF.Identity,
                                              bias=c_bproj[:, oc:oc + 1]), "a_pj")
                    sc.wait_ge(ctx.sem("a_pj"), e)
                    ctx.inc(sc.activation(pr_st[oc % 2], pr_bt[oc % 2],
                                          AF.Square), "a_pj")
                sc.wait_ge(ctx.sem("d_prw"), b_dprw + 2 * (pt + 1))
                if pt >= 1:
                    sc.wait_ge(ctx.sem("d_l2r2"), b_dl2r2 + pt)
                ctx.inc(sc.activation(rows[0:1, 512:768], rows[0:1, 0:256], AF.Square),
                        "a_pm")
                sc.wait_ge(ctx.sem("d_pv"), b_dpv + pt + 1)
                ctx.inc(sc.activation(rows[0:1, 512:768], rows[0:1, 768:1024], AF.Sqrt,
                                      bias=c_eps[0:1, :]), "a_psq")
                sc.wait_ge(ctx.sem("d_x2"), b_dx2 + 48 * (pt + 1))
                for fc in range(16):
                    lidx = 16 * pt + fc
                    if lidx >= 2:
                        sc.wait_ge(ctx.sem("t_l2s"), b_tl2s + 2 * (lidx - 1))
                    e = ctx.inc(sc.activation(pr_bt[fc % 2],
                                              pr_a1[:, PTC * fc:PTC * (fc + 1)],
                                              AF.Copy), "a_l2")
                    sc.wait_ge(ctx.sem("a_l2"), e)
                    ctx.inc(sc.activation(pr_st[fc % 2], pr_bt[fc % 2], AF.Square),
                            "a_l2")
                sc.wait_ge(ctx.sem("d_l2r"), b_dl2r + 2 * (pt + 1))
                sc.wait_ge(ctx.sem("d_prr"), b_dprr + pt + 1)
                ctx.inc(sc.activation(rows[0:1, 512:768], rows[0:1, 0:256], AF.Square),
                        "a_l2m")
                sc.wait_ge(ctx.sem("d_l2v"), b_dl2v + pt + 1)
                ctx.inc(sc.activation(rows[0:1, 512:768], rows[0:1, 768:1024], AF.Sqrt,
                                      bias=c_eps[0:1, :]), "a_l2q")
                for tsb in range(2):
                    for fc in range(16):
                        ti = 32 * pt + 16 * tsb + fc
                        sc.wait_ge(ctx.sem("t_xt"), b_txt + ti + 1)
                        ctx.inc(sc.activation(pr_x2t[:, 128 * fc:128 * (fc + 1)],
                                              pb[4 + fc % 2][:, 0:128],
                                              AF.Copy), "a_xt")
                    if not (pt == NPT - 1 and tsb == 1):
                        sc.wait_ge(ctx.sem("s_x2w"), b_sx2w + 16 * (2 * pt + tsb + 1))
        blk.scalar(f_sca)

        def f_vec(v):
            for pt in range(NPT):
                v.wait_ge(ctx.sem("t_ps"), b_tps + 2 * 16 * (pt + 1))
                if pt >= 1:
                    v.wait_ge(ctx.sem("t_l2bc"), b_tl2bc + 2 * pt)
                ctx.inc(v.tensor_scalar_mul(rows[0:1, 0:PTC], pb[2][0:1, 0:PTC], RH), "d_prw")
                ctx.inc(v.tensor_scalar_mul(rows[0:1, 256:256 + PTC], pb[3][0:1, 0:PTC], RH), "d_prw")
                v.wait_ge(ctx.sem("a_pm"), b_apm + pt + 1)
                ctx.inc(v.tensor_sub(rows[0:1, 768:768 + PTC], rows[0:1, 256:256 + PTC],
                                     rows[0:1, 512:512 + PTC]), "d_pv")
                v.wait_ge(ctx.sem("a_psq"), b_apsq + pt + 1)
                ctx.inc(v.reciprocal(rows[0:1, 768:768 + PTC], rows[0:1, 512:512 + PTC]), "d_prr")
                v.wait_ge(ctx.sem("t_pbc"), b_tpbc + 2 * (pt + 1))
                v.wait_ge(ctx.sem("s_xr"), b_sxr + 16 * (pt + 1))
                for fc in range(16):
                    tm = pr_tmp[fc % 2]
                    a1 = pr_a1[:, PTC * fc:PTC * (fc + 1)]
                    e1 = ctx.inc(v.tensor_sub(tm, a1, pb[4][:, 0:PTC]), "dse")
                    v.wait_ge(ctx.sem("dse"), e1)
                    e2 = ctx.inc(v.tensor_mul(tm, tm, pb[5][:, 0:PTC]), "dse")
                    v.wait_ge(ctx.sem("dse"), e2)
                    ctx.inc(v.tensor_add(a1, tm, pr_xr[:, PTC * fc:PTC * (fc + 1)]),
                            "d_x2", 3)
                v.wait_ge(ctx.sem("t_l2s"), b_tl2s + 2 * 16 * (pt + 1))
                ctx.inc(v.tensor_scalar_mul(rows[0:1, 0:PTC], pb[2][0:1, 0:PTC], RH), "d_l2r")
                ctx.inc(v.tensor_scalar_mul(rows[0:1, 256:256 + PTC], pb[3][0:1, 0:PTC], RH), "d_l2r")
                v.wait_ge(ctx.sem("a_l2m"), b_al2m + pt + 1)
                ctx.inc(v.tensor_sub(rows[0:1, 768:768 + PTC], rows[0:1, 256:256 + PTC],
                                     rows[0:1, 512:512 + PTC]), "d_l2v")
                v.wait_ge(ctx.sem("a_l2q"), b_al2q + pt + 1)
                ctx.inc(v.reciprocal(rows[0:1, 768:768 + PTC], rows[0:1, 512:512 + PTC]), "d_l2r2")
                v.wait_ge(ctx.sem("t_l2bc"), b_tl2bc + 2 * (pt + 1))
                for fc in range(16):
                    tm = pr_tmp[fc % 2]
                    a1 = pr_a1[:, PTC * fc:PTC * (fc + 1)]
                    e1 = ctx.inc(v.tensor_sub(tm, a1, pb[4][:, 0:PTC]), "dse")
                    v.wait_ge(ctx.sem("dse"), e1)
                    ctx.inc(v.tensor_mul(
                        bufB[:, 2048 * fc + PTC * pt:2048 * fc + PTC * (pt + 1)],
                        tm, pb[5][:, 0:PTC]), "d_ln2")
        blk.vector(f_vec)

    def ph_router():
        b_trt = V("t_rt")
        b_drt = V("d_rt")
        b_art = V("a_rt")

        def f_ten(t):
            for ts in range(16):
                if ts >= 1:
                    t.wait_ge(ctx.sem("a_rt"), b_art + 2 * ts)
                mm = None
                for fc in range(16):
                    mm = t.matmul(pb7[:, 0:17],
                                  bufB[:, 2048 * fc + 128 * ts:2048 * fc + 128 * (ts + 1)],
                                  c_wrt[:, 17 * fc:17 * (fc + 1)],
                                  start=(fc == 0), stop=(fc == 15))
                ctx.inc(mm, "t_rt")
        blk.tensor(f_ten)

        def f_vec(v):
            for ts in range(16):
                v.wait_ge(ctx.sem("t_rt"), b_trt + ts + 1)
                e = ctx.inc(v.reduce_max(sig[:, 0:1], pb7[:, 0:16], axis=AX.X), "dse")
                v.wait_ge(ctx.sem("dse"), e)
                ctx.inc(v.tensor_scalar_mul(sig[:, 1:2], sig[:, 0:1], -1.0), "d_rt1")
                v.wait_ge(ctx.sem("a_rt1"), ts + 1)
                e = ctx.inc(v.reciprocal(sig[:, 3:4], sig[:, 2:3]), "dse")
                v.wait_ge(ctx.sem("dse"), e)
                e = ctx.inc(v.tensor_scalar_mul(rt_pr, rt_ex, sig[:, 3:4]), "dse")
                v.wait_ge(ctx.sem("dse"), e)
                e = ctx.inc(v.tensor_copy(rt_wk, rt_pr), "dse")
                v.wait_ge(ctx.sem("dse"), e)
                for k in range(3):
                    e = ctx.inc(v.reduce_max(sig[:, 4:5], rt_wk, axis=AX.X), "dse")
                    v.wait_ge(ctx.sem("dse"), e)
                    e = ctx.inc(v.tensor_scalar(out=rt_mk, in0=rt_wk,
                                                scalar1=sig[:, 4:5], scalar2=BIGNEG,
                                                op0=ALU.is_ge, op1=ALU.mult), "dse")
                    v.wait_ge(ctx.sem("dse"), e)
                    e = ctx.inc(v.tensor_add(rt_wk, rt_wk, rt_mk), "dse")
                    v.wait_ge(ctx.sem("dse"), e)
                e = ctx.inc(v.reduce_max(sig[:, 4:5], rt_wk, axis=AX.X), "dse")
                v.wait_ge(ctx.sem("dse"), e)
                e = ctx.inc(v.tensor_scalar(out=rt_mk, in0=rt_pr,
                                            scalar1=sig[:, 4:5], scalar2=None,
                                            op0=ALU.is_ge), "dse")
                v.wait_ge(ctx.sem("dse"), e)
                ctx.inc(v.tensor_mul(combine[:, 17 * ts:17 * ts + 16], rt_pr, rt_mk),
                        "d_rt")
        blk.vector(f_vec)

        def f_sca(sc):
            for ts in range(16):
                sc.wait_ge(ctx.sem("d_rt1"), ts + 1)
                ctx.inc(sc.activation(rt_ex, pb7[:, 0:16], AF.Exp,
                                      bias=sig[:, 1:2], accum_out=sig[:, 2:3]), "a_rt1")
                ctx.inc(sc.activation(combine[:, 17 * ts + 16:17 * ts + 17],
                                      pb7[:, 16:17], AF.Sigmoid), "a_rt", 2)
        blk.scalar(f_sca)

    def ph_ffn(hf):
        sfx = ""
        b_sfw = V("s_fw" + sfx)
        b_sfd = V("s_fd" + sfx)
        b_tgu = V("t_gu" + sfx)
        b_asi = V("a_si" + sfx)
        b_da = V("d_a" + sfx)
        b_ty = V("t_y" + sfx)
        b_dy = V("d_y" + sfx)
        b_dms = V("d_ms" + sfx)

        NES = len(FFN_ES)

        def srcs(pe):
            wi, ce, fam = FFN_ES[pe]
            if fam == "moe":
                return w_g_d[wi], w_u_d[wi], w_d_d[wi], ce
            return ws_g_d[wi], ws_u_d[wi], ws_d_d[wi], ce

        b_ysq0 = V("a_ysq")
        b_do0 = V("d_o")
        b_so0 = V("s_out")

        def f_vec0(v):
            v.wait_ge(ctx.sem("a_ysq"), b_ysq0)
            if FFN_MEMSET:
                ctx.inc(v.memset(y_acc[:, 0:8192], 0.0), "d_ms" + sfx)
                ctx.inc(v.memset(y_acc[:, 8192:16384], 0.0), "d_ms" + sfx)
            else:
                ctx.inc(v.engine_nop(), "d_ms" + sfx)
                ctx.inc(v.engine_nop(), "d_ms" + sfx)
        blk.vector(f_vec0)

        def f_gp(g):
            g.wait_ge(ctx.sem("d_ms" + sfx), b_dms + 2)
            for pe in range(NES):
                gsrc, usrc, dsrc, ce = srcs(pe)
                for gg in range(2):
                    for ic in range(11):
                        icg = (pe * 2 + gg) * 11 + ic
                        if icg >= 2:
                            g.wait_ge(ctx.sem("d_a" + sfx), b_da + icg - 1)
                        elif hf == 1:
                            g.wait_ge(ctx.sem("d_a" + sfx), b_da)
                        ctx.inc(g.dma_start(out=wg_buf[icg % 2], in_=gsrc[ic]),
                                "s_fw" + sfx, 16)
                        ctx.inc(g.dma_start(out=wu_buf[icg % 2], in_=usrc[ic]),
                                "s_fw" + sfx, 16)
                    for hgc in range(8):
                        hgg = (pe * 2 + gg) * 8 + hgc
                        if hgg >= 3:
                            g.wait_ge(ctx.sem("t_y" + sfx), b_ty + 4 * (hgg - 2))
                        elif hf == 1 and hgg == 0:
                            g.wait_ge(ctx.sem("t_y" + sfx), b_ty)
                        ctx.inc(g.dma_start(out=wd_buf[hgc % 3], in_=dsrc[hgc]),
                                "s_fd" + sfx, 16)
        blk.gpsimd(f_gp)

        def f_ten(t):
            for pe in range(NES):
                for gg in range(2):
                    rhs0 = 1024 * hf + 512 * gg
                    for ic in range(11):
                        icg = (pe * 2 + gg) * 11 + ic
                        t.wait_ge(ctx.sem("s_fw" + sfx), b_sfw + 32 * (icg + 1))
                        if SERIAL_FFN and icg >= 1:
                            t.wait_ge(ctx.sem("a_si" + sfx), b_asi + icg)
                            t.wait_ge(ctx.sem("d_a" + sfx), b_da + icg)
                        elif icg >= 2:
                            t.wait_ge(ctx.sem("a_si" + sfx), b_asi + icg - 1)
                            t.wait_ge(ctx.sem("d_a" + sfx), b_da + icg - 1)
                        elif hf == 1:
                            t.wait_ge(ctx.sem("a_si" + sfx), b_asi)
                            t.wait_ge(ctx.sem("d_a" + sfx), b_da)
                        for which, buf in ((0, wg_buf), (1, wu_buf)):
                            mm = None
                            for fc in range(16):
                                mm = t.matmul(
                                    pb[which * 2 + icg % 2][:, 0:512],
                                    buf[icg % 2][:, 128 * fc:128 * (fc + 1)],
                                    bufB[:, 2048 * fc + rhs0:2048 * fc + rhs0 + 512],
                                    start=(fc == 0), stop=(fc == 15))
                            ctx.inc(mm, "t_gu" + sfx)
                    for hgc in range(8):
                        hgg = (pe * 2 + gg) * 8 + hgc
                        t.wait_ge(ctx.sem("s_fd" + sfx), b_sfd + 16 * (hgg + 1))
                        t.wait_ge(ctx.sem("d_a" + sfx), b_da + (pe * 2 + gg) * 11 + 11)
                        for tsub in range(4):
                            yt = 4 * hgg + tsub
                            if SERIAL_FFN and yt >= 1:
                                t.wait_ge(ctx.sem("d_y" + sfx), b_dy + yt)
                            elif yt >= 2:
                                t.wait_ge(ctx.sem("d_y" + sfx), b_dy + yt - 1)
                            elif hf == 1:
                                t.wait_ge(ctx.sem("d_y" + sfx), b_dy)
                            mm = None
                            for ic in range(11):
                                mm = t.matmul(
                                    pb[4 + yt % 2][:, 0:256],
                                    ff_a[:, 512 * ic + 128 * tsub:512 * ic + 128 * (tsub + 1)],
                                    wd_buf[hgc % 3][:, 256 * ic:256 * (ic + 1)],
                                    start=(ic == 0), stop=(ic == 10))
                            ctx.inc(mm, "t_y" + sfx)
        blk.tensor(f_ten)

        def f_sca(sc):
            for pe in range(NES):
                for gg in range(2):
                    for ic in range(11):
                        icg = (pe * 2 + gg) * 11 + ic
                        sc.wait_ge(ctx.sem("t_gu" + sfx), b_tgu + 2 * icg + 1)
                        if icg >= 2:
                            sc.wait_ge(ctx.sem("d_a" + sfx), b_da + icg - 1)
                        elif hf == 1:
                            sc.wait_ge(ctx.sem("d_o"), b_do0)
                        ctx.inc(sc.activation(ff_sg[icg % 2], pb[icg % 2][:, 0:512],
                                              AF.Silu), "a_si" + sfx)
        blk.scalar(f_sca)

        def f_vec(v):
            for pe in range(NES):
                _, ce, _ = FFN_ES[pe]
                for gg in range(2):
                    for ic in range(11):
                        icg = (pe * 2 + gg) * 11 + ic
                        v.wait_ge(ctx.sem("a_si" + sfx), b_asi + icg + 1)
                        v.wait_ge(ctx.sem("t_gu" + sfx), b_tgu + 2 * (icg + 1))
                        if hf == 1 and icg == 0:
                            v.wait_ge(ctx.sem("s_out"), b_so0)
                        if pe * 2 + gg >= 1:
                            v.wait_ge(ctx.sem("t_y" + sfx), b_ty + 32 * (pe * 2 + gg))
                        ctx.inc(v.tensor_mul(ff_a[:, 512 * ic:512 * (ic + 1)],
                                             pb[2 + icg % 2][:, 0:512],
                                             ff_sg[icg % 2]), "d_a" + sfx)
                    for hgc in range(8):
                        hgg = (pe * 2 + gg) * 8 + hgc
                        for tsub in range(4):
                            yt = 4 * hgg + tsub
                            v.wait_ge(ctx.sem("t_y" + sfx), b_ty + yt + 1)
                            gts = 4 * gg + tsub
                            ysl = y_acc[:, 2048 * gts + 256 * hgc:2048 * gts + 256 * (hgc + 1)]
                            ccol = combine[:, 17 * (8 * hf + gts) + ce:17 * (8 * hf + gts) + ce + 1]
                            n = ctx.cnt["d_y" + sfx] + 1 - b_dy
                            if n > 64:
                                v.wait_ge(ctx.sem("d_y" + sfx), b_dy + n - 64)
                            ctx.inc(v.scalar_tensor_tensor(
                                out=ysl, in0=pb[4 + yt % 2][:, 0:256],
                                scalar=ccol, in1=ysl,
                                op0=ALU.mult, op1=ALU.add), "d_y" + sfx)
        blk.vector(f_vec)

    def ph_out(hf):
        sfx = ""
        b_do = V("d_o" + sfx)
        b_dmu = V("d_mu" + sfx)
        b_ay = V("a_ysq" + sfx)
        b_am2 = V("a_m2" + sfx)
        b_dt2 = V("d_t2" + sfx)
        b_dvr = V("d_vr" + sfx)
        b_asr = V("a_sr" + sfx)
        b_sxr2 = V("s_xr2" + sfx)
        b_sout = V("s_out")

        def f_sync(sy):
            sy.wait_ge(ctx.sem("d_y" + sfx), V("d_y" + sfx))
            for gts in range(8):
                if gts >= 1:
                    sy.wait_ge(ctx.sem("d_o" + sfx), b_do + gts)
                ctx.inc(sy.dma_start(
                    out=out_x2r,
                    in_=x2_dram[1024 * hf + 128 * gts:1024 * hf + 128 * (gts + 1), :]),
                    "s_xr2" + sfx, 16)
                sy.wait_ge(ctx.sem("d_o" + sfx), b_do + gts + 1)
                ctx.inc(sy.dma_start(
                    out=out_d[1024 * hf + 128 * gts:1024 * hf + 128 * (gts + 1), :],
                    in_=out_ot), "s_out", 16)
        blk.sync(f_sync)

        def f_vec(v):
            v.wait_ge(ctx.sem("t_y" + sfx), V("t_y" + sfx))
            for gts in range(8):
                ysl = y_acc[:, 2048 * gts:2048 * (gts + 1)]
                e = ctx.inc(v.reduce_sum(sig[:, 0:1], ysl, axis=AX.X), "dse")
                v.wait_ge(ctx.sem("dse"), e)
                if gts >= 1:
                    v.wait_ge(ctx.sem("a_m2" + sfx), b_am2 + gts)
                ctx.inc(v.tensor_scalar_mul(sig[:, 2:3], sig[:, 0:1], RH),
                        "d_mu" + sfx)
                v.wait_ge(ctx.sem("a_ysq" + sfx), b_ay + gts + 1)
                e = ctx.inc(v.tensor_scalar_mul(sig[:, 3:4], sig[:, 1:2], RH),
                            "d_t2" + sfx)
                v.wait_ge(ctx.sem("d_t2" + sfx), e)
                v.wait_ge(ctx.sem("a_m2" + sfx), b_am2 + gts + 1)
                e = ctx.inc(v.tensor_sub(sig[:, 5:6], sig[:, 3:4], sig[:, 4:5]),
                            "d_vr" + sfx)
                v.wait_ge(ctx.sem("d_vr" + sfx), b_dvr + gts + 1)
                v.wait_ge(ctx.sem("a_sr" + sfx), b_asr + gts + 1)
                e = ctx.inc(v.reciprocal(sig[:, 7:8], sig[:, 6:7]), "dse")
                v.wait_ge(ctx.sem("dse"), e)
                e = ctx.inc(v.tensor_scalar(out=out_t1, in0=ysl,
                                            scalar1=sig[:, 2:3], scalar2=sig[:, 7:8],
                                            op0=ALU.subtract, op1=ALU.mult), "dse")
                v.wait_ge(ctx.sem("dse"), e)
                v.wait_ge(ctx.sem("s_xr2" + sfx), b_sxr2 + 16 * (gts + 1))
                if gts >= 1:
                    v.wait_ge(ctx.sem("s_out"), b_sout + 16 * gts)
                ctx.inc(v.tensor_add(out_ot, out_t1, out_x2r), "d_o" + sfx)
        blk.vector(f_vec)

        def f_sca(sc):
            for gts in range(8):
                ysl = y_acc[:, 2048 * gts:2048 * (gts + 1)]
                sc.wait_ge(ctx.sem("t_y" + sfx), V("t_y" + sfx))
                if gts >= 1:
                    sc.wait_ge(ctx.sem("d_t2" + sfx), b_dt2 + gts)
                ctx.inc(sc.activation(ff_ysq, ysl, AF.Square, accum_out=sig[:, 1:2]),
                        "a_ysq" + sfx)
                sc.wait_ge(ctx.sem("d_mu" + sfx), b_dmu + gts + 1)
                ctx.inc(sc.activation(sig[:, 4:5], sig[:, 2:3], AF.Square),
                        "a_m2" + sfx)
                sc.wait_ge(ctx.sem("d_vr" + sfx), b_dvr + gts + 1)
                ctx.inc(sc.activation(sig[:, 6:7], sig[:, 5:6], AF.Sqrt,
                                      bias=c_eps[:]), "a_sr" + sfx)
        blk.scalar(f_sca)

    def ph_dbg(names):
        if not DEBUG:
            return
        def f_sync(sy):
            for nm in names:
                src = {"dbg_k": k_sb[:], "dbg_v": v_sb[:], "dbg_q": bufB[:],
                       "dbg_lnd": bufA[:], "dbg_o": bufA[:], "dbg_ln2": bufB[:],
                       "dbg_cmb": combine[:], "dbg_y": y_acc[:, 0:16384],
                       "dbg_a": ff_a}[nm]
                ctx.inc(sy.dma_start(out=dbg[nm], in_=src), "s_dbg", 16)
            sy.wait_ge(ctx.sem("s_dbg"), ctx.cnt["s_dbg"])
        blk.sync(f_sync)

    def ph_dbg_x2():
        if not DEBUG:
            return
        def f_sync(sy):
            ctx.inc(sy.dma_start(out=dbg["dbg_x2"], in_=x2_dram), "s_dbg", 16)
            sy.wait_ge(ctx.sem("s_dbg"), ctx.cnt["s_dbg"])
        blk.sync(f_sync)

    def ph_final():
        def f_vec(v):
            v.wait_ge(ctx.sem("s_out"), ctx.cnt["s_out"])
        blk.vector(f_vec)

    ph_ld()
    ph_p0a()
    ph_barrier()
    if STAGE in ("p0", "attn", "proj", "full", "ffn0"):
        ph_p0b()
        ph_barrier()
    if STAGE == "p0":
        ph_dbg(["dbg_k", "dbg_v", "dbg_q", "dbg_lnd"])
    if STAGE in ("attn", "proj", "full", "ffn0"):
        ph_bounds()
        ph_barrier()
        ph_attn()
        ph_barrier()
    if STAGE == "attn":
        ph_dbg(["dbg_k", "dbg_v", "dbg_o"])
    if STAGE in ("proj", "full", "ffn0"):
        ph_proj()
        ph_barrier()
        ph_router()
        ph_barrier()
    if STAGE == "proj":
        ph_dbg(["dbg_ln2", "dbg_cmb"])
        ph_dbg_x2()
    if STAGE in ("full", "ffn0"):
        if FFN_ES:
            ph_ffn(0)
    if STAGE == "ffn0":
        ph_dbg(DBG_FFN_LIST)
        if not SKIP_OUT:
            ph_out(0)
            ph_barrier()
            ph_dbg_x2()
            ph_final()
        else:
            def f_fin(v):
                v.wait_ge(ctx.sem("s_dbg"), ctx.cnt["s_dbg"])
            blk.vector(f_fin)
    if STAGE in ("full",):
        ph_out(0)
        ph_ffn(1)
        ph_out(1)
        ph_final()
    blk.__exit__(None, None, None)
    return nc



def assemble(results):
    out = np.zeros((2, S, H), np.float32)
    for c in range(NCORE):
        b, r = divmod(c, 4)
        ch = chunks_of_core(r)
        o = results[c]["out"]
        for s_i, cc in enumerate(ch):
            out[b, CH * cc:CH * (cc + 1), :] = o[CH * s_i:CH * (s_i + 1), :]
    return out


def kernel(**inputs):
    in_maps = stage_inputs(inputs)
    nc = build_program()
    res = run_bass_kernel_spmd(nc, in_maps, core_ids=list(range(NCORE)))
    return assemble(res.results)


# version 05 (final)
# speedup vs baseline: 1.2243x; 1.1860x over previous
import numpy as np
import ml_dtypes
import concourse.bass as bass
import concourse.mybir as mybir
from concourse.bass_utils import run_bass_kernel_spmd

F32 = mybir.dt.float32
BF16 = mybir.dt.bfloat16
AF = mybir.ActivationFunctionType
ALU = mybir.AluOpType
AX = mybir.AxisListType

H = 2048
NH = 16
NF = 16
S = 8192
SCALE = float(128) ** -0.5
NCORE = 8
CH = 512
T_OWN = 2048
T_EXT = S + T_OWN
SLOT_PAD = [3, 7, 11, 15]
NPLAIN = sum(SLOT_PAD)
BIGNEG = -1.0e6
EPS = 1e-5
RH = 1.0 / H
E = 16

TC = 256
NTC = T_EXT // TC
PTC = 256
NPT = T_OWN // PTC

FFN_ES = [(e, e, "moe") for e in range(E)] + [(i, 16, "sh") for i in range(4)]

STAGE = "full"
DEBUG = False
SKIP_OUT = False
SERIAL_FFN = False
FFN_MEMSET = True
DBG_FFN_LIST = ["dbg_y", "dbg_a", "dbg_ln2", "dbg_cmb"]


def chunks_of_core(r):
    return [r, 7 - r, 8 + r, 15 - r]



def stage_inputs(inp):
    x = np.asarray(inp["L0_attn_x"])
    W_qkv = np.asarray(inp["L0_attn_W_qkv"])
    b_qkv = np.asarray(inp["L0_attn_b_qkv"])
    W_proj = np.asarray(inp["L0_attn_W_proj"])
    b_proj = np.asarray(inp["L0_attn_b_proj"])
    w_router = np.asarray(inp["L0_ffn_w_router"])
    w_gate = np.asarray(inp["L0_ffn_w_gate"])
    w_up = np.asarray(inp["L0_ffn_w_up"])
    w_down = np.asarray(inp["L0_ffn_w_down"])
    ws_gate = np.asarray(inp["L0_ffn_ws_gate"])
    ws_up = np.asarray(inp["L0_ffn_ws_up"])
    ws_down = np.asarray(inp["L0_ffn_ws_down"])
    w_sg = np.asarray(inp["L0_ffn_w_shared_gate"])

    C = np.ascontiguousarray
    sh = {}
    sh["w_kv"] = C(W_qkv[:, 2048:].reshape(16, 128, 256).transpose(1, 0, 2).reshape(128, 16 * 256))
    sh["w_q"] = C(W_qkv[:, :2048].reshape(16, 128, 16, 128).transpose(2, 1, 0, 3).reshape(16, 128, 16 * 128))
    sh["w_proj"] = C(W_proj.reshape(16, 128, 16, 128).transpose(2, 1, 0, 3).reshape(16, 128, 16 * 128))
    sh["b_q"] = C(b_qkv[:2048].reshape(16, 128).T)
    sh["b_k"] = C(b_qkv[2048:2176].reshape(128, 1))
    sh["b_v"] = C(b_qkv[2176:2304].reshape(128, 1))
    sh["b_proj"] = C(b_proj.reshape(16, 128).T)
    sh["w_rt"] = C(np.concatenate([w_router, w_sg], axis=1).reshape(16, 128, 17).transpose(1, 0, 2).reshape(128, 16 * 17))
    sh["w_g"] = C(w_gate.reshape(E, 16, 128, 11, 128).transpose(0, 3, 2, 1, 4).reshape(E, 11, 128, 16 * 128))
    sh["w_u"] = C(w_up.reshape(E, 16, 128, 11, 128).transpose(0, 3, 2, 1, 4).reshape(E, 11, 128, 16 * 128))
    sh["w_d"] = C(w_down.reshape(E, 11, 128, 8, 256).transpose(0, 3, 2, 1, 4).reshape(E, 8, 128, 11 * 256))
    sh["ws_g"] = C(ws_gate.reshape(16, 128, 4, 11, 128).transpose(2, 3, 1, 0, 4).reshape(4, 11, 128, 16 * 128))
    sh["ws_u"] = C(ws_up.reshape(16, 128, 4, 11, 128).transpose(2, 3, 1, 0, 4).reshape(4, 11, 128, 16 * 128))
    sh["ws_d"] = C(ws_down.reshape(4, 11, 128, 8, 256).transpose(0, 3, 2, 1, 4).reshape(4, 8, 128, 11 * 256))
    ii = np.arange(CH)
    tri_t = np.where(ii[None, :] >= ii[:, None], 0.0, BIGNEG).astype(np.float32)
    sh["tri_t"] = C(tri_t.reshape(4, 128, CH).transpose(1, 0, 2).reshape(128, 4 * CH))

    in_maps = []
    for c in range(NCORE):
        b, r = divmod(c, 4)
        ch = chunks_of_core(r)
        xb = x[b]
        own = np.concatenate([xb[CH * cc:CH * (cc + 1)] for cc in ch], axis=0)
        x_ext = np.concatenate([xb, own], axis=0).T
        x_chunks = C(x_ext.reshape(16, 128, NTC, TC).transpose(2, 1, 0, 3).reshape(NTC, 128, 16 * TC))
        m = dict(sh)
        m["x_ext"] = x_chunks
        bn = np.zeros((128, NPLAIN + 1), np.float32)
        col = 0
        for s_i in range(4):
            for j in range(SLOT_PAD[s_i]):
                if j >= ch[s_i]:
                    bn[:, col] = BIGNEG
                col += 1
        m["bigneg"] = C(bn * SCALE)
        in_maps.append(m)
    return in_maps



class Ctx:
    def __init__(self, nc):
        self.nc = nc
        self.sems = {}
        self.cnt = {}

    def sem(self, name):
        if name not in self.sems:
            self.sems[name] = self.nc.alloc_semaphore(name)
            self.cnt[name] = 0
        return self.sems[name]

    def inc(self, inst, name, n=1):
        inst.then_inc(self.sem(name), n)
        self.cnt[name] += n
        return self.cnt[name]

    def val(self, name):
        self.sem(name)
        return self.cnt[name]


def build_program():
    nc = bass.Bass("TRN2", target_bir_lowering=False)
    ctx = Ctx(nc)

    def din(name, shape, dtype=F32):
        return nc.dram_tensor(name, list(shape), dtype, kind="ExternalInput").ap()

    x_ext = din("x_ext", [NTC, 128, 16 * TC])
    w_kv_d = din("w_kv", [128, 16 * 256])
    w_q_d = din("w_q", [16, 128, 16 * 128])
    w_proj_d = din("w_proj", [16, 128, 16 * 128])
    b_q_d = din("b_q", [128, 16])
    b_k_d = din("b_k", [128, 1])
    b_v_d = din("b_v", [128, 1])
    b_proj_dd = din("b_proj", [128, 16])
    w_rt_d = din("w_rt", [128, 16 * 17])
    w_g_d = din("w_g", [E, 11, 128, 16 * 128])
    w_u_d = din("w_u", [E, 11, 128, 16 * 128])
    w_d_d = din("w_d", [E, 8, 128, 11 * 256])
    ws_g_d = din("ws_g", [4, 11, 128, 16 * 128])
    ws_u_d = din("ws_u", [4, 11, 128, 16 * 128])
    ws_d_d = din("ws_d", [4, 8, 128, 11 * 256])
    tri_t_d = din("tri_t", [128, 4 * CH])
    bigneg_d = din("bigneg", [128, NPLAIN + 1])

    out_d = nc.dram_tensor("out", [T_OWN, H], F32, kind="ExternalOutput").ap()
    dbg = {}
    if DEBUG:
        for name, shape, dt_ in [
            ("dbg_k", [128, T_EXT], BF16),
            ("dbg_v", [128, T_EXT], BF16),
            ("dbg_q", [128, NH * T_OWN], BF16),
            ("dbg_lnd", [128, NF * T_OWN], BF16),
            ("dbg_o", [128, NH * T_OWN], BF16),
            ("dbg_ln2", [128, NF * T_OWN], BF16),
            ("dbg_x2", [T_OWN, H], F32),
            ("dbg_cmb", [128, 16 * 17], F32),
            ("dbg_y", [128, NF * 1024], F32),
            ("dbg_a", [128, 11 * 512], BF16),
        ]:
            dbg[name] = nc.dram_tensor(name, shape, dt_, kind="ExternalOutput").ap()

    x2_dram = nc.dram_tensor("x2_scratch", [T_OWN, H], F32, kind="Internal").ap()

    it = nc.inline_tensor
    ident_bf_d = it(np.eye(128, dtype=ml_dtypes.bfloat16), name="ident_bf").ap()
    ident_f_d = it(np.eye(128, dtype=np.float32), name="ident_f").ap()
    ones_bf_d = it(np.ones((128, 1), dtype=ml_dtypes.bfloat16), name="ones_bf").ap()
    ones1f_d = it(np.ones((1, 128), dtype=np.float32), name="ones1f").ap()
    eps_d = it(np.full((128, 1), EPS, dtype=np.float32), name="epsc").ap()

    sb = nc.alloc_sbuf_tensor
    k_sb = sb("k_sb", [128, T_EXT], BF16)
    v_sb = sb("v_sb", [128, T_EXT], BF16)
    bufA = sb("bufA", [128, NF * T_OWN], BF16)
    bufB = sb("bufB", [128, NF * T_OWN], BF16)
    bufD = sb("bufD", [128, 8192], F32)
    c_ident_bf = sb("c_ident_bf", [128, 128], BF16)
    c_ident_f = sb("c_ident_f", [128, 128], F32)
    c_ones_bf = sb("c_ones_bf", [128, 1], BF16)
    c_ones1f = sb("c_ones1f", [1, 128], F32)
    c_eps = sb("c_eps", [128, 1], F32)
    c_bq = sb("c_bq", [128, 16], F32)
    c_bk = sb("c_bk", [128, 1], F32)
    c_bv = sb("c_bv", [128, 1], F32)
    c_bproj = sb("c_bproj", [128, 16], F32)
    c_bigneg = sb("c_bigneg", [128, NPLAIN + 1], F32)
    c_wrt = sb("c_wrt", [128, 16 * 17], BF16)
    rows = sb("rows", [1, 1024], F32)
    maxk = sb("maxk", [1, 1], F32)
    combine = sb("combine", [128, 16 * 17], F32)
    sig = sb("sig", [128, 8], F32)

    bufA_f = bufA[:].bitcast(F32)
    bufB_f = bufB[:].bitcast(F32)
    bufD_bf = bufD[:].bitcast(BF16)
    k_bfv = k_sb[:]
    v_bfv = v_sb[:]

    pb = [nc.alloc_psum_tensor(f"pb{i}", [128, 512], F32) for i in range(6)]
    pb_bf = nc.alloc_psum_tensor("pb_bf", [128, 1024], BF16)
    pb7 = nc.alloc_psum_tensor("pb7", [128, 512], F32)

    xbf_t = [bufB[:, 0:4096], bufB[:, 20480:24576]]
    sq_t = [bufB[:, 4096:8192], bufB[:, 24576:28672]]
    ln1c_t = [bufB[:, 8192:12288], bufB[:, 28672:32768]]
    vfm_t = [bufB[:, 12288:12544], bufB[:, 13824:14080]]
    wkv_sb = bufB[:, 16384:16384 + 16 * 256]
    tmp0 = bufB_f[:, 6400:6656]
    tmp1 = bufB_f[:, 6656:6912]
    rows2 = bufB_f[0:1, 7040:8064]

    def xstage(i):
        o = (i % 2) * (NF * TC)
        return bufD[:, o:o + NF * TC]

    att_stt = [bufD[:, 0:512], bufD[:, 512:1024]]
    att_bb = bufD[:, 1024:1536]
    att_rs = bufD[:, 1536:2048]
    att_pt = [bufD_bf[:, 2048:2560], bufD_bf[:, 2560:3072],
              bufD_bf[:, 3072:3584], bufD_bf[:, 3584:4096]]
    att_sq = bufD_bf[:, 10240:10240 + 2048]
    att_tri = bufD[:, 6144:8192]
    bnd_row = bufD[0:1, 2560:4608]

    pr_xr = bufD[:, 0:NF * PTC]
    pr_tmp = [bufD[:, NF * PTC:NF * PTC + 256], bufD[:, NF * PTC + 256:NF * PTC + 512]]
    pr_x2t = bufD[:, NF * PTC + 512:NF * PTC + 512 + 2048]
    v_f32 = v_sb[:].bitcast(F32)
    pr_a1 = v_f32[:, 0:NF * PTC]
    wp_buf = [k_bfv[:, 0:2048], k_bfv[:, 2048:4096]]

    rt_ex = bufD[:, 0:16]
    rt_pr = bufD[:, 16:32]
    rt_wk = bufD[:, 32:48]
    rt_mk = bufD[:, 48:64]

    ff_a = bufD_bf[:, 0:11 * 512]
    ff_sg = [bufD[:, 4096:4608], bufD[:, 4608:5120]]
    wg_buf = [k_bfv[:, 0:2048], k_bfv[:, 2048:4096]]
    wu_buf = [k_bfv[:, 4096:6144], k_bfv[:, 6144:8192]]
    wd_buf = [v_bfv[:, 0:2816], v_bfv[:, 2816:5632], v_bfv[:, 5632:8448]]
    ff_ysq = k_bfv[:, 8192:10240]
    out_x2r = bufD[:, 5632:5632 + 2048]
    out_x2r = bufD[:, 6144 - 2048 + 2048:8192]
    out_x2r = bufD[:, 6144:8192]
    out_t1 = bufD[:, 4096 - 2048 + 0:4096]
    out_t1 = bufD[:, 2048 + 1024:2048 + 1024 + 2048]
    out_ot = bufD[:, 0:2048]
    y_acc = bufA_f
    ln2 = bufB

    blk = nc.Block()
    blk.__enter__()

    V = ctx.val

    def ph_ld():
        def f_sync(sy):
            for dst, src in [
                (c_ident_bf[:], ident_bf_d), (c_ident_f[:], ident_f_d),
                (c_ones_bf[:], ones_bf_d), (c_ones1f[:], ones1f_d),
                (c_eps[:], eps_d), (c_bq[:], b_q_d), (c_bk[:], b_k_d),
                (c_bv[:], b_v_d), (c_bproj[:], b_proj_dd),
                (c_bigneg[:], bigneg_d),
            ]:
                ctx.inc(sy.dma_start(out=dst, in_=src), "ld", 16)
        blk.sync(f_sync)

        def f_gp(g):
            ctx.inc(g.dma_start(out=wkv_sb, in_=w_kv_d), "ld", 16)
            ctx.inc(g.dma_start(out=c_wrt[:], in_=w_rt_d), "ld", 16)
        blk.gpsimd(f_gp)

    def ph_p0a():
        ld_done = V("ld")

        def rset(i):
            return rows[0:1, :] if i % 2 == 0 else rows2

        def f_sync(sy):
            sy.wait_ge(ctx.sem("ld"), ld_done)
            for i in range(NTC):
                if i >= 2:
                    sy.wait_ge(ctx.sem("d_norm"), i - 1)
                ctx.inc(sy.dma_start(out=xstage(i), in_=x_ext[i]), "s_x", 16)
        blk.sync(f_sync)

        def f_sca(sc):
            sc.wait_ge(ctx.sem("ld"), ld_done)
            for i in range(NTC):
                p = i % 2
                r = rset(i)
                xs = xstage(i)
                sc.wait_ge(ctx.sem("s_x"), 16 * (i + 1))
                if i >= 2:
                    sc.wait_ge(ctx.sem("t_stats"), 2 * (i - 1))
                ctx.inc(sc.activation(xbf_t[p], xs, AF.Copy), "a_cast")
                ctx.inc(sc.activation(sq_t[p], xs, AF.Square), "a_sq")
                sc.wait_ge(ctx.sem("d_rows"), 2 * (i + 1))
                if i >= 2:
                    sc.wait_ge(ctx.sem("d_rr"), i - 1)
                ctx.inc(sc.activation(r[0:1, 512:768], r[0:1, 0:256],
                                      AF.Square), "a_mu2")
                sc.wait_ge(ctx.sem("d_var"), i + 1)
                ctx.inc(sc.activation(r[0:1, 512:768], r[0:1, 768:1024],
                                      AF.Sqrt, bias=c_eps[0:1, :]), "a_sqrt")
                sc.wait_ge(ctx.sem("t_kv"), 2 * (i + 1))
                if i >= 2:
                    sc.wait_ge(ctx.sem("t_vt"), 2 * (i - 1))
                ctx.inc(sc.activation(k_sb[:, TC * i:TC * (i + 1)], pb[0][:, 0:TC],
                                      AF.Identity, bias=c_bk[:]), "a_k")
                ctx.inc(sc.activation(vfm_t[p], pb[1][:, 0:TC],
                                      AF.Identity, bias=c_bv[:]), "a_vf")
        blk.scalar(f_sca)

        def f_ten(t):
            for i in range(NTC):
                p = i % 2
                r = rset(i)
                t.wait_ge(ctx.sem("a_cast"), i + 1)
                t.wait_ge(ctx.sem("a_sq"), i + 1)
                if i >= 1:
                    t.wait_ge(ctx.sem("d_rows"), 2 * i)
                mm = None
                for fc in range(16):
                    mm = t.matmul(pb[2][0:1, 0:TC], c_ones_bf[:],
                                  xbf_t[p][:, TC * fc:TC * (fc + 1)],
                                  start=(fc == 0), stop=(fc == 15))
                ctx.inc(mm, "t_stats")
                for fc in range(16):
                    mm = t.matmul(pb[3][0:1, 0:TC], c_ones_bf[:],
                                  sq_t[p][:, TC * fc:TC * (fc + 1)],
                                  start=(fc == 0), stop=(fc == 15))
                ctx.inc(mm, "t_stats")
                t.wait_ge(ctx.sem("d_rr"), i + 1)
                if i >= 1:
                    t.wait_ge(ctx.sem("d_norm"), i)
                ctx.inc(t.matmul(pb[4][:, 0:TC], c_ones1f[:], r[0:1, 0:TC],
                                 start=True, stop=True), "t_bc")
                ctx.inc(t.matmul(pb[5][:, 0:TC], c_ones1f[:], r[0:1, 768:768 + TC],
                                 start=True, stop=True), "t_bc")
                t.wait_ge(ctx.sem("d_norm"), i + 1)
                if i >= 1:
                    t.wait_ge(ctx.sem("a_k"), i)
                    t.wait_ge(ctx.sem("a_vf"), i)
                for which, pbi in ((0, pb[0]), (1, pb[1])):
                    mm = None
                    for fc in range(16):
                        if i < 32:
                            rhs = ln1c_t[p][:, TC * fc:TC * (fc + 1)]
                        else:
                            rhs = bufA[:, 2048 * fc + TC * (i - 32):2048 * fc + TC * (i - 32) + TC]
                        lhsT = wkv_sb[:, 256 * fc + 128 * which:256 * fc + 128 * which + 128]
                        mm = t.matmul(pbi[:, 0:TC], lhsT, rhs,
                                      start=(fc == 0), stop=(fc == 15))
                    ctx.inc(mm, "t_kv")
                t.wait_ge(ctx.sem("a_vf"), i + 1)
                if i >= 1:
                    t.wait_ge(ctx.sem("d_vt"), 2 * i)
                ctx.inc(t.transpose(pb_bf[:, 0:128], vfm_t[p][:, 0:128], c_ident_bf[:]), "t_vt")
                ctx.inc(t.transpose(pb_bf[:, 128:256], vfm_t[p][:, 128:256], c_ident_bf[:]), "t_vt")
        blk.tensor(f_ten)

        def f_vec(v):
            for i in range(NTC):
                p = i % 2
                r = rset(i)
                v.wait_ge(ctx.sem("t_stats"), 2 * (i + 1))
                if i >= 2:
                    v.wait_ge(ctx.sem("t_bc"), 2 * (i - 1))
                ctx.inc(v.tensor_scalar_mul(r[0:1, 0:TC], pb[2][0:1, 0:TC], RH), "d_rows")
                ctx.inc(v.tensor_scalar_mul(r[0:1, 256:256 + TC], pb[3][0:1, 0:TC], RH), "d_rows")
                v.wait_ge(ctx.sem("a_mu2"), i + 1)
                ctx.inc(v.tensor_sub(r[0:1, 768:768 + TC], r[0:1, 256:256 + TC],
                                     r[0:1, 512:512 + TC]), "d_var")
                v.wait_ge(ctx.sem("a_sqrt"), i + 1)
                ctx.inc(v.reciprocal(r[0:1, 768:768 + TC], r[0:1, 512:512 + TC]), "d_rr")
                v.wait_ge(ctx.sem("t_bc"), 2 * (i + 1))
                if i >= 2:
                    v.wait_ge(ctx.sem("t_kv"), 2 * (i - 1))
                xs = xstage(i)
                for fc in range(16):
                    tm = tmp0 if fc % 2 == 0 else tmp1
                    sub = v.tensor_sub(tm, xs[:, TC * fc:TC * (fc + 1)], pb[4][:, 0:TC])
                    se = ctx.inc(sub, "dse")
                    v.wait_ge(ctx.sem("dse"), se)
                    if i < 32:
                        dst = ln1c_t[p][:, TC * fc:TC * (fc + 1)]
                    else:
                        dst = bufA[:, 2048 * fc + TC * (i - 32):2048 * fc + TC * (i - 32) + TC]
                    mul = v.tensor_mul(dst, tm, pb[5][:, 0:TC])
                    if fc == 15:
                        ctx.inc(mul, "d_norm")
                v.wait_ge(ctx.sem("t_vt"), 2 * (i + 1))
                ctx.inc(v.tensor_copy(v_sb[:, TC * i:TC * (i + 1)], pb_bf[:, 0:256]),
                        "d_vt", 2)
        blk.vector(f_vec)

    def ph_barrier():
        tgt = ctx.val("bar") + 5
        for attr in (blk.sync, blk.gpsimd, blk.tensor, blk.vector, blk.scalar):
            def f(eng, tgt=tgt):
                ctx.inc(eng.nop(), "bar")
                eng.wait_ge(ctx.sem("bar"), tgt)
            attr(f)
        assert ctx.cnt["bar"] == tgt

    def ph_p0b():
        b_swq = V("s_wq")
        b_tq = V("t_q")
        b_aq = V("a_q")
        wqb = [bufD_bf[:, 0:2048], bufD_bf[:, 2048:4096]]

        def f_gp(g):
            for oc in range(16):
                if oc >= 2:
                    g.wait_ge(ctx.sem("t_q"), b_tq + 4 * (oc - 1))
                ctx.inc(g.dma_start(out=wqb[oc % 2], in_=w_q_d[oc]), "s_wq", 16)
        blk.gpsimd(f_gp)

        def f_ten(t):
            for oc in range(16):
                t.wait_ge(ctx.sem("s_wq"), b_swq + 16 * (oc + 1))
                for tg in range(4):
                    idx = 4 * oc + tg
                    if idx >= 2:
                        t.wait_ge(ctx.sem("a_q"), b_aq + idx - 1)
                    mm = None
                    for fc in range(16):
                        mm = t.matmul(pb[idx % 2][:, 0:512],
                                      wqb[oc % 2][:, 128 * fc:128 * (fc + 1)],
                                      bufA[:, 2048 * fc + 512 * tg:2048 * fc + 512 * (tg + 1)],
                                      start=(fc == 0), stop=(fc == 15))
                    ctx.inc(mm, "t_q")
        blk.tensor(f_ten)

        def f_sca(sc):
            for oc in range(16):
                for tg in range(4):
                    idx = 4 * oc + tg
                    sc.wait_ge(ctx.sem("t_q"), b_tq + idx + 1)
                    ctx.inc(sc.activation(
                        bufB[:, 2048 * oc + 512 * tg:2048 * oc + 512 * (tg + 1)],
                        pb[idx % 2][:, 0:512], AF.Identity,
                        bias=c_bq[:, oc:oc + 1]), "a_q")
        blk.scalar(f_sca)

    def ph_bounds():
        b_aks = V("a_ks")
        b_tkn = V("t_kn")
        b_dkm = V("d_km")
        b_aqs = V("a_qs")
        b_tqn = V("t_qn")
        b_aqb = V("a_qb")
        b_dqb = V("d_qb")

        b_ld = V("ld")

        def f_sync(sy):
            ctx.inc(sy.dma_start(out=att_tri, in_=tri_t_d), "ld", 16)
        blk.sync(f_sync)

        def f_vec0(v):
            ctx.inc(v.memset(maxk[:], 0.0), "d_km")
        blk.vector(f_vec0)

        def f_sca(sc):
            sc.wait_ge(ctx.sem("ld"), b_ld + 16)
            for cc in range(20):
                if cc >= 1:
                    sc.wait_ge(ctx.sem("t_kn"), b_tkn + cc)
                ctx.inc(sc.activation(att_sq[:, 0:512], k_sb[:, 512 * cc:512 * (cc + 1)],
                                      AF.Square), "a_ks")
        blk.scalar(f_sca)

        def f_ten(t):
            for cc in range(20):
                t.wait_ge(ctx.sem("a_ks"), b_aks + cc + 1)
                if cc >= 1:
                    t.wait_ge(ctx.sem("d_km"), b_dkm + 2 * cc)
                ctx.inc(t.matmul(pb[2][0:1, 0:512], c_ones_bf[:], att_sq[:, 0:512],
                                 start=True, stop=True), "t_kn")
        blk.tensor(f_ten)

        def f_vec(v):
            v.wait_ge(ctx.sem("d_km"), 1)
            for cc in range(20):
                v.wait_ge(ctx.sem("t_kn"), b_tkn + cc + 1)
                e1 = ctx.inc(v.reduce_max(rows[0:1, 4:5], pb[2][0:1, 0:512], axis=AX.X),
                             "d_km")
                v.wait_ge(ctx.sem("d_km"), e1)
                ctx.inc(v.tensor_max(maxk[:], maxk[:], rows[0:1, 4:5]), "d_km")
            e1 = ctx.inc(v.tensor_scalar_mul(maxk[:], maxk[:], 1.0404), "d_km")
        blk.vector(f_vec)

        def f_sca2(sc):
            sc.wait_ge(ctx.sem("d_km"), b_dkm + 42)
            ctx.inc(sc.activation(maxk[:], maxk[:], AF.Sqrt), "a_mk")
        blk.scalar(f_sca2)

    def ph_attn():
        b_tbb = V("t_bb")
        b_abb = V("a_bb")
        b_tst = V("t_st")
        b_dbt = V("d_bt")
        b_aexp = V("a_exp")
        b_tpv = V("t_pv")
        b_tsg = V("t_sg")
        b_drsg = V("d_rsg")
        b_trs = V("t_rs")
        b_ars = V("a_rs")
        b_dof = V("d_of")
        b_dtri = V("d_tri")
        b_tbn = V("t_bn")
        b_aqs = V("a_qs")
        b_abq = V("a_bq")
        b_dbn = V("d_bn")
        b_dbs = V("d_bs")

        def iters():
            for h_ in range(16):
                for s_ in range(4):
                    yield h_, s_

        def kbs(s_):
            off = sum(SLOT_PAD[:s_])
            out = [(512 * j, off + j, False) for j in range(SLOT_PAD[s_])]
            out.append((S + 512 * s_, NPLAIN, True))
            return out

        tiles_before = {}
        iters_list = list(iters())
        tb_cum = 0
        for h_, s_ in iters_list:
            tiles_before[(h_, s_)] = tb_cum
            tb_cum += 4 * (SLOT_PAD[s_] + 1)

        bias_tab = [bufD[:, 4608 + 32 * i:4608 + 32 * i + 16] for i in range(2)]
        nb_hs = [bufD[:, 4672 + 32 * i:4672 + 32 * i + 1] for i in range(2)]
        bsc = bufD[0:1, 4740:4744]

        def f_ten_real(t):
            t.wait_ge(ctx.sem("a_mk"), 1)
            for itn, (h_, s_) in enumerate(iters()):
                kl = kbs(s_)
                t0 = tiles_before[(h_, s_)]
                ntile = 4 * len(kl)
                p2 = itn % 2
                if s_ == 0:
                    t.wait_ge(ctx.sem("a_qs"), b_aqs + h_ + 1)
                    for tb in range(4):
                        if itn >= 1 or tb >= 1:
                            t.wait_ge(ctx.sem("a_bq"), b_abq + 4 * h_ + tb)
                        if tb == 0 and itn >= 1:
                            t.wait_ge(ctx.sem("a_rs"), b_ars + itn)
                        ctx.inc(t.matmul(pb7[0:1, 0:512], c_ones_bf[:],
                                         att_sq[:, 512 * tb:512 * (tb + 1)],
                                         start=True, stop=True), "t_bn")
                t.wait_ge(ctx.sem("d_bs"), b_dbs + itn + 1)
                if s_ >= 1:
                    t.wait_ge(ctx.sem("a_rs"), b_ars + itn)
                else:
                    t.wait_ge(ctx.sem("a_bq"), b_abq + 4 * (h_ + 1))
                ctx.inc(t.matmul(pb7[:, 0:1], c_ones1f[:], bsc[0:1, 0:1],
                                 start=True, stop=True), "t_bb")
                qrhs = bufB[:, 2048 * h_ + 512 * s_:2048 * h_ + 512 * s_ + 512]

                def pv_sg(ti):
                    gt = t0 + ti
                    kb, ks = divmod(ti, 4)
                    kcol, _, _ = kl[kb]
                    t.wait_ge(ctx.sem("a_exp"), b_aexp + gt + 1)
                    if ti == 0 and itn >= 2:
                        t.wait_ge(ctx.sem("d_of"), b_dof + itn - 1)
                        t.wait_ge(ctx.sem("d_rsg"), b_drsg + itn - 1)
                    ctx.inc(t.matmul(pb[2 + p2][:, 0:512],
                                     v_sb[:, kcol + 128 * ks:kcol + 128 * (ks + 1)],
                                     att_pt[gt % 4],
                                     start=(ti == 0), stop=(ti == ntile - 1)), "t_pv")
                    ctx.inc(t.matmul(pb[4 + p2][0:1, 0:512], c_ones_bf[:],
                                     att_pt[gt % 4],
                                     start=(ti == 0), stop=(ti == ntile - 1)), "t_sg")

                for ti in range(ntile):
                    kb, ks = divmod(ti, 4)
                    kcol, bcol, isdiag = kl[kb]
                    gt = t0 + ti
                    if gt >= 2:
                        t.wait_ge(ctx.sem("a_exp"), b_aexp + gt - 1)
                    ctx.inc(t.matmul(pb[gt % 2][:, 0:512],
                                     k_sb[:, kcol + 128 * ks:kcol + 128 * (ks + 1)],
                                     qrhs, start=True, stop=True), "t_st")
                    if ti >= 1:
                        pv_sg(ti - 1)
                pv_sg(ntile - 1)
                t.wait_ge(ctx.sem("d_rsg"), b_drsg + itn + 1)
                t.wait_ge(ctx.sem("a_bb"), b_abb + itn + 1)
                ctx.inc(t.matmul(pb7[:, 0:512], c_ones1f[:], rows[0:1, 0:512],
                                 start=True, stop=True), "t_rs")
        blk.tensor(f_ten_real)

        def f_vec(v):
            for itn, (h_, s_) in enumerate(iters()):
                kl = kbs(s_)
                t0 = tiles_before[(h_, s_)]
                ntile = 4 * len(kl)
                p2 = itn % 2
                if s_ == 0:
                    for tb in range(4):
                        v.wait_ge(ctx.sem("a_bq"), b_abq + 4 * h_ + tb + 1)
                        ctx.inc(v.tensor_scalar_mul(
                            bnd_row[0:1, 512 * tb:512 * (tb + 1)],
                            bnd_row[0:1, 512 * tb:512 * (tb + 1)], maxk[0:1, :]),
                            "d_bn")
                v.wait_ge(ctx.sem("d_bn"), b_dbn + 4 * h_ + s_ + 1)
                if itn >= 1:
                    v.wait_ge(ctx.sem("t_bb"), b_tbb + itn)
                e = ctx.inc(v.reduce_max(bsc[0:1, 0:1],
                                         bnd_row[0:1, 512 * s_:512 * (s_ + 1)],
                                         axis=AX.X), "dse")
                v.wait_ge(ctx.sem("dse"), e)
                ctx.inc(v.tensor_scalar_mul(bsc[0:1, 0:1], bsc[0:1, 0:1], -SCALE),
                        "d_bs")
                off = sum(SLOT_PAD[:s_])
                v.wait_ge(ctx.sem("a_bb"), b_abb + itn + 1)
                if itn >= 2:
                    prev_t0 = t0 - 4 * (SLOT_PAD[iters_list[itn - 1][1]] + 1)
                    v.wait_ge(ctx.sem("a_exp"), b_aexp + prev_t0)
                ctx.inc(v.tensor_scalar_add(bias_tab[p2][:, 0:SLOT_PAD[s_]],
                                            c_bigneg[:, off:off + SLOT_PAD[s_]],
                                            nb_hs[p2][:, 0:1]), "d_bt")
                for ti in range(ntile - 4, ntile):
                    kb, ks = divmod(ti, 4)
                    gt = t0 + ti
                    v.wait_ge(ctx.sem("t_st"), b_tst + gt + 1)
                    ctx.inc(v.tensor_add(pb[gt % 2][:, 0:512], pb[gt % 2][:, 0:512],
                                         att_tri[:, 512 * ks:512 * (ks + 1)]),
                            "d_tri")
                v.wait_ge(ctx.sem("t_sg"), b_tsg + t0 + ntile)
                if itn >= 1:
                    v.wait_ge(ctx.sem("t_rs"), b_trs + itn)
                ctx.inc(v.reciprocal(rows[0:1, 0:512], pb[4 + p2][0:1, 0:512]), "d_rsg")
                v.wait_ge(ctx.sem("a_rs"), b_ars + itn + 1)
                v.wait_ge(ctx.sem("t_pv"), b_tpv + t0 + ntile)
                ctx.inc(v.tensor_mul(
                    bufA[:, 2048 * h_ + 512 * s_:2048 * h_ + 512 * s_ + 512],
                    pb[2 + p2][:, 0:512], att_rs), "d_of")
        blk.vector(f_vec)

        def f_sca(sc):
            sc.wait_ge(ctx.sem("a_mk"), 1)
            for itn, (h_, s_) in enumerate(iters()):
                kl = kbs(s_)
                t0 = tiles_before[(h_, s_)]
                ntile = 4 * len(kl)
                p2 = itn % 2
                if s_ == 0:
                    if itn >= 1:
                        sc.wait_ge(ctx.sem("t_bn"), b_tbn + 4 * h_)
                    ctx.inc(sc.activation(att_sq[:, 0:2048],
                                          bufB[:, 2048 * h_:2048 * (h_ + 1)],
                                          AF.Square), "a_qs")
                    for tb in range(4):
                        sc.wait_ge(ctx.sem("t_bn"), b_tbn + 4 * h_ + tb + 1)
                        if itn >= 1 and tb == 0:
                            sc.wait_ge(ctx.sem("t_bb"), b_tbb + itn)
                        ctx.inc(sc.activation(bnd_row[0:1, 512 * tb:512 * (tb + 1)],
                                              pb7[0:1, 0:512], AF.Sqrt), "a_bq")
                sc.wait_ge(ctx.sem("t_bb"), b_tbb + itn + 1)
                ctx.inc(sc.activation(nb_hs[p2], pb7[:, 0:1], AF.Copy), "a_bb")
                ndiag = ntile - 4
                for ti in range(ntile):
                    kb, ks = divmod(ti, 4)
                    kcol, bcol, isdiag = kl[kb]
                    gt = t0 + ti
                    sc.wait_ge(ctx.sem("t_st"), b_tst + gt + 1)
                    sc.wait_ge(ctx.sem("d_bt"), b_dbt + itn + 1)
                    if isdiag:
                        sc.wait_ge(ctx.sem("d_tri"), b_dtri + 4 * itn + ks + 1)
                    if gt >= 4:
                        sc.wait_ge(ctx.sem("t_pv"), b_tpv + gt - 3)
                    bias = nb_hs[p2] if isdiag else bias_tab[p2][:, kb:kb + 1]
                    ctx.inc(sc.activation(att_pt[gt % 4], pb[gt % 2][:, 0:512],
                                          AF.Exp, scale=SCALE, bias=bias), "a_exp")
                sc.wait_ge(ctx.sem("t_rs"), b_trs + itn + 1)
                ctx.inc(sc.activation(att_rs, pb7[:, 0:512], AF.Copy), "a_rs")
        blk.scalar(f_sca)

    def ph_proj():
        b_sxr = V("s_xr")
        b_swp = V("s_wp")
        b_tpj = V("t_pj")
        b_apj = V("a_pj")
        b_tps = V("t_ps")
        b_dprw = V("d_prw")
        b_apm = V("a_pm")
        b_dpv = V("d_pv")
        b_apsq = V("a_psq")
        b_dprr = V("d_prr")
        b_tpbc = V("t_pbc")
        b_dx2 = V("d_x2")
        b_al2 = V("a_l2")
        b_tl2s = V("t_l2s")
        b_dl2r = V("d_l2r")
        b_al2m = V("a_l2m")
        b_dl2v = V("d_l2v")
        b_al2q = V("a_l2q")
        b_dl2r2 = V("d_l2r2")
        b_tl2bc = V("t_l2bc")
        b_dln2 = V("d_ln2")
        b_txt = V("t_xt")
        b_axt = V("a_xt")
        b_sx2w = V("s_x2w")

        pr_bt = [k_bfv[:, 4096:4352], k_bfv[:, 4352:4608]]
        pr_st = [k_bfv[:, 4608:4864], k_bfv[:, 4864:5120]]

        def f_sync(sy):
            for pt in range(NPT):
                if pt >= 1:
                    sy.wait_ge(ctx.sem("d_x2"), b_dx2 + 48 * pt)
                ctx.inc(sy.dma_start(out=pr_xr, in_=x_ext[32 + pt]), "s_xr", 16)
                for tsb in range(2):
                    sy.wait_ge(ctx.sem("a_xt"), b_axt + 16 * (2 * pt + tsb + 1))
                    ctx.inc(sy.dma_start(
                        out=x2_dram[PTC * pt + 128 * tsb:PTC * pt + 128 * (tsb + 1), :],
                        in_=pr_x2t), "s_x2w", 16)
        blk.sync(f_sync)

        def f_gp(g):
            for pt in range(NPT):
                for oc in range(16):
                    idx = 16 * pt + oc
                    if idx >= 2:
                        g.wait_ge(ctx.sem("t_pj"), b_tpj + idx - 1)
                    ctx.inc(g.dma_start(out=wp_buf[oc % 2], in_=w_proj_d[oc]), "s_wp", 16)
        blk.gpsimd(f_gp)

        def f_ten(t):
            for pt in range(NPT):
                for oc in range(16):
                    idx = 16 * pt + oc
                    t.wait_ge(ctx.sem("s_wp"), b_swp + 16 * (idx + 1))
                    if idx >= 2:
                        t.wait_ge(ctx.sem("a_pj"), b_apj + 3 * (idx - 1))
                    mm = None
                    for fc in range(16):
                        mm = t.matmul(pb[oc % 2][:, 0:PTC],
                                      wp_buf[oc % 2][:, 128 * fc:128 * (fc + 1)],
                                      bufA[:, 2048 * fc + PTC * pt:2048 * fc + PTC * (pt + 1)],
                                      start=(fc == 0), stop=(fc == 15))
                    ctx.inc(mm, "t_pj")
                    t.wait_ge(ctx.sem("a_pj"), b_apj + 3 * (idx + 1))
                    if pt >= 1 and oc == 0:
                        t.wait_ge(ctx.sem("d_prw"), b_dprw + 2 * pt)
                    ctx.inc(t.matmul(pb[2][0:1, 0:PTC], c_ones_bf[:], pr_bt[oc % 2],
                                     start=(oc == 0), stop=(oc == 15)), "t_ps")
                    ctx.inc(t.matmul(pb[3][0:1, 0:PTC], c_ones_bf[:], pr_st[oc % 2],
                                     start=(oc == 0), stop=(oc == 15)), "t_ps")
                t.wait_ge(ctx.sem("d_prr"), b_dprr + pt + 1)
                if pt >= 1:
                    t.wait_ge(ctx.sem("d_x2"), b_dx2 + 48 * pt)
                ctx.inc(t.matmul(pb[4][:, 0:PTC], c_ones1f[:], rows[0:1, 0:PTC],
                                 start=True, stop=True), "t_pbc")
                ctx.inc(t.matmul(pb[5][:, 0:PTC], c_ones1f[:], rows[0:1, 768:768 + PTC],
                                 start=True, stop=True), "t_pbc")
                for fc in range(16):
                    lidx = 16 * pt + fc
                    t.wait_ge(ctx.sem("a_l2"), b_al2 + 2 * (lidx + 1))
                    if pt >= 1 and fc == 0:
                        t.wait_ge(ctx.sem("d_l2r"), b_dl2r + 2 * pt)
                    ctx.inc(t.matmul(pb[2][0:1, 0:PTC], c_ones_bf[:], pr_bt[fc % 2],
                                     start=(fc == 0), stop=(fc == 15)), "t_l2s")
                    ctx.inc(t.matmul(pb[3][0:1, 0:PTC], c_ones_bf[:], pr_st[fc % 2],
                                     start=(fc == 0), stop=(fc == 15)), "t_l2s")
                t.wait_ge(ctx.sem("d_l2r2"), b_dl2r2 + pt + 1)
                t.wait_ge(ctx.sem("d_x2"), b_dx2 + 48 * (pt + 1))
                ctx.inc(t.matmul(pb[4][:, 0:PTC], c_ones1f[:], rows[0:1, 0:PTC],
                                 start=True, stop=True), "t_l2bc")
                ctx.inc(t.matmul(pb[5][:, 0:PTC], c_ones1f[:], rows[0:1, 768:768 + PTC],
                                 start=True, stop=True), "t_l2bc")
                t.wait_ge(ctx.sem("d_ln2"), b_dln2 + 16 * (pt + 1))
                for tsb in range(2):
                    for fc in range(16):
                        ti = 32 * pt + 16 * tsb + fc
                        if ti >= 2:
                            t.wait_ge(ctx.sem("a_xt"), b_axt + ti - 1)
                        ctx.inc(t.transpose(
                            pb[4 + fc % 2][:, 0:128],
                            pr_a1[:, PTC * fc + 128 * tsb:PTC * fc + 128 * (tsb + 1)],
                            c_ident_f[:]), "t_xt")
        blk.tensor(f_ten)

        def f_sca(sc):
            for pt in range(NPT):
                for oc in range(16):
                    idx = 16 * pt + oc
                    sc.wait_ge(ctx.sem("t_pj"), b_tpj + idx + 1)
                    if pt >= 1 and oc == 0:
                        sc.wait_ge(ctx.sem("t_xt"), b_txt + 32 * pt)
                    if idx >= 2:
                        sc.wait_ge(ctx.sem("t_ps"), b_tps + 2 * (idx - 1))
                    ctx.inc(sc.activation(pr_a1[:, PTC * oc:PTC * (oc + 1)],
                                          pb[oc % 2][:, 0:PTC], AF.Identity,
                                          bias=c_bproj[:, oc:oc + 1]), "a_pj")
                    e = ctx.inc(sc.activation(pr_bt[oc % 2],
                                              pb[oc % 2][:, 0:PTC], AF.Identity,
                                              bias=c_bproj[:, oc:oc + 1]), "a_pj")
                    sc.wait_ge(ctx.sem("a_pj"), e)
                    ctx.inc(sc.activation(pr_st[oc % 2], pr_bt[oc % 2],
                                          AF.Square), "a_pj")
                sc.wait_ge(ctx.sem("d_prw"), b_dprw + 2 * (pt + 1))
                if pt >= 1:
                    sc.wait_ge(ctx.sem("d_l2r2"), b_dl2r2 + pt)
                ctx.inc(sc.activation(rows[0:1, 512:768], rows[0:1, 0:256], AF.Square),
                        "a_pm")
                sc.wait_ge(ctx.sem("d_pv"), b_dpv + pt + 1)
                ctx.inc(sc.activation(rows[0:1, 512:768], rows[0:1, 768:1024], AF.Sqrt,
                                      bias=c_eps[0:1, :]), "a_psq")
                sc.wait_ge(ctx.sem("d_x2"), b_dx2 + 48 * (pt + 1))
                for fc in range(16):
                    lidx = 16 * pt + fc
                    if lidx >= 2:
                        sc.wait_ge(ctx.sem("t_l2s"), b_tl2s + 2 * (lidx - 1))
                    e = ctx.inc(sc.activation(pr_bt[fc % 2],
                                              pr_a1[:, PTC * fc:PTC * (fc + 1)],
                                              AF.Copy), "a_l2")
                    sc.wait_ge(ctx.sem("a_l2"), e)
                    ctx.inc(sc.activation(pr_st[fc % 2], pr_bt[fc % 2], AF.Square),
                            "a_l2")
                sc.wait_ge(ctx.sem("d_l2r"), b_dl2r + 2 * (pt + 1))
                sc.wait_ge(ctx.sem("d_prr"), b_dprr + pt + 1)
                ctx.inc(sc.activation(rows[0:1, 512:768], rows[0:1, 0:256], AF.Square),
                        "a_l2m")
                sc.wait_ge(ctx.sem("d_l2v"), b_dl2v + pt + 1)
                ctx.inc(sc.activation(rows[0:1, 512:768], rows[0:1, 768:1024], AF.Sqrt,
                                      bias=c_eps[0:1, :]), "a_l2q")
                for tsb in range(2):
                    for fc in range(16):
                        ti = 32 * pt + 16 * tsb + fc
                        sc.wait_ge(ctx.sem("t_xt"), b_txt + ti + 1)
                        ctx.inc(sc.activation(pr_x2t[:, 128 * fc:128 * (fc + 1)],
                                              pb[4 + fc % 2][:, 0:128],
                                              AF.Copy), "a_xt")
                    if not (pt == NPT - 1 and tsb == 1):
                        sc.wait_ge(ctx.sem("s_x2w"), b_sx2w + 16 * (2 * pt + tsb + 1))
        blk.scalar(f_sca)

        def f_vec(v):
            for pt in range(NPT):
                v.wait_ge(ctx.sem("t_ps"), b_tps + 2 * 16 * (pt + 1))
                if pt >= 1:
                    v.wait_ge(ctx.sem("t_l2bc"), b_tl2bc + 2 * pt)
                ctx.inc(v.tensor_scalar_mul(rows[0:1, 0:PTC], pb[2][0:1, 0:PTC], RH), "d_prw")
                ctx.inc(v.tensor_scalar_mul(rows[0:1, 256:256 + PTC], pb[3][0:1, 0:PTC], RH), "d_prw")
                v.wait_ge(ctx.sem("a_pm"), b_apm + pt + 1)
                ctx.inc(v.tensor_sub(rows[0:1, 768:768 + PTC], rows[0:1, 256:256 + PTC],
                                     rows[0:1, 512:512 + PTC]), "d_pv")
                v.wait_ge(ctx.sem("a_psq"), b_apsq + pt + 1)
                ctx.inc(v.reciprocal(rows[0:1, 768:768 + PTC], rows[0:1, 512:512 + PTC]), "d_prr")
                v.wait_ge(ctx.sem("t_pbc"), b_tpbc + 2 * (pt + 1))
                v.wait_ge(ctx.sem("s_xr"), b_sxr + 16 * (pt + 1))
                for fc in range(16):
                    tm = pr_tmp[fc % 2]
                    a1 = pr_a1[:, PTC * fc:PTC * (fc + 1)]
                    e1 = ctx.inc(v.tensor_sub(tm, a1, pb[4][:, 0:PTC]), "dse")
                    v.wait_ge(ctx.sem("dse"), e1)
                    e2 = ctx.inc(v.tensor_mul(tm, tm, pb[5][:, 0:PTC]), "dse")
                    v.wait_ge(ctx.sem("dse"), e2)
                    ctx.inc(v.tensor_add(a1, tm, pr_xr[:, PTC * fc:PTC * (fc + 1)]),
                            "d_x2", 3)
                v.wait_ge(ctx.sem("t_l2s"), b_tl2s + 2 * 16 * (pt + 1))
                ctx.inc(v.tensor_scalar_mul(rows[0:1, 0:PTC], pb[2][0:1, 0:PTC], RH), "d_l2r")
                ctx.inc(v.tensor_scalar_mul(rows[0:1, 256:256 + PTC], pb[3][0:1, 0:PTC], RH), "d_l2r")
                v.wait_ge(ctx.sem("a_l2m"), b_al2m + pt + 1)
                ctx.inc(v.tensor_sub(rows[0:1, 768:768 + PTC], rows[0:1, 256:256 + PTC],
                                     rows[0:1, 512:512 + PTC]), "d_l2v")
                v.wait_ge(ctx.sem("a_l2q"), b_al2q + pt + 1)
                ctx.inc(v.reciprocal(rows[0:1, 768:768 + PTC], rows[0:1, 512:512 + PTC]), "d_l2r2")
                v.wait_ge(ctx.sem("t_l2bc"), b_tl2bc + 2 * (pt + 1))
                for fc in range(16):
                    tm = pr_tmp[fc % 2]
                    a1 = pr_a1[:, PTC * fc:PTC * (fc + 1)]
                    e1 = ctx.inc(v.tensor_sub(tm, a1, pb[4][:, 0:PTC]), "dse")
                    v.wait_ge(ctx.sem("dse"), e1)
                    ctx.inc(v.tensor_mul(
                        bufB[:, 2048 * fc + PTC * pt:2048 * fc + PTC * (pt + 1)],
                        tm, pb[5][:, 0:PTC]), "d_ln2")
        blk.vector(f_vec)

    def ph_router():
        b_trt = V("t_rt")
        b_drt = V("d_rt")
        b_art = V("a_rt")

        def f_ten(t):
            for ts in range(16):
                if ts >= 1:
                    t.wait_ge(ctx.sem("a_rt"), b_art + 2 * ts)
                mm = None
                for fc in range(16):
                    mm = t.matmul(pb7[:, 0:17],
                                  bufB[:, 2048 * fc + 128 * ts:2048 * fc + 128 * (ts + 1)],
                                  c_wrt[:, 17 * fc:17 * (fc + 1)],
                                  start=(fc == 0), stop=(fc == 15))
                ctx.inc(mm, "t_rt")
        blk.tensor(f_ten)

        def f_vec(v):
            for ts in range(16):
                v.wait_ge(ctx.sem("t_rt"), b_trt + ts + 1)
                e = ctx.inc(v.reduce_max(sig[:, 0:1], pb7[:, 0:16], axis=AX.X), "dse")
                v.wait_ge(ctx.sem("dse"), e)
                ctx.inc(v.tensor_scalar_mul(sig[:, 1:2], sig[:, 0:1], -1.0), "d_rt1")
                v.wait_ge(ctx.sem("a_rt1"), ts + 1)
                e = ctx.inc(v.reciprocal(sig[:, 3:4], sig[:, 2:3]), "dse")
                v.wait_ge(ctx.sem("dse"), e)
                e = ctx.inc(v.tensor_scalar_mul(rt_pr, rt_ex, sig[:, 3:4]), "dse")
                v.wait_ge(ctx.sem("dse"), e)
                e = ctx.inc(v.tensor_copy(rt_wk, rt_pr), "dse")
                v.wait_ge(ctx.sem("dse"), e)
                for k in range(3):
                    e = ctx.inc(v.reduce_max(sig[:, 4:5], rt_wk, axis=AX.X), "dse")
                    v.wait_ge(ctx.sem("dse"), e)
                    e = ctx.inc(v.tensor_scalar(out=rt_mk, in0=rt_wk,
                                                scalar1=sig[:, 4:5], scalar2=BIGNEG,
                                                op0=ALU.is_ge, op1=ALU.mult), "dse")
                    v.wait_ge(ctx.sem("dse"), e)
                    e = ctx.inc(v.tensor_add(rt_wk, rt_wk, rt_mk), "dse")
                    v.wait_ge(ctx.sem("dse"), e)
                e = ctx.inc(v.reduce_max(sig[:, 4:5], rt_wk, axis=AX.X), "dse")
                v.wait_ge(ctx.sem("dse"), e)
                e = ctx.inc(v.tensor_scalar(out=rt_mk, in0=rt_pr,
                                            scalar1=sig[:, 4:5], scalar2=None,
                                            op0=ALU.is_ge), "dse")
                v.wait_ge(ctx.sem("dse"), e)
                ctx.inc(v.tensor_mul(combine[:, 17 * ts:17 * ts + 16], rt_pr, rt_mk),
                        "d_rt")
        blk.vector(f_vec)

        def f_sca(sc):
            for ts in range(16):
                sc.wait_ge(ctx.sem("d_rt1"), ts + 1)
                ctx.inc(sc.activation(rt_ex, pb7[:, 0:16], AF.Exp,
                                      bias=sig[:, 1:2], accum_out=sig[:, 2:3]), "a_rt1")
                ctx.inc(sc.activation(combine[:, 17 * ts + 16:17 * ts + 17],
                                      pb7[:, 16:17], AF.Sigmoid), "a_rt", 2)
        blk.scalar(f_sca)

    def ph_ffn(hf):
        sfx = ""
        b_sfw = V("s_fw" + sfx)
        b_sfd = V("s_fd" + sfx)
        b_tgu = V("t_gu" + sfx)
        b_asi = V("a_si" + sfx)
        b_da = V("d_a" + sfx)
        b_ty = V("t_y" + sfx)
        b_dy = V("d_y" + sfx)
        b_dms = V("d_ms" + sfx)

        NES = len(FFN_ES)

        def srcs(pe):
            wi, ce, fam = FFN_ES[pe]
            if fam == "moe":
                return w_g_d[wi], w_u_d[wi], w_d_d[wi], ce
            return ws_g_d[wi], ws_u_d[wi], ws_d_d[wi], ce

        b_ysq0 = V("a_ysq")
        b_do0 = V("d_o")
        b_so0 = V("s_out")

        def f_vec0(v):
            v.wait_ge(ctx.sem("a_ysq"), b_ysq0)
            if FFN_MEMSET:
                ctx.inc(v.memset(y_acc[:, 0:8192], 0.0), "d_ms" + sfx)
                ctx.inc(v.memset(y_acc[:, 8192:16384], 0.0), "d_ms" + sfx)
            else:
                ctx.inc(v.engine_nop(), "d_ms" + sfx)
                ctx.inc(v.engine_nop(), "d_ms" + sfx)
        blk.vector(f_vec0)

        def f_gp(g):
            g.wait_ge(ctx.sem("d_ms" + sfx), b_dms + 2)
            for pe in range(NES):
                gsrc, usrc, dsrc, ce = srcs(pe)
                for gg in range(2):
                    for ic in range(11):
                        icg = (pe * 2 + gg) * 11 + ic
                        if icg >= 2:
                            g.wait_ge(ctx.sem("d_a" + sfx), b_da + icg - 1)
                        elif hf == 1:
                            g.wait_ge(ctx.sem("d_a" + sfx), b_da)
                        ctx.inc(g.dma_start(out=wg_buf[icg % 2], in_=gsrc[ic]),
                                "s_fw" + sfx, 16)
                        ctx.inc(g.dma_start(out=wu_buf[icg % 2], in_=usrc[ic]),
                                "s_fw" + sfx, 16)
                    for hgc in range(8):
                        hgg = (pe * 2 + gg) * 8 + hgc
                        if hgg >= 3:
                            g.wait_ge(ctx.sem("t_y" + sfx), b_ty + 4 * (hgg - 2))
                        elif hf == 1 and hgg == 0:
                            g.wait_ge(ctx.sem("t_y" + sfx), b_ty)
                        ctx.inc(g.dma_start(out=wd_buf[hgc % 3], in_=dsrc[hgc]),
                                "s_fd" + sfx, 16)
        blk.gpsimd(f_gp)

        def f_ten(t):
            for pe in range(NES):
                for gg in range(2):
                    rhs0 = 1024 * hf + 512 * gg
                    for ic in range(11):
                        icg = (pe * 2 + gg) * 11 + ic
                        t.wait_ge(ctx.sem("s_fw" + sfx), b_sfw + 32 * (icg + 1))
                        if SERIAL_FFN and icg >= 1:
                            t.wait_ge(ctx.sem("a_si" + sfx), b_asi + icg)
                            t.wait_ge(ctx.sem("d_a" + sfx), b_da + icg)
                        elif icg >= 2:
                            t.wait_ge(ctx.sem("a_si" + sfx), b_asi + icg - 1)
                            t.wait_ge(ctx.sem("d_a" + sfx), b_da + icg - 1)
                        elif hf == 1:
                            t.wait_ge(ctx.sem("a_si" + sfx), b_asi)
                            t.wait_ge(ctx.sem("d_a" + sfx), b_da)
                        for which, buf in ((0, wg_buf), (1, wu_buf)):
                            mm = None
                            for fc in range(16):
                                mm = t.matmul(
                                    pb[which * 2 + icg % 2][:, 0:512],
                                    buf[icg % 2][:, 128 * fc:128 * (fc + 1)],
                                    bufB[:, 2048 * fc + rhs0:2048 * fc + rhs0 + 512],
                                    start=(fc == 0), stop=(fc == 15))
                            ctx.inc(mm, "t_gu" + sfx)
                    for hgc in range(8):
                        hgg = (pe * 2 + gg) * 8 + hgc
                        t.wait_ge(ctx.sem("s_fd" + sfx), b_sfd + 16 * (hgg + 1))
                        t.wait_ge(ctx.sem("d_a" + sfx), b_da + (pe * 2 + gg) * 11 + 11)
                        for tsub in range(4):
                            yt = 4 * hgg + tsub
                            if SERIAL_FFN and yt >= 1:
                                t.wait_ge(ctx.sem("d_y" + sfx), b_dy + yt)
                            elif yt >= 2:
                                t.wait_ge(ctx.sem("d_y" + sfx), b_dy + yt - 1)
                            elif hf == 1:
                                t.wait_ge(ctx.sem("d_y" + sfx), b_dy)
                            mm = None
                            for ic in range(11):
                                mm = t.matmul(
                                    pb[4 + yt % 2][:, 0:256],
                                    ff_a[:, 512 * ic + 128 * tsub:512 * ic + 128 * (tsub + 1)],
                                    wd_buf[hgc % 3][:, 256 * ic:256 * (ic + 1)],
                                    start=(ic == 0), stop=(ic == 10))
                            ctx.inc(mm, "t_y" + sfx)
        blk.tensor(f_ten)

        def f_sca(sc):
            for pe in range(NES):
                for gg in range(2):
                    for ic in range(11):
                        icg = (pe * 2 + gg) * 11 + ic
                        sc.wait_ge(ctx.sem("t_gu" + sfx), b_tgu + 2 * icg + 1)
                        if icg >= 2:
                            sc.wait_ge(ctx.sem("d_a" + sfx), b_da + icg - 1)
                        elif hf == 1:
                            sc.wait_ge(ctx.sem("d_o"), b_do0)
                        ctx.inc(sc.activation(ff_sg[icg % 2], pb[icg % 2][:, 0:512],
                                              AF.Silu), "a_si" + sfx)
        blk.scalar(f_sca)

        def f_vec(v):
            for pe in range(NES):
                _, ce, _ = FFN_ES[pe]
                for gg in range(2):
                    for ic in range(11):
                        icg = (pe * 2 + gg) * 11 + ic
                        v.wait_ge(ctx.sem("a_si" + sfx), b_asi + icg + 1)
                        v.wait_ge(ctx.sem("t_gu" + sfx), b_tgu + 2 * (icg + 1))
                        if hf == 1 and icg == 0:
                            v.wait_ge(ctx.sem("s_out"), b_so0)
                        if pe * 2 + gg >= 1:
                            v.wait_ge(ctx.sem("t_y" + sfx), b_ty + 32 * (pe * 2 + gg))
                        ctx.inc(v.tensor_mul(ff_a[:, 512 * ic:512 * (ic + 1)],
                                             pb[2 + icg % 2][:, 0:512],
                                             ff_sg[icg % 2]), "d_a" + sfx)
                    for hgc in range(8):
                        hgg = (pe * 2 + gg) * 8 + hgc
                        for tsub in range(4):
                            yt = 4 * hgg + tsub
                            v.wait_ge(ctx.sem("t_y" + sfx), b_ty + yt + 1)
                            gts = 4 * gg + tsub
                            ysl = y_acc[:, 2048 * gts + 256 * hgc:2048 * gts + 256 * (hgc + 1)]
                            ccol = combine[:, 17 * (8 * hf + gts) + ce:17 * (8 * hf + gts) + ce + 1]
                            n = ctx.cnt["d_y" + sfx] + 1 - b_dy
                            if n > 64:
                                v.wait_ge(ctx.sem("d_y" + sfx), b_dy + n - 64)
                            ctx.inc(v.scalar_tensor_tensor(
                                out=ysl, in0=pb[4 + yt % 2][:, 0:256],
                                scalar=ccol, in1=ysl,
                                op0=ALU.mult, op1=ALU.add), "d_y" + sfx)
        blk.vector(f_vec)

    def ph_out(hf):
        sfx = ""
        b_do = V("d_o" + sfx)
        b_dmu = V("d_mu" + sfx)
        b_ay = V("a_ysq" + sfx)
        b_am2 = V("a_m2" + sfx)
        b_dt2 = V("d_t2" + sfx)
        b_dvr = V("d_vr" + sfx)
        b_asr = V("a_sr" + sfx)
        b_sxr2 = V("s_xr2" + sfx)
        b_sout = V("s_out")

        def f_sync(sy):
            sy.wait_ge(ctx.sem("d_y" + sfx), V("d_y" + sfx))
            for gts in range(8):
                if gts >= 1:
                    sy.wait_ge(ctx.sem("d_o" + sfx), b_do + gts)
                ctx.inc(sy.dma_start(
                    out=out_x2r,
                    in_=x2_dram[1024 * hf + 128 * gts:1024 * hf + 128 * (gts + 1), :]),
                    "s_xr2" + sfx, 16)
                sy.wait_ge(ctx.sem("d_o" + sfx), b_do + gts + 1)
                ctx.inc(sy.dma_start(
                    out=out_d[1024 * hf + 128 * gts:1024 * hf + 128 * (gts + 1), :],
                    in_=out_ot), "s_out", 16)
        blk.sync(f_sync)

        def f_vec(v):
            v.wait_ge(ctx.sem("t_y" + sfx), V("t_y" + sfx))
            for gts in range(8):
                ysl = y_acc[:, 2048 * gts:2048 * (gts + 1)]
                e = ctx.inc(v.reduce_sum(sig[:, 0:1], ysl, axis=AX.X), "dse")
                v.wait_ge(ctx.sem("dse"), e)
                if gts >= 1:
                    v.wait_ge(ctx.sem("a_m2" + sfx), b_am2 + gts)
                ctx.inc(v.tensor_scalar_mul(sig[:, 2:3], sig[:, 0:1], RH),
                        "d_mu" + sfx)
                v.wait_ge(ctx.sem("a_ysq" + sfx), b_ay + gts + 1)
                e = ctx.inc(v.tensor_scalar_mul(sig[:, 3:4], sig[:, 1:2], RH),
                            "d_t2" + sfx)
                v.wait_ge(ctx.sem("d_t2" + sfx), e)
                v.wait_ge(ctx.sem("a_m2" + sfx), b_am2 + gts + 1)
                e = ctx.inc(v.tensor_sub(sig[:, 5:6], sig[:, 3:4], sig[:, 4:5]),
                            "d_vr" + sfx)
                v.wait_ge(ctx.sem("d_vr" + sfx), b_dvr + gts + 1)
                v.wait_ge(ctx.sem("a_sr" + sfx), b_asr + gts + 1)
                e = ctx.inc(v.reciprocal(sig[:, 7:8], sig[:, 6:7]), "dse")
                v.wait_ge(ctx.sem("dse"), e)
                e = ctx.inc(v.tensor_scalar(out=out_t1, in0=ysl,
                                            scalar1=sig[:, 2:3], scalar2=sig[:, 7:8],
                                            op0=ALU.subtract, op1=ALU.mult), "dse")
                v.wait_ge(ctx.sem("dse"), e)
                v.wait_ge(ctx.sem("s_xr2" + sfx), b_sxr2 + 16 * (gts + 1))
                if gts >= 1:
                    v.wait_ge(ctx.sem("s_out"), b_sout + 16 * gts)
                ctx.inc(v.tensor_add(out_ot, out_t1, out_x2r), "d_o" + sfx)
        blk.vector(f_vec)

        def f_sca(sc):
            for gts in range(8):
                ysl = y_acc[:, 2048 * gts:2048 * (gts + 1)]
                sc.wait_ge(ctx.sem("t_y" + sfx), V("t_y" + sfx))
                if gts >= 1:
                    sc.wait_ge(ctx.sem("d_t2" + sfx), b_dt2 + gts)
                ctx.inc(sc.activation(ff_ysq, ysl, AF.Square, accum_out=sig[:, 1:2]),
                        "a_ysq" + sfx)
                sc.wait_ge(ctx.sem("d_mu" + sfx), b_dmu + gts + 1)
                ctx.inc(sc.activation(sig[:, 4:5], sig[:, 2:3], AF.Square),
                        "a_m2" + sfx)
                sc.wait_ge(ctx.sem("d_vr" + sfx), b_dvr + gts + 1)
                ctx.inc(sc.activation(sig[:, 6:7], sig[:, 5:6], AF.Sqrt,
                                      bias=c_eps[:]), "a_sr" + sfx)
        blk.scalar(f_sca)

    def ph_dbg(names):
        if not DEBUG:
            return
        def f_sync(sy):
            for nm in names:
                src = {"dbg_k": k_sb[:], "dbg_v": v_sb[:], "dbg_q": bufB[:],
                       "dbg_lnd": bufA[:], "dbg_o": bufA[:], "dbg_ln2": bufB[:],
                       "dbg_cmb": combine[:], "dbg_y": y_acc[:, 0:16384],
                       "dbg_a": ff_a}[nm]
                ctx.inc(sy.dma_start(out=dbg[nm], in_=src), "s_dbg", 16)
            sy.wait_ge(ctx.sem("s_dbg"), ctx.cnt["s_dbg"])
        blk.sync(f_sync)

    def ph_dbg_x2():
        if not DEBUG:
            return
        def f_sync(sy):
            ctx.inc(sy.dma_start(out=dbg["dbg_x2"], in_=x2_dram), "s_dbg", 16)
            sy.wait_ge(ctx.sem("s_dbg"), ctx.cnt["s_dbg"])
        blk.sync(f_sync)

    def ph_final():
        def f_vec(v):
            v.wait_ge(ctx.sem("s_out"), ctx.cnt["s_out"])
        blk.vector(f_vec)

    ph_ld()
    ph_p0a()
    ph_barrier()
    if STAGE in ("p0", "attn", "proj", "full", "ffn0"):
        ph_p0b()
        ph_barrier()
    if STAGE == "p0":
        ph_dbg(["dbg_k", "dbg_v", "dbg_q", "dbg_lnd"])
    if STAGE in ("attn", "proj", "full", "ffn0"):
        ph_bounds()
        ph_barrier()
        ph_attn()
        ph_barrier()
    if STAGE == "attn":
        ph_dbg(["dbg_k", "dbg_v", "dbg_o"])
    if STAGE in ("proj", "full", "ffn0"):
        ph_proj()
        ph_barrier()
        ph_router()
        ph_barrier()
    if STAGE == "proj":
        ph_dbg(["dbg_ln2", "dbg_cmb"])
        ph_dbg_x2()
    if STAGE in ("full", "ffn0"):
        if FFN_ES:
            ph_ffn(0)
    if STAGE == "ffn0":
        ph_dbg(DBG_FFN_LIST)
        if not SKIP_OUT:
            ph_out(0)
            ph_barrier()
            ph_dbg_x2()
            ph_final()
        else:
            def f_fin(v):
                v.wait_ge(ctx.sem("s_dbg"), ctx.cnt["s_dbg"])
            blk.vector(f_fin)
    if STAGE in ("full",):
        ph_out(0)
        ph_ffn(1)
        ph_out(1)
        ph_final()
    blk.__exit__(None, None, None)
    return nc



def assemble(results):
    out = np.zeros((2, S, H), np.float32)
    for c in range(NCORE):
        b, r = divmod(c, 4)
        ch = chunks_of_core(r)
        o = results[c]["out"]
        for s_i, cc in enumerate(ch):
            out[b, CH * cc:CH * (cc + 1), :] = o[CH * s_i:CH * (s_i + 1), :]
    return out


def kernel(**inputs):
    in_maps = stage_inputs(inputs)
    nc = build_program()
    res = run_bass_kernel_spmd(nc, in_maps, core_ids=list(range(NCORE)))
    return assemble(res.results)


# version 06
# speedup vs baseline: 1.2468x; 1.0184x over previous
import numpy as np
import ml_dtypes
import concourse.bass as bass
import concourse.mybir as mybir
from concourse.bass_utils import run_bass_kernel_spmd

F32 = mybir.dt.float32
BF16 = mybir.dt.bfloat16
AF = mybir.ActivationFunctionType
ALU = mybir.AluOpType
AX = mybir.AxisListType

H = 2048
NH = 16
NF = 16
S = 8192
SCALE = float(128) ** -0.5
NCORE = 8
CH = 512
T_OWN = 2048
T_EXT = S + T_OWN
SLOT_PAD = [3, 7, 11, 15]
NPLAIN = sum(SLOT_PAD)
BIGNEG = -1.0e6
EPS = 1e-5
RH = 1.0 / H
E = 16

TC = 256
NTC = T_EXT // TC
PTC = 256
NPT = T_OWN // PTC

FFN_ES = [(e, e, "moe") for e in range(E)] + [(i, 16, "sh") for i in range(4)]

STAGE = "full"
DEBUG = False
SKIP_OUT = False
SERIAL_FFN = False
FFN_MEMSET = True
DBG_FFN_LIST = ["dbg_y", "dbg_a", "dbg_ln2", "dbg_cmb"]


def chunks_of_core(r):
    return [r, 7 - r, 8 + r, 15 - r]



def stage_inputs(inp):
    x = np.asarray(inp["L0_attn_x"])
    W_qkv = np.asarray(inp["L0_attn_W_qkv"])
    b_qkv = np.asarray(inp["L0_attn_b_qkv"])
    W_proj = np.asarray(inp["L0_attn_W_proj"])
    b_proj = np.asarray(inp["L0_attn_b_proj"])
    w_router = np.asarray(inp["L0_ffn_w_router"])
    w_gate = np.asarray(inp["L0_ffn_w_gate"])
    w_up = np.asarray(inp["L0_ffn_w_up"])
    w_down = np.asarray(inp["L0_ffn_w_down"])
    ws_gate = np.asarray(inp["L0_ffn_ws_gate"])
    ws_up = np.asarray(inp["L0_ffn_ws_up"])
    ws_down = np.asarray(inp["L0_ffn_ws_down"])
    w_sg = np.asarray(inp["L0_ffn_w_shared_gate"])

    C = np.ascontiguousarray
    sh = {}
    sh["w_kv"] = C(W_qkv[:, 2048:].reshape(16, 128, 256).transpose(1, 0, 2).reshape(128, 16 * 256))
    sh["w_q"] = C(W_qkv[:, :2048].reshape(16, 128, 16, 128).transpose(2, 1, 0, 3).reshape(16, 128, 16 * 128))
    sh["w_proj"] = C(W_proj.reshape(16, 128, 16, 128).transpose(2, 1, 0, 3).reshape(16, 128, 16 * 128))
    sh["b_q"] = C(b_qkv[:2048].reshape(16, 128).T)
    sh["b_k"] = C(b_qkv[2048:2176].reshape(128, 1))
    sh["b_v"] = C(b_qkv[2176:2304].reshape(128, 1))
    sh["b_proj"] = C(b_proj.reshape(16, 128).T)
    sh["w_rt"] = C(np.concatenate([w_router, w_sg], axis=1).reshape(16, 128, 17).transpose(1, 0, 2).reshape(128, 16 * 17))
    sh["w_g"] = C(w_gate.reshape(E, 16, 128, 11, 128).transpose(0, 3, 2, 1, 4).reshape(E, 11, 128, 16 * 128))
    sh["w_u"] = C(w_up.reshape(E, 16, 128, 11, 128).transpose(0, 3, 2, 1, 4).reshape(E, 11, 128, 16 * 128))
    sh["w_d"] = C(w_down.reshape(E, 11, 128, 8, 256).transpose(0, 3, 2, 1, 4).reshape(E, 8, 128, 11 * 256))
    sh["ws_g"] = C(ws_gate.reshape(16, 128, 4, 11, 128).transpose(2, 3, 1, 0, 4).reshape(4, 11, 128, 16 * 128))
    sh["ws_u"] = C(ws_up.reshape(16, 128, 4, 11, 128).transpose(2, 3, 1, 0, 4).reshape(4, 11, 128, 16 * 128))
    sh["ws_d"] = C(ws_down.reshape(4, 11, 128, 8, 256).transpose(0, 3, 2, 1, 4).reshape(4, 8, 128, 11 * 256))
    ii = np.arange(CH)
    tri_t = np.where(ii[None, :] >= ii[:, None], 0.0, BIGNEG).astype(np.float32)
    sh["tri_t"] = C(tri_t.reshape(4, 128, CH).transpose(1, 0, 2).reshape(128, 4 * CH))

    in_maps = []
    for c in range(NCORE):
        b, r = divmod(c, 4)
        ch = chunks_of_core(r)
        xb = x[b]
        own = np.concatenate([xb[CH * cc:CH * (cc + 1)] for cc in ch], axis=0)
        x_ext = np.concatenate([xb, own], axis=0).T
        x_chunks = C(x_ext.reshape(16, 128, NTC, TC).transpose(2, 1, 0, 3).reshape(NTC, 128, 16 * TC))
        m = dict(sh)
        m["x_ext"] = x_chunks
        bn = np.zeros((128, NPLAIN + 1), np.float32)
        col = 0
        for s_i in range(4):
            for j in range(SLOT_PAD[s_i]):
                if j >= ch[s_i]:
                    bn[:, col] = BIGNEG
                col += 1
        m["bigneg"] = C(bn * SCALE)
        in_maps.append(m)
    return in_maps



class Ctx:
    def __init__(self, nc):
        self.nc = nc
        self.sems = {}
        self.cnt = {}

    def sem(self, name):
        if name not in self.sems:
            self.sems[name] = self.nc.alloc_semaphore(name)
            self.cnt[name] = 0
        return self.sems[name]

    def inc(self, inst, name, n=1):
        inst.then_inc(self.sem(name), n)
        self.cnt[name] += n
        return self.cnt[name]

    def val(self, name):
        self.sem(name)
        return self.cnt[name]


def build_program():
    nc = bass.Bass("TRN2", target_bir_lowering=False)
    ctx = Ctx(nc)

    def din(name, shape, dtype=F32):
        return nc.dram_tensor(name, list(shape), dtype, kind="ExternalInput").ap()

    x_ext = din("x_ext", [NTC, 128, 16 * TC])
    w_kv_d = din("w_kv", [128, 16 * 256])
    w_q_d = din("w_q", [16, 128, 16 * 128])
    w_proj_d = din("w_proj", [16, 128, 16 * 128])
    b_q_d = din("b_q", [128, 16])
    b_k_d = din("b_k", [128, 1])
    b_v_d = din("b_v", [128, 1])
    b_proj_dd = din("b_proj", [128, 16])
    w_rt_d = din("w_rt", [128, 16 * 17])
    w_g_d = din("w_g", [E, 11, 128, 16 * 128])
    w_u_d = din("w_u", [E, 11, 128, 16 * 128])
    w_d_d = din("w_d", [E, 8, 128, 11 * 256])
    ws_g_d = din("ws_g", [4, 11, 128, 16 * 128])
    ws_u_d = din("ws_u", [4, 11, 128, 16 * 128])
    ws_d_d = din("ws_d", [4, 8, 128, 11 * 256])
    tri_t_d = din("tri_t", [128, 4 * CH])
    bigneg_d = din("bigneg", [128, NPLAIN + 1])

    out_d = nc.dram_tensor("out", [T_OWN, H], F32, kind="ExternalOutput").ap()
    dbg = {}
    if DEBUG:
        for name, shape, dt_ in [
            ("dbg_k", [128, T_EXT], BF16),
            ("dbg_v", [128, T_EXT], BF16),
            ("dbg_q", [128, NH * T_OWN], BF16),
            ("dbg_lnd", [128, NF * T_OWN], BF16),
            ("dbg_o", [128, NH * T_OWN], BF16),
            ("dbg_ln2", [128, NF * T_OWN], BF16),
            ("dbg_x2", [T_OWN, H], F32),
            ("dbg_cmb", [128, 16 * 17], F32),
            ("dbg_y", [128, NF * 1024], F32),
            ("dbg_a", [128, 11 * 512], BF16),
        ]:
            dbg[name] = nc.dram_tensor(name, shape, dt_, kind="ExternalOutput").ap()

    x2_dram = nc.dram_tensor("x2_scratch", [T_OWN, H], F32, kind="Internal").ap()

    it = nc.inline_tensor
    ident_bf_d = it(np.eye(128, dtype=ml_dtypes.bfloat16), name="ident_bf").ap()
    ident_f_d = it(np.eye(128, dtype=np.float32), name="ident_f").ap()
    ones_bf_d = it(np.ones((128, 1), dtype=ml_dtypes.bfloat16), name="ones_bf").ap()
    ones1f_d = it(np.ones((1, 128), dtype=np.float32), name="ones1f").ap()
    eps_d = it(np.full((128, 1), EPS, dtype=np.float32), name="epsc").ap()

    sb = nc.alloc_sbuf_tensor
    k_sb = sb("k_sb", [128, T_EXT], BF16)
    v_sb = sb("v_sb", [128, T_EXT], BF16)
    bufA = sb("bufA", [128, NF * T_OWN], BF16)
    bufB = sb("bufB", [128, NF * T_OWN], BF16)
    bufD = sb("bufD", [128, 8192], F32)
    c_ident_bf = sb("c_ident_bf", [128, 128], BF16)
    c_ident_f = sb("c_ident_f", [128, 128], F32)
    c_ones_bf = sb("c_ones_bf", [128, 1], BF16)
    c_ones1f = sb("c_ones1f", [1, 128], F32)
    c_eps = sb("c_eps", [128, 1], F32)
    c_bq = sb("c_bq", [128, 16], F32)
    c_bk = sb("c_bk", [128, 1], F32)
    c_bv = sb("c_bv", [128, 1], F32)
    c_bproj = sb("c_bproj", [128, 16], F32)
    c_bigneg = sb("c_bigneg", [128, NPLAIN + 1], F32)
    c_wrt = sb("c_wrt", [128, 16 * 17], BF16)
    rows = sb("rows", [1, 1024], F32)
    maxk = sb("maxk", [1, 1], F32)
    combine = sb("combine", [128, 16 * 17], F32)
    sig = sb("sig", [128, 8], F32)

    bufA_f = bufA[:].bitcast(F32)
    bufB_f = bufB[:].bitcast(F32)
    bufD_bf = bufD[:].bitcast(BF16)
    k_bfv = k_sb[:]
    v_bfv = v_sb[:]

    pb = [nc.alloc_psum_tensor(f"pb{i}", [128, 512], F32) for i in range(6)]
    pb_bf = nc.alloc_psum_tensor("pb_bf", [128, 1024], BF16)
    pb7 = nc.alloc_psum_tensor("pb7", [128, 512], F32)

    xbf_t = [bufB[:, 0:4096], bufB[:, 20480:24576]]
    sq_t = [bufB[:, 4096:8192], bufB[:, 24576:28672]]
    ln1c_t = [bufB[:, 8192:12288], bufB[:, 28672:32768]]
    vfm_t = [bufB[:, 12288:12544], bufB[:, 13824:14080]]
    wkv_sb = bufB[:, 16384:16384 + 16 * 256]
    tmp0 = bufB_f[:, 6400:6656]
    tmp1 = bufB_f[:, 6656:6912]
    rows2 = bufB_f[0:1, 7040:8064]

    def xstage(i):
        o = (i % 2) * (NF * TC)
        return bufD[:, o:o + NF * TC]

    att_stt = [bufD[:, 0:512], bufD[:, 512:1024]]
    att_bb = bufD[:, 1024:1536]
    att_rs = bufD[:, 1536:2048]
    att_pt = [bufD_bf[:, 2048:2560], bufD_bf[:, 2560:3072],
              bufD_bf[:, 3072:3584], bufD_bf[:, 3584:4096]]
    att_sq = bufD_bf[:, 10240:10240 + 2048]
    att_tri = bufD[:, 6144:8192]
    bnd_row = bufD[0:1, 2560:4608]

    pr_xr = bufD[:, 0:NF * PTC]
    pr_tmp = [bufD[:, NF * PTC:NF * PTC + 256], bufD[:, NF * PTC + 256:NF * PTC + 512]]
    pr_x2t = bufD[:, NF * PTC + 512:NF * PTC + 512 + 2048]
    v_f32 = v_sb[:].bitcast(F32)
    pr_a1 = v_f32[:, 0:NF * PTC]
    wp_buf = [k_bfv[:, 0:2048], k_bfv[:, 2048:4096]]

    rt_ex = bufD[:, 0:16]
    rt_pr = bufD[:, 16:32]
    rt_wk = bufD[:, 32:48]
    rt_mk = bufD[:, 48:64]

    ff_a = bufD_bf[:, 0:11 * 512]
    ff_sg = [bufD[:, 4096:4608], bufD[:, 4608:5120]]
    wg_buf = [k_bfv[:, 0:2048], k_bfv[:, 2048:4096]]
    wu_buf = [k_bfv[:, 4096:6144], k_bfv[:, 6144:8192]]
    wd_buf = [v_bfv[:, 0:2816], v_bfv[:, 2816:5632], v_bfv[:, 5632:8448]]
    ff_ysq = k_bfv[:, 8192:10240]
    out_x2r = bufD[:, 5632:5632 + 2048]
    out_x2r = bufD[:, 6144 - 2048 + 2048:8192]
    out_x2r = bufD[:, 6144:8192]
    out_t1 = bufD[:, 4096 - 2048 + 0:4096]
    out_t1 = bufD[:, 2048 + 1024:2048 + 1024 + 2048]
    out_ot = bufD[:, 0:2048]
    y_acc = bufA_f
    ln2 = bufB

    blk = nc.Block()
    blk.__enter__()

    V = ctx.val

    def ph_ld():
        def f_sync(sy):
            for dst, src in [
                (c_ident_bf[:], ident_bf_d), (c_ident_f[:], ident_f_d),
                (c_ones_bf[:], ones_bf_d), (c_ones1f[:], ones1f_d),
                (c_eps[:], eps_d), (c_bq[:], b_q_d), (c_bk[:], b_k_d),
                (c_bv[:], b_v_d), (c_bproj[:], b_proj_dd),
                (c_bigneg[:], bigneg_d),
            ]:
                ctx.inc(sy.dma_start(out=dst, in_=src), "ld", 16)
        blk.sync(f_sync)

        def f_gp(g):
            ctx.inc(g.dma_start(out=wkv_sb, in_=w_kv_d), "ld", 16)
            ctx.inc(g.dma_start(out=c_wrt[:], in_=w_rt_d), "ld", 16)
        blk.gpsimd(f_gp)

    def ph_p0a():
        ld_done = V("ld")

        def rset(i):
            return rows[0:1, :] if i % 2 == 0 else rows2

        def f_sync(sy):
            sy.wait_ge(ctx.sem("ld"), ld_done)
            for i in range(NTC):
                if i >= 2:
                    sy.wait_ge(ctx.sem("d_norm"), i - 1)
                ctx.inc(sy.dma_start(out=xstage(i), in_=x_ext[i]), "s_x", 16)
        blk.sync(f_sync)

        def f_sca(sc):
            sc.wait_ge(ctx.sem("ld"), ld_done)
            for i in range(NTC + 1):
                if i < NTC:
                    p = i % 2
                    xs = xstage(i)
                    sc.wait_ge(ctx.sem("s_x"), 16 * (i + 1))
                    if i >= 2:
                        sc.wait_ge(ctx.sem("t_stats"), 2 * (i - 1))
                    ctx.inc(sc.activation(xbf_t[p], xs, AF.Copy), "a_cast")
                    ctx.inc(sc.activation(sq_t[p], xs, AF.Square), "a_sq")
                if i == 0:
                    continue
                j = i - 1
                pj = j % 2
                r = rset(j)
                sc.wait_ge(ctx.sem("d_rows"), 2 * (j + 1))
                if j >= 2:
                    sc.wait_ge(ctx.sem("d_rr"), j - 1)
                ctx.inc(sc.activation(r[0:1, 512:768], r[0:1, 0:256],
                                      AF.Square), "a_mu2")
                sc.wait_ge(ctx.sem("d_var"), j + 1)
                ctx.inc(sc.activation(r[0:1, 512:768], r[0:1, 768:1024],
                                      AF.Sqrt, bias=c_eps[0:1, :]), "a_sqrt")
                sc.wait_ge(ctx.sem("t_kv"), 2 * (j + 1))
                if j >= 2:
                    sc.wait_ge(ctx.sem("t_vt"), 2 * (j - 1))
                ctx.inc(sc.activation(k_sb[:, TC * j:TC * (j + 1)], pb[0][:, 0:TC],
                                      AF.Identity, bias=c_bk[:]), "a_k")
                ctx.inc(sc.activation(vfm_t[pj], pb[1][:, 0:TC],
                                      AF.Identity, bias=c_bv[:]), "a_vf")
        blk.scalar(f_sca)

        def f_ten(t):
            for i in range(NTC):
                p = i % 2
                r = rset(i)
                t.wait_ge(ctx.sem("a_cast"), i + 1)
                t.wait_ge(ctx.sem("a_sq"), i + 1)
                if i >= 1:
                    t.wait_ge(ctx.sem("d_rows"), 2 * i)
                mm = None
                for fc in range(16):
                    mm = t.matmul(pb[2][0:1, 0:TC], c_ones_bf[:],
                                  xbf_t[p][:, TC * fc:TC * (fc + 1)],
                                  start=(fc == 0), stop=(fc == 15))
                ctx.inc(mm, "t_stats")
                for fc in range(16):
                    mm = t.matmul(pb[3][0:1, 0:TC], c_ones_bf[:],
                                  sq_t[p][:, TC * fc:TC * (fc + 1)],
                                  start=(fc == 0), stop=(fc == 15))
                ctx.inc(mm, "t_stats")
                t.wait_ge(ctx.sem("d_rr"), i + 1)
                if i >= 1:
                    t.wait_ge(ctx.sem("d_norm"), i)
                ctx.inc(t.matmul(pb[4][:, 0:TC], c_ones1f[:], r[0:1, 0:TC],
                                 start=True, stop=True), "t_bc")
                ctx.inc(t.matmul(pb[5][:, 0:TC], c_ones1f[:], r[0:1, 768:768 + TC],
                                 start=True, stop=True), "t_bc")
                t.wait_ge(ctx.sem("d_norm"), i + 1)
                if i >= 1:
                    t.wait_ge(ctx.sem("a_k"), i)
                    t.wait_ge(ctx.sem("a_vf"), i)
                for which, pbi in ((0, pb[0]), (1, pb[1])):
                    mm = None
                    for fc in range(16):
                        if i < 32:
                            rhs = ln1c_t[p][:, TC * fc:TC * (fc + 1)]
                        else:
                            rhs = bufA[:, 2048 * fc + TC * (i - 32):2048 * fc + TC * (i - 32) + TC]
                        lhsT = wkv_sb[:, 256 * fc + 128 * which:256 * fc + 128 * which + 128]
                        mm = t.matmul(pbi[:, 0:TC], lhsT, rhs,
                                      start=(fc == 0), stop=(fc == 15))
                    ctx.inc(mm, "t_kv")
                t.wait_ge(ctx.sem("a_vf"), i + 1)
                if i >= 1:
                    t.wait_ge(ctx.sem("d_vt"), 2 * i)
                ctx.inc(t.transpose(pb_bf[:, 0:128], vfm_t[p][:, 0:128], c_ident_bf[:]), "t_vt")
                ctx.inc(t.transpose(pb_bf[:, 128:256], vfm_t[p][:, 128:256], c_ident_bf[:]), "t_vt")
        blk.tensor(f_ten)

        def f_vec(v):
            for i in range(NTC):
                p = i % 2
                r = rset(i)
                v.wait_ge(ctx.sem("t_stats"), 2 * (i + 1))
                if i >= 2:
                    v.wait_ge(ctx.sem("t_bc"), 2 * (i - 1))
                ctx.inc(v.tensor_scalar_mul(r[0:1, 0:TC], pb[2][0:1, 0:TC], RH), "d_rows")
                ctx.inc(v.tensor_scalar_mul(r[0:1, 256:256 + TC], pb[3][0:1, 0:TC], RH), "d_rows")
                v.wait_ge(ctx.sem("a_mu2"), i + 1)
                ctx.inc(v.tensor_sub(r[0:1, 768:768 + TC], r[0:1, 256:256 + TC],
                                     r[0:1, 512:512 + TC]), "d_var")
                v.wait_ge(ctx.sem("a_sqrt"), i + 1)
                ctx.inc(v.reciprocal(r[0:1, 768:768 + TC], r[0:1, 512:512 + TC]), "d_rr")
                v.wait_ge(ctx.sem("t_bc"), 2 * (i + 1))
                if i >= 2:
                    v.wait_ge(ctx.sem("t_kv"), 2 * (i - 1))
                xs = xstage(i)
                for fc in range(16):
                    tm = tmp0 if fc % 2 == 0 else tmp1
                    sub = v.tensor_sub(tm, xs[:, TC * fc:TC * (fc + 1)], pb[4][:, 0:TC])
                    se = ctx.inc(sub, "dse")
                    v.wait_ge(ctx.sem("dse"), se)
                    if i < 32:
                        dst = ln1c_t[p][:, TC * fc:TC * (fc + 1)]
                    else:
                        dst = bufA[:, 2048 * fc + TC * (i - 32):2048 * fc + TC * (i - 32) + TC]
                    mul = v.tensor_mul(dst, tm, pb[5][:, 0:TC])
                    if fc == 15:
                        ctx.inc(mul, "d_norm")
                v.wait_ge(ctx.sem("t_vt"), 2 * (i + 1))
                ctx.inc(v.tensor_copy(v_sb[:, TC * i:TC * (i + 1)], pb_bf[:, 0:256]),
                        "d_vt", 2)
        blk.vector(f_vec)

    def ph_barrier():
        tgt = ctx.val("bar") + 5
        for attr in (blk.sync, blk.gpsimd, blk.tensor, blk.vector, blk.scalar):
            def f(eng, tgt=tgt):
                ctx.inc(eng.nop(), "bar")
                eng.wait_ge(ctx.sem("bar"), tgt)
            attr(f)
        assert ctx.cnt["bar"] == tgt

    def ph_p0b():
        b_swq = V("s_wq")
        b_tq = V("t_q")
        b_aq = V("a_q")
        wqb = [bufD_bf[:, 0:2048], bufD_bf[:, 2048:4096]]

        def f_gp(g):
            for oc in range(16):
                if oc >= 2:
                    g.wait_ge(ctx.sem("t_q"), b_tq + 4 * (oc - 1))
                ctx.inc(g.dma_start(out=wqb[oc % 2], in_=w_q_d[oc]), "s_wq", 16)
        blk.gpsimd(f_gp)

        def f_ten(t):
            for oc in range(16):
                t.wait_ge(ctx.sem("s_wq"), b_swq + 16 * (oc + 1))
                for tg in range(4):
                    idx = 4 * oc + tg
                    if idx >= 2:
                        t.wait_ge(ctx.sem("a_q"), b_aq + idx - 1)
                    mm = None
                    for fc in range(16):
                        mm = t.matmul(pb[idx % 2][:, 0:512],
                                      wqb[oc % 2][:, 128 * fc:128 * (fc + 1)],
                                      bufA[:, 2048 * fc + 512 * tg:2048 * fc + 512 * (tg + 1)],
                                      start=(fc == 0), stop=(fc == 15))
                    ctx.inc(mm, "t_q")
        blk.tensor(f_ten)

        def f_sca(sc):
            for oc in range(16):
                for tg in range(4):
                    idx = 4 * oc + tg
                    sc.wait_ge(ctx.sem("t_q"), b_tq + idx + 1)
                    ctx.inc(sc.activation(
                        bufB[:, 2048 * oc + 512 * tg:2048 * oc + 512 * (tg + 1)],
                        pb[idx % 2][:, 0:512], AF.Identity,
                        bias=c_bq[:, oc:oc + 1]), "a_q")
        blk.scalar(f_sca)

    def ph_bounds():
        b_aks = V("a_ks")
        b_tkn = V("t_kn")
        b_dkm = V("d_km")
        b_aqs = V("a_qs")
        b_tqn = V("t_qn")
        b_aqb = V("a_qb")
        b_dqb = V("d_qb")

        b_ld = V("ld")

        def f_sync(sy):
            ctx.inc(sy.dma_start(out=att_tri, in_=tri_t_d), "ld", 16)
        blk.sync(f_sync)

        def f_vec0(v):
            ctx.inc(v.memset(maxk[:], 0.0), "d_km")
        blk.vector(f_vec0)

        def f_sca(sc):
            sc.wait_ge(ctx.sem("ld"), b_ld + 16)
            for cc in range(20):
                if cc >= 1:
                    sc.wait_ge(ctx.sem("t_kn"), b_tkn + cc)
                ctx.inc(sc.activation(att_sq[:, 0:512], k_sb[:, 512 * cc:512 * (cc + 1)],
                                      AF.Square), "a_ks")
        blk.scalar(f_sca)

        def f_ten(t):
            for cc in range(20):
                t.wait_ge(ctx.sem("a_ks"), b_aks + cc + 1)
                if cc >= 1:
                    t.wait_ge(ctx.sem("d_km"), b_dkm + 2 * cc)
                ctx.inc(t.matmul(pb[2][0:1, 0:512], c_ones_bf[:], att_sq[:, 0:512],
                                 start=True, stop=True), "t_kn")
        blk.tensor(f_ten)

        def f_vec(v):
            v.wait_ge(ctx.sem("d_km"), 1)
            for cc in range(20):
                v.wait_ge(ctx.sem("t_kn"), b_tkn + cc + 1)
                e1 = ctx.inc(v.reduce_max(rows[0:1, 4:5], pb[2][0:1, 0:512], axis=AX.X),
                             "d_km")
                v.wait_ge(ctx.sem("d_km"), e1)
                ctx.inc(v.tensor_max(maxk[:], maxk[:], rows[0:1, 4:5]), "d_km")
            e1 = ctx.inc(v.tensor_scalar_mul(maxk[:], maxk[:], 1.0404), "d_km")
        blk.vector(f_vec)

        def f_sca2(sc):
            sc.wait_ge(ctx.sem("d_km"), b_dkm + 42)
            ctx.inc(sc.activation(maxk[:], maxk[:], AF.Sqrt), "a_mk")
        blk.scalar(f_sca2)

    def ph_attn():
        b_tbb = V("t_bb")
        b_abb = V("a_bb")
        b_tst = V("t_st")
        b_dbt = V("d_bt")
        b_aexp = V("a_exp")
        b_tpv = V("t_pv")
        b_tsg = V("t_sg")
        b_drsg = V("d_rsg")
        b_trs = V("t_rs")
        b_ars = V("a_rs")
        b_dof = V("d_of")
        b_dtri = V("d_tri")
        b_tbn = V("t_bn")
        b_aqs = V("a_qs")
        b_abq = V("a_bq")
        b_dbn = V("d_bn")
        b_dbs = V("d_bs")

        def iters():
            for h_ in range(16):
                for s_ in range(4):
                    yield h_, s_

        def kbs(s_):
            off = sum(SLOT_PAD[:s_])
            out = [(512 * j, off + j, False) for j in range(SLOT_PAD[s_])]
            out.append((S + 512 * s_, NPLAIN, True))
            return out

        tiles_before = {}
        iters_list = list(iters())
        tb_cum = 0
        for h_, s_ in iters_list:
            tiles_before[(h_, s_)] = tb_cum
            tb_cum += 4 * (SLOT_PAD[s_] + 1)

        bias_tab = [bufD[:, 4608 + 32 * i:4608 + 32 * i + 16] for i in range(2)]
        nb_hs = [bufD[:, 4672 + 32 * i:4672 + 32 * i + 1] for i in range(2)]
        bsc = bufD[0:1, 4740:4744]

        def f_ten_real(t):
            t.wait_ge(ctx.sem("a_mk"), 1)
            for itn, (h_, s_) in enumerate(iters()):
                kl = kbs(s_)
                t0 = tiles_before[(h_, s_)]
                ntile = 4 * len(kl)
                p2 = itn % 2
                if s_ == 0:
                    t.wait_ge(ctx.sem("a_qs"), b_aqs + h_ + 1)
                    for tb in range(4):
                        if itn >= 1 or tb >= 1:
                            t.wait_ge(ctx.sem("a_bq"), b_abq + 4 * h_ + tb)
                        if tb == 0 and itn >= 1:
                            t.wait_ge(ctx.sem("a_rs"), b_ars + itn)
                        ctx.inc(t.matmul(pb7[0:1, 0:512], c_ones_bf[:],
                                         att_sq[:, 512 * tb:512 * (tb + 1)],
                                         start=True, stop=True), "t_bn")
                t.wait_ge(ctx.sem("d_bs"), b_dbs + itn + 1)
                if s_ >= 1:
                    t.wait_ge(ctx.sem("a_rs"), b_ars + itn)
                else:
                    t.wait_ge(ctx.sem("a_bq"), b_abq + 4 * (h_ + 1))
                ctx.inc(t.matmul(pb7[:, 0:1], c_ones1f[:], bsc[0:1, 0:1],
                                 start=True, stop=True), "t_bb")
                qrhs = bufB[:, 2048 * h_ + 512 * s_:2048 * h_ + 512 * s_ + 512]

                def pv_sg(ti):
                    gt = t0 + ti
                    kb, ks = divmod(ti, 4)
                    kcol, _, _ = kl[kb]
                    t.wait_ge(ctx.sem("a_exp"), b_aexp + gt + 1)
                    if ti == 0 and itn >= 2:
                        t.wait_ge(ctx.sem("d_of"), b_dof + itn - 1)
                        t.wait_ge(ctx.sem("d_rsg"), b_drsg + itn - 1)
                    ctx.inc(t.matmul(pb[2 + p2][:, 0:512],
                                     v_sb[:, kcol + 128 * ks:kcol + 128 * (ks + 1)],
                                     att_pt[gt % 4],
                                     start=(ti == 0), stop=(ti == ntile - 1)), "t_pv")
                    ctx.inc(t.matmul(pb[4 + p2][0:1, 0:512], c_ones_bf[:],
                                     att_pt[gt % 4],
                                     start=(ti == 0), stop=(ti == ntile - 1)), "t_sg")

                for ti in range(ntile):
                    kb, ks = divmod(ti, 4)
                    kcol, bcol, isdiag = kl[kb]
                    gt = t0 + ti
                    if gt >= 2:
                        t.wait_ge(ctx.sem("a_exp"), b_aexp + gt - 1)
                    ctx.inc(t.matmul(pb[gt % 2][:, 0:512],
                                     k_sb[:, kcol + 128 * ks:kcol + 128 * (ks + 1)],
                                     qrhs, start=True, stop=True), "t_st")
                    if ti >= 1:
                        pv_sg(ti - 1)
                pv_sg(ntile - 1)
                t.wait_ge(ctx.sem("d_rsg"), b_drsg + itn + 1)
                t.wait_ge(ctx.sem("a_bb"), b_abb + itn + 1)
                ctx.inc(t.matmul(pb7[:, 0:512], c_ones1f[:], rows[0:1, 0:512],
                                 start=True, stop=True), "t_rs")
        blk.tensor(f_ten_real)

        def f_vec(v):
            for itn, (h_, s_) in enumerate(iters()):
                kl = kbs(s_)
                t0 = tiles_before[(h_, s_)]
                ntile = 4 * len(kl)
                p2 = itn % 2
                if s_ == 0:
                    for tb in range(4):
                        v.wait_ge(ctx.sem("a_bq"), b_abq + 4 * h_ + tb + 1)
                        ctx.inc(v.tensor_scalar_mul(
                            bnd_row[0:1, 512 * tb:512 * (tb + 1)],
                            bnd_row[0:1, 512 * tb:512 * (tb + 1)], maxk[0:1, :]),
                            "d_bn")
                v.wait_ge(ctx.sem("d_bn"), b_dbn + 4 * h_ + s_ + 1)
                if itn >= 1:
                    v.wait_ge(ctx.sem("t_bb"), b_tbb + itn)
                e = ctx.inc(v.reduce_max(bsc[0:1, 0:1],
                                         bnd_row[0:1, 512 * s_:512 * (s_ + 1)],
                                         axis=AX.X), "dse")
                v.wait_ge(ctx.sem("dse"), e)
                ctx.inc(v.tensor_scalar_mul(bsc[0:1, 0:1], bsc[0:1, 0:1], -SCALE),
                        "d_bs")
                off = sum(SLOT_PAD[:s_])
                v.wait_ge(ctx.sem("a_bb"), b_abb + itn + 1)
                if itn >= 2:
                    prev_t0 = t0 - 4 * (SLOT_PAD[iters_list[itn - 1][1]] + 1)
                    v.wait_ge(ctx.sem("a_exp"), b_aexp + prev_t0)
                ctx.inc(v.tensor_scalar_add(bias_tab[p2][:, 0:SLOT_PAD[s_]],
                                            c_bigneg[:, off:off + SLOT_PAD[s_]],
                                            nb_hs[p2][:, 0:1]), "d_bt")
                for ti in range(ntile - 4, ntile):
                    kb, ks = divmod(ti, 4)
                    gt = t0 + ti
                    v.wait_ge(ctx.sem("t_st"), b_tst + gt + 1)
                    ctx.inc(v.tensor_add(pb[gt % 2][:, 0:512], pb[gt % 2][:, 0:512],
                                         att_tri[:, 512 * ks:512 * (ks + 1)]),
                            "d_tri")
                v.wait_ge(ctx.sem("t_sg"), b_tsg + t0 + ntile)
                if itn >= 1:
                    v.wait_ge(ctx.sem("t_rs"), b_trs + itn)
                ctx.inc(v.reciprocal(rows[0:1, 0:512], pb[4 + p2][0:1, 0:512]), "d_rsg")
                v.wait_ge(ctx.sem("a_rs"), b_ars + itn + 1)
                v.wait_ge(ctx.sem("t_pv"), b_tpv + t0 + ntile)
                ctx.inc(v.tensor_mul(
                    bufA[:, 2048 * h_ + 512 * s_:2048 * h_ + 512 * s_ + 512],
                    pb[2 + p2][:, 0:512], att_rs), "d_of")
        blk.vector(f_vec)

        def f_sca(sc):
            sc.wait_ge(ctx.sem("a_mk"), 1)
            for itn, (h_, s_) in enumerate(iters()):
                kl = kbs(s_)
                t0 = tiles_before[(h_, s_)]
                ntile = 4 * len(kl)
                p2 = itn % 2
                if s_ == 0:
                    if itn >= 1:
                        sc.wait_ge(ctx.sem("t_bn"), b_tbn + 4 * h_)
                    ctx.inc(sc.activation(att_sq[:, 0:2048],
                                          bufB[:, 2048 * h_:2048 * (h_ + 1)],
                                          AF.Square), "a_qs")
                    for tb in range(4):
                        sc.wait_ge(ctx.sem("t_bn"), b_tbn + 4 * h_ + tb + 1)
                        if itn >= 1 and tb == 0:
                            sc.wait_ge(ctx.sem("t_bb"), b_tbb + itn)
                        ctx.inc(sc.activation(bnd_row[0:1, 512 * tb:512 * (tb + 1)],
                                              pb7[0:1, 0:512], AF.Sqrt), "a_bq")
                sc.wait_ge(ctx.sem("t_bb"), b_tbb + itn + 1)
                ctx.inc(sc.activation(nb_hs[p2], pb7[:, 0:1], AF.Copy), "a_bb")
                ndiag = ntile - 4
                for ti in range(ntile):
                    kb, ks = divmod(ti, 4)
                    kcol, bcol, isdiag = kl[kb]
                    gt = t0 + ti
                    sc.wait_ge(ctx.sem("t_st"), b_tst + gt + 1)
                    sc.wait_ge(ctx.sem("d_bt"), b_dbt + itn + 1)
                    if isdiag:
                        sc.wait_ge(ctx.sem("d_tri"), b_dtri + 4 * itn + ks + 1)
                    if gt >= 4:
                        sc.wait_ge(ctx.sem("t_pv"), b_tpv + gt - 3)
                    bias = nb_hs[p2] if isdiag else bias_tab[p2][:, kb:kb + 1]
                    ctx.inc(sc.activation(att_pt[gt % 4], pb[gt % 2][:, 0:512],
                                          AF.Exp, scale=SCALE, bias=bias), "a_exp")
                sc.wait_ge(ctx.sem("t_rs"), b_trs + itn + 1)
                ctx.inc(sc.activation(att_rs, pb7[:, 0:512], AF.Copy), "a_rs")
        blk.scalar(f_sca)

    def ph_proj():
        b_sxr = V("s_xr")
        b_swp = V("s_wp")
        b_tpj = V("t_pj")
        b_apj = V("a_pj")
        b_tps = V("t_ps")
        b_dprw = V("d_prw")
        b_apm = V("a_pm")
        b_dpv = V("d_pv")
        b_apsq = V("a_psq")
        b_dprr = V("d_prr")
        b_tpbc = V("t_pbc")
        b_dx2 = V("d_x2")
        b_al2 = V("a_l2")
        b_tl2s = V("t_l2s")
        b_dl2r = V("d_l2r")
        b_al2m = V("a_l2m")
        b_dl2v = V("d_l2v")
        b_al2q = V("a_l2q")
        b_dl2r2 = V("d_l2r2")
        b_tl2bc = V("t_l2bc")
        b_dln2 = V("d_ln2")
        b_txt = V("t_xt")
        b_axt = V("a_xt")
        b_sx2w = V("s_x2w")

        pr_bt = [k_bfv[:, 4096:4352], k_bfv[:, 4352:4608]]
        pr_st = [k_bfv[:, 4608:4864], k_bfv[:, 4864:5120]]

        def f_sync(sy):
            for pt in range(NPT):
                if pt >= 1:
                    sy.wait_ge(ctx.sem("d_x2"), b_dx2 + 48 * pt)
                ctx.inc(sy.dma_start(out=pr_xr, in_=x_ext[32 + pt]), "s_xr", 16)
                for tsb in range(2):
                    sy.wait_ge(ctx.sem("a_xt"), b_axt + 16 * (2 * pt + tsb + 1))
                    ctx.inc(sy.dma_start(
                        out=x2_dram[PTC * pt + 128 * tsb:PTC * pt + 128 * (tsb + 1), :],
                        in_=pr_x2t), "s_x2w", 16)
        blk.sync(f_sync)

        def f_gp(g):
            for pt in range(NPT):
                for oc in range(16):
                    idx = 16 * pt + oc
                    if idx >= 2:
                        g.wait_ge(ctx.sem("t_pj"), b_tpj + idx - 1)
                    ctx.inc(g.dma_start(out=wp_buf[oc % 2], in_=w_proj_d[oc]), "s_wp", 16)
        blk.gpsimd(f_gp)

        def f_ten(t):
            for pt in range(NPT):
                for oc in range(16):
                    idx = 16 * pt + oc
                    t.wait_ge(ctx.sem("s_wp"), b_swp + 16 * (idx + 1))
                    if idx >= 2:
                        t.wait_ge(ctx.sem("a_pj"), b_apj + 3 * (idx - 1))
                    mm = None
                    for fc in range(16):
                        mm = t.matmul(pb[oc % 2][:, 0:PTC],
                                      wp_buf[oc % 2][:, 128 * fc:128 * (fc + 1)],
                                      bufA[:, 2048 * fc + PTC * pt:2048 * fc + PTC * (pt + 1)],
                                      start=(fc == 0), stop=(fc == 15))
                    ctx.inc(mm, "t_pj")
                    t.wait_ge(ctx.sem("a_pj"), b_apj + 3 * (idx + 1))
                    if pt >= 1 and oc == 0:
                        t.wait_ge(ctx.sem("d_prw"), b_dprw + 2 * pt)
                    ctx.inc(t.matmul(pb[2][0:1, 0:PTC], c_ones_bf[:], pr_bt[oc % 2],
                                     start=(oc == 0), stop=(oc == 15)), "t_ps")
                    ctx.inc(t.matmul(pb[3][0:1, 0:PTC], c_ones_bf[:], pr_st[oc % 2],
                                     start=(oc == 0), stop=(oc == 15)), "t_ps")
                t.wait_ge(ctx.sem("d_prr"), b_dprr + pt + 1)
                if pt >= 1:
                    t.wait_ge(ctx.sem("d_x2"), b_dx2 + 48 * pt)
                ctx.inc(t.matmul(pb[4][:, 0:PTC], c_ones1f[:], rows[0:1, 0:PTC],
                                 start=True, stop=True), "t_pbc")
                ctx.inc(t.matmul(pb[5][:, 0:PTC], c_ones1f[:], rows[0:1, 768:768 + PTC],
                                 start=True, stop=True), "t_pbc")
                for fc in range(16):
                    lidx = 16 * pt + fc
                    t.wait_ge(ctx.sem("a_l2"), b_al2 + 2 * (lidx + 1))
                    if pt >= 1 and fc == 0:
                        t.wait_ge(ctx.sem("d_l2r"), b_dl2r + 2 * pt)
                    ctx.inc(t.matmul(pb[2][0:1, 0:PTC], c_ones_bf[:], pr_bt[fc % 2],
                                     start=(fc == 0), stop=(fc == 15)), "t_l2s")
                    ctx.inc(t.matmul(pb[3][0:1, 0:PTC], c_ones_bf[:], pr_st[fc % 2],
                                     start=(fc == 0), stop=(fc == 15)), "t_l2s")
                t.wait_ge(ctx.sem("d_l2r2"), b_dl2r2 + pt + 1)
                t.wait_ge(ctx.sem("d_x2"), b_dx2 + 48 * (pt + 1))
                ctx.inc(t.matmul(pb[4][:, 0:PTC], c_ones1f[:], rows[0:1, 0:PTC],
                                 start=True, stop=True), "t_l2bc")
                ctx.inc(t.matmul(pb[5][:, 0:PTC], c_ones1f[:], rows[0:1, 768:768 + PTC],
                                 start=True, stop=True), "t_l2bc")
                t.wait_ge(ctx.sem("d_ln2"), b_dln2 + 16 * (pt + 1))
                for tsb in range(2):
                    for fc in range(16):
                        ti = 32 * pt + 16 * tsb + fc
                        if ti >= 2:
                            t.wait_ge(ctx.sem("a_xt"), b_axt + ti - 1)
                        ctx.inc(t.transpose(
                            pb[4 + fc % 2][:, 0:128],
                            pr_a1[:, PTC * fc + 128 * tsb:PTC * fc + 128 * (tsb + 1)],
                            c_ident_f[:]), "t_xt")
        blk.tensor(f_ten)

        def f_sca(sc):
            for pt in range(NPT):
                for oc in range(16):
                    idx = 16 * pt + oc
                    sc.wait_ge(ctx.sem("t_pj"), b_tpj + idx + 1)
                    if pt >= 1 and oc == 0:
                        sc.wait_ge(ctx.sem("t_xt"), b_txt + 32 * pt)
                    if idx >= 2:
                        sc.wait_ge(ctx.sem("t_ps"), b_tps + 2 * (idx - 1))
                    ctx.inc(sc.activation(pr_a1[:, PTC * oc:PTC * (oc + 1)],
                                          pb[oc % 2][:, 0:PTC], AF.Identity,
                                          bias=c_bproj[:, oc:oc + 1]), "a_pj")
                    e = ctx.inc(sc.activation(pr_bt[oc % 2],
                                              pb[oc % 2][:, 0:PTC], AF.Identity,
                                              bias=c_bproj[:, oc:oc + 1]), "a_pj")
                    sc.wait_ge(ctx.sem("a_pj"), e)
                    ctx.inc(sc.activation(pr_st[oc % 2], pr_bt[oc % 2],
                                          AF.Square), "a_pj")
                sc.wait_ge(ctx.sem("d_prw"), b_dprw + 2 * (pt + 1))
                if pt >= 1:
                    sc.wait_ge(ctx.sem("d_l2r2"), b_dl2r2 + pt)
                ctx.inc(sc.activation(rows[0:1, 512:768], rows[0:1, 0:256], AF.Square),
                        "a_pm")
                sc.wait_ge(ctx.sem("d_pv"), b_dpv + pt + 1)
                ctx.inc(sc.activation(rows[0:1, 512:768], rows[0:1, 768:1024], AF.Sqrt,
                                      bias=c_eps[0:1, :]), "a_psq")
                sc.wait_ge(ctx.sem("d_x2"), b_dx2 + 48 * (pt + 1))
                for fc in range(16):
                    lidx = 16 * pt + fc
                    if lidx >= 2:
                        sc.wait_ge(ctx.sem("t_l2s"), b_tl2s + 2 * (lidx - 1))
                    e = ctx.inc(sc.activation(pr_bt[fc % 2],
                                              pr_a1[:, PTC * fc:PTC * (fc + 1)],
                                              AF.Copy), "a_l2")
                    sc.wait_ge(ctx.sem("a_l2"), e)
                    ctx.inc(sc.activation(pr_st[fc % 2], pr_bt[fc % 2], AF.Square),
                            "a_l2")
                sc.wait_ge(ctx.sem("d_l2r"), b_dl2r + 2 * (pt + 1))
                sc.wait_ge(ctx.sem("d_prr"), b_dprr + pt + 1)
                ctx.inc(sc.activation(rows[0:1, 512:768], rows[0:1, 0:256], AF.Square),
                        "a_l2m")
                sc.wait_ge(ctx.sem("d_l2v"), b_dl2v + pt + 1)
                ctx.inc(sc.activation(rows[0:1, 512:768], rows[0:1, 768:1024], AF.Sqrt,
                                      bias=c_eps[0:1, :]), "a_l2q")
                for tsb in range(2):
                    for fc in range(16):
                        ti = 32 * pt + 16 * tsb + fc
                        sc.wait_ge(ctx.sem("t_xt"), b_txt + ti + 1)
                        ctx.inc(sc.activation(pr_x2t[:, 128 * fc:128 * (fc + 1)],
                                              pb[4 + fc % 2][:, 0:128],
                                              AF.Copy), "a_xt")
                    if not (pt == NPT - 1 and tsb == 1):
                        sc.wait_ge(ctx.sem("s_x2w"), b_sx2w + 16 * (2 * pt + tsb + 1))
        blk.scalar(f_sca)

        def f_vec(v):
            for pt in range(NPT):
                v.wait_ge(ctx.sem("t_ps"), b_tps + 2 * 16 * (pt + 1))
                if pt >= 1:
                    v.wait_ge(ctx.sem("t_l2bc"), b_tl2bc + 2 * pt)
                ctx.inc(v.tensor_scalar_mul(rows[0:1, 0:PTC], pb[2][0:1, 0:PTC], RH), "d_prw")
                ctx.inc(v.tensor_scalar_mul(rows[0:1, 256:256 + PTC], pb[3][0:1, 0:PTC], RH), "d_prw")
                v.wait_ge(ctx.sem("a_pm"), b_apm + pt + 1)
                ctx.inc(v.tensor_sub(rows[0:1, 768:768 + PTC], rows[0:1, 256:256 + PTC],
                                     rows[0:1, 512:512 + PTC]), "d_pv")
                v.wait_ge(ctx.sem("a_psq"), b_apsq + pt + 1)
                ctx.inc(v.reciprocal(rows[0:1, 768:768 + PTC], rows[0:1, 512:512 + PTC]), "d_prr")
                v.wait_ge(ctx.sem("t_pbc"), b_tpbc + 2 * (pt + 1))
                v.wait_ge(ctx.sem("s_xr"), b_sxr + 16 * (pt + 1))
                for fc in range(16):
                    tm = pr_tmp[fc % 2]
                    a1 = pr_a1[:, PTC * fc:PTC * (fc + 1)]
                    e1 = ctx.inc(v.tensor_sub(tm, a1, pb[4][:, 0:PTC]), "dse")
                    v.wait_ge(ctx.sem("dse"), e1)
                    e2 = ctx.inc(v.tensor_mul(tm, tm, pb[5][:, 0:PTC]), "dse")
                    v.wait_ge(ctx.sem("dse"), e2)
                    ctx.inc(v.tensor_add(a1, tm, pr_xr[:, PTC * fc:PTC * (fc + 1)]),
                            "d_x2", 3)
                v.wait_ge(ctx.sem("t_l2s"), b_tl2s + 2 * 16 * (pt + 1))
                ctx.inc(v.tensor_scalar_mul(rows[0:1, 0:PTC], pb[2][0:1, 0:PTC], RH), "d_l2r")
                ctx.inc(v.tensor_scalar_mul(rows[0:1, 256:256 + PTC], pb[3][0:1, 0:PTC], RH), "d_l2r")
                v.wait_ge(ctx.sem("a_l2m"), b_al2m + pt + 1)
                ctx.inc(v.tensor_sub(rows[0:1, 768:768 + PTC], rows[0:1, 256:256 + PTC],
                                     rows[0:1, 512:512 + PTC]), "d_l2v")
                v.wait_ge(ctx.sem("a_l2q"), b_al2q + pt + 1)
                ctx.inc(v.reciprocal(rows[0:1, 768:768 + PTC], rows[0:1, 512:512 + PTC]), "d_l2r2")
                v.wait_ge(ctx.sem("t_l2bc"), b_tl2bc + 2 * (pt + 1))
                for fc in range(16):
                    tm = pr_tmp[fc % 2]
                    a1 = pr_a1[:, PTC * fc:PTC * (fc + 1)]
                    e1 = ctx.inc(v.tensor_sub(tm, a1, pb[4][:, 0:PTC]), "dse")
                    v.wait_ge(ctx.sem("dse"), e1)
                    ctx.inc(v.tensor_mul(
                        bufB[:, 2048 * fc + PTC * pt:2048 * fc + PTC * (pt + 1)],
                        tm, pb[5][:, 0:PTC]), "d_ln2")
        blk.vector(f_vec)

    def ph_router():
        b_trt = V("t_rt")
        b_drt = V("d_rt")
        b_art = V("a_rt")

        def f_ten(t):
            for ts in range(16):
                if ts >= 1:
                    t.wait_ge(ctx.sem("a_rt"), b_art + 2 * ts)
                mm = None
                for fc in range(16):
                    mm = t.matmul(pb7[:, 0:17],
                                  bufB[:, 2048 * fc + 128 * ts:2048 * fc + 128 * (ts + 1)],
                                  c_wrt[:, 17 * fc:17 * (fc + 1)],
                                  start=(fc == 0), stop=(fc == 15))
                ctx.inc(mm, "t_rt")
        blk.tensor(f_ten)

        def f_vec(v):
            for ts in range(16):
                v.wait_ge(ctx.sem("t_rt"), b_trt + ts + 1)
                e = ctx.inc(v.reduce_max(sig[:, 0:1], pb7[:, 0:16], axis=AX.X), "dse")
                v.wait_ge(ctx.sem("dse"), e)
                ctx.inc(v.tensor_scalar_mul(sig[:, 1:2], sig[:, 0:1], -1.0), "d_rt1")
                v.wait_ge(ctx.sem("a_rt1"), ts + 1)
                e = ctx.inc(v.reciprocal(sig[:, 3:4], sig[:, 2:3]), "dse")
                v.wait_ge(ctx.sem("dse"), e)
                e = ctx.inc(v.tensor_scalar_mul(rt_pr, rt_ex, sig[:, 3:4]), "dse")
                v.wait_ge(ctx.sem("dse"), e)
                e = ctx.inc(v.tensor_copy(rt_wk, rt_pr), "dse")
                v.wait_ge(ctx.sem("dse"), e)
                for k in range(3):
                    e = ctx.inc(v.reduce_max(sig[:, 4:5], rt_wk, axis=AX.X), "dse")
                    v.wait_ge(ctx.sem("dse"), e)
                    e = ctx.inc(v.tensor_scalar(out=rt_mk, in0=rt_wk,
                                                scalar1=sig[:, 4:5], scalar2=BIGNEG,
                                                op0=ALU.is_ge, op1=ALU.mult), "dse")
                    v.wait_ge(ctx.sem("dse"), e)
                    e = ctx.inc(v.tensor_add(rt_wk, rt_wk, rt_mk), "dse")
                    v.wait_ge(ctx.sem("dse"), e)
                e = ctx.inc(v.reduce_max(sig[:, 4:5], rt_wk, axis=AX.X), "dse")
                v.wait_ge(ctx.sem("dse"), e)
                e = ctx.inc(v.tensor_scalar(out=rt_mk, in0=rt_pr,
                                            scalar1=sig[:, 4:5], scalar2=None,
                                            op0=ALU.is_ge), "dse")
                v.wait_ge(ctx.sem("dse"), e)
                ctx.inc(v.tensor_mul(combine[:, 17 * ts:17 * ts + 16], rt_pr, rt_mk),
                        "d_rt")
        blk.vector(f_vec)

        def f_sca(sc):
            for ts in range(16):
                sc.wait_ge(ctx.sem("d_rt1"), ts + 1)
                ctx.inc(sc.activation(rt_ex, pb7[:, 0:16], AF.Exp,
                                      bias=sig[:, 1:2], accum_out=sig[:, 2:3]), "a_rt1")
                ctx.inc(sc.activation(combine[:, 17 * ts + 16:17 * ts + 17],
                                      pb7[:, 16:17], AF.Sigmoid), "a_rt", 2)
        blk.scalar(f_sca)

    def ph_ffn(hf):
        sfx = ""
        b_sfw = V("s_fw" + sfx)
        b_sfd = V("s_fd" + sfx)
        b_tgu = V("t_gu" + sfx)
        b_asi = V("a_si" + sfx)
        b_da = V("d_a" + sfx)
        b_ty = V("t_y" + sfx)
        b_dy = V("d_y" + sfx)
        b_dms = V("d_ms" + sfx)

        NES = len(FFN_ES)

        def srcs(pe):
            wi, ce, fam = FFN_ES[pe]
            if fam == "moe":
                return w_g_d[wi], w_u_d[wi], w_d_d[wi], ce
            return ws_g_d[wi], ws_u_d[wi], ws_d_d[wi], ce

        b_ysq0 = V("a_ysq")
        b_do0 = V("d_o")
        b_so0 = V("s_out")

        def f_vec0(v):
            v.wait_ge(ctx.sem("a_ysq"), b_ysq0)
            if FFN_MEMSET:
                ctx.inc(v.memset(y_acc[:, 0:8192], 0.0), "d_ms" + sfx)
                ctx.inc(v.memset(y_acc[:, 8192:16384], 0.0), "d_ms" + sfx)
            else:
                ctx.inc(v.engine_nop(), "d_ms" + sfx)
                ctx.inc(v.engine_nop(), "d_ms" + sfx)
        blk.vector(f_vec0)

        def f_gp(g):
            g.wait_ge(ctx.sem("d_ms" + sfx), b_dms + 2)
            for pe in range(NES):
                gsrc, usrc, dsrc, ce = srcs(pe)
                for gg in range(2):
                    for ic in range(11):
                        icg = (pe * 2 + gg) * 11 + ic
                        if icg >= 2:
                            g.wait_ge(ctx.sem("d_a" + sfx), b_da + icg - 1)
                        elif hf == 1:
                            g.wait_ge(ctx.sem("d_a" + sfx), b_da)
                        ctx.inc(g.dma_start(out=wg_buf[icg % 2], in_=gsrc[ic]),
                                "s_fw" + sfx, 16)
                        ctx.inc(g.dma_start(out=wu_buf[icg % 2], in_=usrc[ic]),
                                "s_fw" + sfx, 16)
                    for hgc in range(8):
                        hgg = (pe * 2 + gg) * 8 + hgc
                        if hgg >= 3:
                            g.wait_ge(ctx.sem("t_y" + sfx), b_ty + 4 * (hgg - 2))
                        elif hf == 1 and hgg == 0:
                            g.wait_ge(ctx.sem("t_y" + sfx), b_ty)
                        ctx.inc(g.dma_start(out=wd_buf[hgc % 3], in_=dsrc[hgc]),
                                "s_fd" + sfx, 16)
        blk.gpsimd(f_gp)

        def f_ten(t):
            for pe in range(NES):
                for gg in range(2):
                    rhs0 = 1024 * hf + 512 * gg
                    for ic in range(11):
                        icg = (pe * 2 + gg) * 11 + ic
                        t.wait_ge(ctx.sem("s_fw" + sfx), b_sfw + 32 * (icg + 1))
                        if SERIAL_FFN and icg >= 1:
                            t.wait_ge(ctx.sem("a_si" + sfx), b_asi + icg)
                            t.wait_ge(ctx.sem("d_a" + sfx), b_da + icg)
                        elif icg >= 2:
                            t.wait_ge(ctx.sem("a_si" + sfx), b_asi + icg - 1)
                            t.wait_ge(ctx.sem("d_a" + sfx), b_da + icg - 1)
                        elif hf == 1:
                            t.wait_ge(ctx.sem("a_si" + sfx), b_asi)
                            t.wait_ge(ctx.sem("d_a" + sfx), b_da)
                        for which, buf in ((0, wg_buf), (1, wu_buf)):
                            mm = None
                            for fc in range(16):
                                mm = t.matmul(
                                    pb[which * 2 + icg % 2][:, 0:512],
                                    buf[icg % 2][:, 128 * fc:128 * (fc + 1)],
                                    bufB[:, 2048 * fc + rhs0:2048 * fc + rhs0 + 512],
                                    start=(fc == 0), stop=(fc == 15))
                            ctx.inc(mm, "t_gu" + sfx)
                    for hgc in range(8):
                        hgg = (pe * 2 + gg) * 8 + hgc
                        t.wait_ge(ctx.sem("s_fd" + sfx), b_sfd + 16 * (hgg + 1))
                        t.wait_ge(ctx.sem("d_a" + sfx), b_da + (pe * 2 + gg) * 11 + 11)
                        for tsub in range(4):
                            yt = 4 * hgg + tsub
                            if SERIAL_FFN and yt >= 1:
                                t.wait_ge(ctx.sem("d_y" + sfx), b_dy + yt)
                            elif yt >= 2:
                                t.wait_ge(ctx.sem("d_y" + sfx), b_dy + yt - 1)
                            elif hf == 1:
                                t.wait_ge(ctx.sem("d_y" + sfx), b_dy)
                            mm = None
                            for ic in range(11):
                                mm = t.matmul(
                                    pb[4 + yt % 2][:, 0:256],
                                    ff_a[:, 512 * ic + 128 * tsub:512 * ic + 128 * (tsub + 1)],
                                    wd_buf[hgc % 3][:, 256 * ic:256 * (ic + 1)],
                                    start=(ic == 0), stop=(ic == 10))
                            ctx.inc(mm, "t_y" + sfx)
        blk.tensor(f_ten)

        def f_sca(sc):
            for pe in range(NES):
                for gg in range(2):
                    for ic in range(11):
                        icg = (pe * 2 + gg) * 11 + ic
                        sc.wait_ge(ctx.sem("t_gu" + sfx), b_tgu + 2 * icg + 1)
                        if icg >= 2:
                            sc.wait_ge(ctx.sem("d_a" + sfx), b_da + icg - 1)
                        elif hf == 1:
                            sc.wait_ge(ctx.sem("d_o"), b_do0)
                        ctx.inc(sc.activation(ff_sg[icg % 2], pb[icg % 2][:, 0:512],
                                              AF.Silu), "a_si" + sfx)
        blk.scalar(f_sca)

        def f_vec(v):
            for pe in range(NES):
                _, ce, _ = FFN_ES[pe]
                for gg in range(2):
                    for ic in range(11):
                        icg = (pe * 2 + gg) * 11 + ic
                        v.wait_ge(ctx.sem("a_si" + sfx), b_asi + icg + 1)
                        v.wait_ge(ctx.sem("t_gu" + sfx), b_tgu + 2 * (icg + 1))
                        if hf == 1 and icg == 0:
                            v.wait_ge(ctx.sem("s_out"), b_so0)
                        if pe * 2 + gg >= 1:
                            v.wait_ge(ctx.sem("t_y" + sfx), b_ty + 32 * (pe * 2 + gg))
                        ctx.inc(v.tensor_mul(ff_a[:, 512 * ic:512 * (ic + 1)],
                                             pb[2 + icg % 2][:, 0:512],
                                             ff_sg[icg % 2]), "d_a" + sfx)
                    for hgc in range(8):
                        hgg = (pe * 2 + gg) * 8 + hgc
                        for tsub in range(4):
                            yt = 4 * hgg + tsub
                            v.wait_ge(ctx.sem("t_y" + sfx), b_ty + yt + 1)
                            gts = 4 * gg + tsub
                            ysl = y_acc[:, 2048 * gts + 256 * hgc:2048 * gts + 256 * (hgc + 1)]
                            ccol = combine[:, 17 * (8 * hf + gts) + ce:17 * (8 * hf + gts) + ce + 1]
                            n = ctx.cnt["d_y" + sfx] + 1 - b_dy
                            if n > 64:
                                v.wait_ge(ctx.sem("d_y" + sfx), b_dy + n - 64)
                            ctx.inc(v.scalar_tensor_tensor(
                                out=ysl, in0=pb[4 + yt % 2][:, 0:256],
                                scalar=ccol, in1=ysl,
                                op0=ALU.mult, op1=ALU.add), "d_y" + sfx)
        blk.vector(f_vec)

    def ph_out(hf):
        sfx = ""
        b_do = V("d_o" + sfx)
        b_dmu = V("d_mu" + sfx)
        b_ay = V("a_ysq" + sfx)
        b_am2 = V("a_m2" + sfx)
        b_dt2 = V("d_t2" + sfx)
        b_dvr = V("d_vr" + sfx)
        b_asr = V("a_sr" + sfx)
        b_sxr2 = V("s_xr2" + sfx)
        b_sout = V("s_out")

        def f_sync(sy):
            sy.wait_ge(ctx.sem("d_y" + sfx), V("d_y" + sfx))
            for gts in range(8):
                if gts >= 1:
                    sy.wait_ge(ctx.sem("d_o" + sfx), b_do + gts)
                ctx.inc(sy.dma_start(
                    out=out_x2r,
                    in_=x2_dram[1024 * hf + 128 * gts:1024 * hf + 128 * (gts + 1), :]),
                    "s_xr2" + sfx, 16)
                sy.wait_ge(ctx.sem("d_o" + sfx), b_do + gts + 1)
                ctx.inc(sy.dma_start(
                    out=out_d[1024 * hf + 128 * gts:1024 * hf + 128 * (gts + 1), :],
                    in_=out_ot), "s_out", 16)
        blk.sync(f_sync)

        def f_vec(v):
            v.wait_ge(ctx.sem("t_y" + sfx), V("t_y" + sfx))
            for gts in range(8):
                ysl = y_acc[:, 2048 * gts:2048 * (gts + 1)]
                e = ctx.inc(v.reduce_sum(sig[:, 0:1], ysl, axis=AX.X), "dse")
                v.wait_ge(ctx.sem("dse"), e)
                if gts >= 1:
                    v.wait_ge(ctx.sem("a_m2" + sfx), b_am2 + gts)
                ctx.inc(v.tensor_scalar_mul(sig[:, 2:3], sig[:, 0:1], RH),
                        "d_mu" + sfx)
                v.wait_ge(ctx.sem("a_ysq" + sfx), b_ay + gts + 1)
                e = ctx.inc(v.tensor_scalar_mul(sig[:, 3:4], sig[:, 1:2], RH),
                            "d_t2" + sfx)
                v.wait_ge(ctx.sem("d_t2" + sfx), e)
                v.wait_ge(ctx.sem("a_m2" + sfx), b_am2 + gts + 1)
                e = ctx.inc(v.tensor_sub(sig[:, 5:6], sig[:, 3:4], sig[:, 4:5]),
                            "d_vr" + sfx)
                v.wait_ge(ctx.sem("d_vr" + sfx), b_dvr + gts + 1)
                v.wait_ge(ctx.sem("a_sr" + sfx), b_asr + gts + 1)
                e = ctx.inc(v.reciprocal(sig[:, 7:8], sig[:, 6:7]), "dse")
                v.wait_ge(ctx.sem("dse"), e)
                e = ctx.inc(v.tensor_scalar(out=out_t1, in0=ysl,
                                            scalar1=sig[:, 2:3], scalar2=sig[:, 7:8],
                                            op0=ALU.subtract, op1=ALU.mult), "dse")
                v.wait_ge(ctx.sem("dse"), e)
                v.wait_ge(ctx.sem("s_xr2" + sfx), b_sxr2 + 16 * (gts + 1))
                if gts >= 1:
                    v.wait_ge(ctx.sem("s_out"), b_sout + 16 * gts)
                ctx.inc(v.tensor_add(out_ot, out_t1, out_x2r), "d_o" + sfx)
        blk.vector(f_vec)

        def f_sca(sc):
            for gts in range(8):
                ysl = y_acc[:, 2048 * gts:2048 * (gts + 1)]
                sc.wait_ge(ctx.sem("t_y" + sfx), V("t_y" + sfx))
                if gts >= 1:
                    sc.wait_ge(ctx.sem("d_t2" + sfx), b_dt2 + gts)
                ctx.inc(sc.activation(ff_ysq, ysl, AF.Square, accum_out=sig[:, 1:2]),
                        "a_ysq" + sfx)
                sc.wait_ge(ctx.sem("d_mu" + sfx), b_dmu + gts + 1)
                ctx.inc(sc.activation(sig[:, 4:5], sig[:, 2:3], AF.Square),
                        "a_m2" + sfx)
                sc.wait_ge(ctx.sem("d_vr" + sfx), b_dvr + gts + 1)
                ctx.inc(sc.activation(sig[:, 6:7], sig[:, 5:6], AF.Sqrt,
                                      bias=c_eps[:]), "a_sr" + sfx)
        blk.scalar(f_sca)

    def ph_dbg(names):
        if not DEBUG:
            return
        def f_sync(sy):
            for nm in names:
                src = {"dbg_k": k_sb[:], "dbg_v": v_sb[:], "dbg_q": bufB[:],
                       "dbg_lnd": bufA[:], "dbg_o": bufA[:], "dbg_ln2": bufB[:],
                       "dbg_cmb": combine[:], "dbg_y": y_acc[:, 0:16384],
                       "dbg_a": ff_a}[nm]
                ctx.inc(sy.dma_start(out=dbg[nm], in_=src), "s_dbg", 16)
            sy.wait_ge(ctx.sem("s_dbg"), ctx.cnt["s_dbg"])
        blk.sync(f_sync)

    def ph_dbg_x2():
        if not DEBUG:
            return
        def f_sync(sy):
            ctx.inc(sy.dma_start(out=dbg["dbg_x2"], in_=x2_dram), "s_dbg", 16)
            sy.wait_ge(ctx.sem("s_dbg"), ctx.cnt["s_dbg"])
        blk.sync(f_sync)

    def ph_final():
        def f_vec(v):
            v.wait_ge(ctx.sem("s_out"), ctx.cnt["s_out"])
        blk.vector(f_vec)

    ph_ld()
    ph_p0a()
    ph_barrier()
    if STAGE in ("p0", "attn", "proj", "full", "ffn0"):
        ph_p0b()
        ph_barrier()
    if STAGE == "p0":
        ph_dbg(["dbg_k", "dbg_v", "dbg_q", "dbg_lnd"])
    if STAGE in ("attn", "proj", "full", "ffn0"):
        ph_bounds()
        ph_barrier()
        ph_attn()
        ph_barrier()
    if STAGE == "attn":
        ph_dbg(["dbg_k", "dbg_v", "dbg_o"])
    if STAGE in ("proj", "full", "ffn0"):
        ph_proj()
        ph_barrier()
        ph_router()
        ph_barrier()
    if STAGE == "proj":
        ph_dbg(["dbg_ln2", "dbg_cmb"])
        ph_dbg_x2()
    if STAGE in ("full", "ffn0"):
        if FFN_ES:
            ph_ffn(0)
    if STAGE == "ffn0":
        ph_dbg(DBG_FFN_LIST)
        if not SKIP_OUT:
            ph_out(0)
            ph_barrier()
            ph_dbg_x2()
            ph_final()
        else:
            def f_fin(v):
                v.wait_ge(ctx.sem("s_dbg"), ctx.cnt["s_dbg"])
            blk.vector(f_fin)
    if STAGE in ("full",):
        ph_out(0)
        ph_ffn(1)
        ph_out(1)
        ph_final()
    blk.__exit__(None, None, None)
    return nc



def assemble(results):
    out = np.zeros((2, S, H), np.float32)
    for c in range(NCORE):
        b, r = divmod(c, 4)
        ch = chunks_of_core(r)
        o = results[c]["out"]
        for s_i, cc in enumerate(ch):
            out[b, CH * cc:CH * (cc + 1), :] = o[CH * s_i:CH * (s_i + 1), :]
    return out


def kernel(**inputs):
    in_maps = stage_inputs(inputs)
    nc = build_program()
    res = run_bass_kernel_spmd(nc, in_maps, core_ids=list(range(NCORE)))
    return assemble(res.results)
